# version 27
# baseline (speedup 1.0000x reference)
import contextlib
import numpy as np
import ml_dtypes
import concourse.bass as bass
import concourse.mybir as mybir
from concourse.bass_utils import run_bass_kernel_spmd

F32 = mybir.dt.float32
BF16 = mybir.dt.bfloat16
AF = mybir.ActivationFunctionType
ALU = mybir.AluOpType
AX = mybir.AxisListType

B, S, D = 8, 4096, 1024
H, DH = 16, 64
NP = 8
NE = 36
NT = S // 128
NG = 2
GS = S // NG
GT = GS // 128
GC = GS // 512
NEG = -1e30

ENGS = ("sync", "tensor", "scalar", "vector", "gpsimd")

DEV = {"pairs": NP, "experts": NE, "groups": NG, "attn": True, "ffn": True,
       "out": True, "taps": [], "astage": 4}


class Plan:
    def __init__(self):
        self.ops = {e: [] for e in ENGS}
        self.cnt = {s: 0 for s in ("LI", "LX0", "LX1", "CS0", "CS1", "LC", "LD", "LB", "LW", "PU0", "PU1", "T", "SS", "V", "O")}
        self.lastwait = {e: {} for e in ENGS}

    def do(self, eng, fn, inc=None, amt=1):
        if inc is not None:
            self.cnt[inc] += amt
            self.ops[eng].append((fn, inc, amt))
            return self.cnt[inc]
        self.ops[eng].append((fn, None, 0))
        return None

    def dma(self, eng, sem, dst, src):
        return self.do(eng, (lambda E, d=dst, s=src: E.dma_start(out=d, in_=s)),
                       inc=sem, amt=16)

    def wait(self, eng, sem, val=None):
        v = self.cnt[sem] if val is None else val
        if v <= 0:
            return
        if self.lastwait[eng].get(sem, -1) >= v:
            return
        self.lastwait[eng][sem] = v
        self.ops[eng].append((("wait", sem, v), None, 0))

    def drain(self, eng):
        self.ops[eng].append((("drain",), None, 0))

    def run(self, E, eng, handles):
        for fn, inc, amt in self.ops[eng]:
            if isinstance(fn, tuple):
                if fn[0] == "wait":
                    E.wait_ge(handles[fn[1]], fn[2])
                else:
                    E.drain()
                continue
            r = fn(E)
            if inc is not None:
                r.then_inc(handles[inc], amt)

    def n_ops(self):
        return {e: len(v) for e, v in self.ops.items()}


def _pack_k(w):
    K, N = w.shape
    assert K % 128 == 0
    return np.ascontiguousarray(
        w.reshape(K // 128, 128, N).transpose(1, 0, 2).reshape(128, -1))


def build_nc():
    nc = bass.Bass("TRN2", target_bir_lowering=False, debug=False)
    ap = {}

    def din(name, shape, dt=F32):
        ap[name] = nc.dram_tensor(name, list(shape), dt, kind="ExternalInput").ap()

    din("x", (S, D))
    din("wq_p", (128, 8192)); din("wk_p", (128, 8192)); din("wv_p", (128, 8192))
    din("pw_p", (128, 8192))
    din("bqk", (128, 16)); din("bvp", (1, 1024)); din("pjb", (1, 1024))
    din("gw_p", (128, 256)); din("ebt", (128, 32))
    din("upw", (NE, 128, 4096)); din("dnw", (NE, 128, 4096))
    din("cost", (128, S), BF16); din("sint", (128, S), BF16)
    din("mt", (128, 128)); din("idn", (128, 128)); din("msk", (128, 128))
    din("esh", (64, 128))
    out_hbm = nc.dram_tensor("out", [S, D], F32, kind="ExternalOutput").ap()
    upw_bf = nc.dram_tensor("upw_bf", [NE, 128, 4096], BF16, kind="Internal").ap()
    dnw_bf = nc.dram_tensor("dnw_bf", [NE, 128, 4096], BF16, kind="Internal").ap()
    combT_d = nc.dram_tensor("combT_d", [36, S], BF16, kind="Internal").ap()

    es = contextlib.ExitStack()
    tens = {}

    def sb(name, shape, dt):
        t = es.enter_context(nc.sbuf_tensor(name, list(shape), dt))
        tens[name] = t
        return t

    def psm(name, shape, dt):
        return es.enter_context(nc.psum_tensor(name, list(shape), dt))

    xnT = [sb(f"xnT{c}", (128, S), BF16) for c in range(8)]
    acc = sb("acc", (128, GT * 1024), BF16)
    wsl = [sb(f"wsl{i}", (128, 4096), BF16) for i in range(4)]
    cstg = [sb(f"cstg{i}", (36, 128), BF16) for i in range(2)]
    stg = [sb(f"stg{i}", (128, 1024), F32) for i in range(2)]
    attw = sb("attw", (128, 2048), BF16)
    vbuf = sb("vbuf", (128, 4160), BF16)
    qrot = sb("qrot", (128, GS), BF16)
    krot = sb("krot", (128, S), BF16)
    qpre = [sb(f"qpre{i}", (128, 512), BF16) for i in range(2)]
    probs = [sb(f"probs{i}", (128, 1024), BF16) for i in range(4)]
    vt1 = sb("vt1", (128, 512), F32)
    vt2 = sb("vt2", (128, 512), F32)
    sqn = qpre[0][:]
    cbuf = sb("cbuf", (1, 64), F32)
    ncb = sb("ncb", (1, 4), BF16)
    negC = sb("negC", (128, 8), F32)
    stAB = sb("stAB", (1, 1024), BF16)
    stABw = stg[1][0:1, 0:1024]
    bcsb = sb("bcsb", (128, 1024), BF16)
    ctxab = sb("ctxab", (128, 512), BF16)
    tmpb = qpre[1][0:64, :]
    rb = sb("rb", (128, 2048), BF16)
    outs = rb[:].bitcast(F32)
    hb = sb("hb", (128, 4096), BF16)
    crowb = sb("crowb", (1, 2048), BF16)
    bvpb = crowb[0:1, 0:1024]
    pjbb = crowb[0:1, 1024:2048]
    comb4 = [sb(f"comb{i}", (128, 36), F32) for i in range(4)]
    rtr4 = sb("rtr4", (128, 1024), F32)
    accf = acc[:].bitcast(F32)
    xnf = [accf[:, 0:1024], accf[:, 1024:2048]]
    csc = [sb(f"csc{i}", (128, 512), BF16) for i in range(2)]
    css = [sb(f"css{i}", (128, 512), BF16) for i in range(2)]
    mskb = sb("mskb", (128, 128), BF16)
    mtb = sb("mtb", (128, 128), BF16)
    eshb = sb("eshb", (64, 128), BF16)
    idsb = sb("idsb", (128, 128), F32)
    onesb = sb("onesb", (128, 128), BF16)
    gwb = sb("gwb", (128, 256), BF16)
    ebt = sb("ebt_s", (128, 32), F32)
    bqk = sb("bqk_s", (128, 16), F32)
    bqks = sb("bqks", (128, 16), F32)
    bnst = sb("bnst", (128, 12), F32)
    mv = sb("mv", (128, 2), F32)
    sc1 = sb("sc1", (128, 1), F32)
    rstd = sb("rstd", (128, 1), F32)
    nmr = sb("nmr", (128, 1), F32)

    P0 = psm("P0", (128, 1024), F32)
    P1 = psm("P1", (128, 1024), F32)
    P2 = psm("P2", (128, 1024), F32)
    P3 = psm("P3", (128, 1024), F32)

    handles = {}
    for sname in ("LI", "LX0", "LX1", "CS0", "CS1", "LC", "LD", "LB", "LW", "PU0", "PU1", "T", "SS", "V", "O"):
        handles[sname] = es.enter_context(nc.semaphore(sname))

    P = Plan()

    P.dma("sync", "LI", idsb[:], ap["idn"])
    P.dma("sync", "LI", stg[0][:, 0:128], ap["msk"])
    P.dma("sync", "LI", stg[0][:, 128:256], ap["mt"])
    P.dma("sync", "LI", stg[0][:, 256:272], ap["bqk"])
    P.dma("sync", "LI", ebt[:], ap["ebt"])
    P.dma("sync", "LI", stg[0][:, 272:528], ap["gw_p"])
    P.dma("sync", "LI", stg[0][0:64, 528:656], ap["esh"])
    P.wait("vector", "LI")
    P.do("vector", lambda E: E.tensor_copy(out=mskb[:], in_=stg[0][:, 0:128]))
    P.do("vector", lambda E: E.tensor_copy(out=mtb[:], in_=stg[0][:, 128:256]))
    P.do("vector", lambda E: E.tensor_copy(out=bqk[:], in_=stg[0][:, 256:272]))
    P.do("vector", lambda E: E.tensor_copy(out=gwb[:], in_=stg[0][:, 272:528]))
    P.do("vector", lambda E: E.tensor_copy(out=eshb[:], in_=stg[0][0:64, 528:656]))
    P.do("vector", lambda E: E.memset(onesb[:], 1.0))
    P.do("vector", lambda E: E.memset(vbuf[:], 1.0))
    P.do("vector", lambda E: E.memset(rb[:], 0.0))
    P.drain("vector")
    P.do("vector", lambda E: E.tensor_copy(out=bqks[:], in_=bqk[:]))
    P.drain("vector")
    bqv = bqks[:].rearrange("p (a b) -> p a b", b=2)
    P.do("vector", lambda E: E.tensor_scalar_mul(
        out=bqv[:, :, 0:1], in0=bqv[:, :, 0:1], scalar1=0.125), inc="V")

    pump_state = {"k": 0, "cast": {}, "out": {}}
    hbf = hb[:].bitcast(F32)

    PUMP_TOT = NE * 2 * 4
    PUMP_SLOTS = 6
    obuf = [rb[:, 0:2048], acc[:, 12288:14336], acc[:, 14336:16384]]

    def _pump_slot(j):
        s = j % PUMP_SLOTS
        if s < 2:
            return hbf[:, 1024 * s:1024 * (s + 1)]
        return accf[:, 2048 + 1024 * (s - 2):2048 + 1024 * (s - 1)]

    def _pump_issue_in(j):
        if j >= PUMP_TOT:
            return
        mat = j // 4
        jc = j % 4
        e = mat // 2
        srcw = ap["upw"] if mat % 2 == 0 else ap["dnw"]
        if j >= PUMP_SLOTS:
            P.wait("scalar", "V", pump_state["cast"][j - PUMP_SLOTS])
        P.dma("scalar", "PU0", _pump_slot(j), srcw[e, :, 1024 * jc:1024 * (jc + 1)])

    def emit_pump(nchunks):
        for _ in range(nchunks):
            k = pump_state["k"]
            if k >= PUMP_TOT:
                return
            pump_state["k"] = k + 1
            mat = k // 4
            e = mat // 2
            dstw = upw_bf if mat % 2 == 0 else dnw_bf
            h = k // 2
            ob = obuf[h % 3]
            if k == 0:
                P.wait("scalar", "V", 1)
                for j in range(PUMP_SLOTS - 1):
                    _pump_issue_in(j)
            _pump_issue_in(k + PUMP_SLOTS - 1)
            P.wait("vector", "PU0", 16 * (k + 1))
            if h >= 3:
                P.wait("vector", "LB", pump_state["out"][h - 3])
            pump_state["cast"][k] = P.do(
                "vector", lambda E, k=k, ob=ob: E.tensor_copy(
                    out=ob[:, 1024 * (k % 2):1024 * (k % 2) + 1024], in_=_pump_slot(k)),
                inc="V")
            if k % 2 == 1:
                P.wait("gpsimd", "V", pump_state["cast"][k])
                pump_state["out"][h] = P.dma(
                    "gpsimd", "LB", dstw[e, :, 2048 * (h % 2):2048 * (h % 2) + 2048],
                    ob[:])

    ss_xn = [0] * NT
    v_tr = [0] * NT
    for i in range(NT):
        b = i % 2
        if i < 2:
            P.wait("sync", "V")
        else:
            P.wait("sync", "SS", ss_xn[i - 2])
        P.dma("sync", "LX0" if b == 0 else "LX1", stg[b][:], ap["x"][128 * i:128 * (i + 1), :])
        P.wait("vector", "LX0" if b == 0 else "LX1")
        P.do("vector", lambda E, b=b: E.bn_stats(out=bnst[:, 0:6], in_=stg[b][:, 0:512]))
        P.do("vector", lambda E, b=b: E.bn_stats(out=bnst[:, 6:12], in_=stg[b][:, 512:1024]))
        P.drain("vector")
        P.do("vector", lambda E: E.bn_aggr(out=mv[:], in_=bnst[:]))
        P.drain("vector")
        P.do("vector", lambda E: E.tensor_scalar_add(out=sc1[:], in0=mv[:, 1:2], scalar1=1e-5))
        P.drain("vector")
        P.do("vector", lambda E: E.reciprocal(out=sc1[:], in_=sc1[:]), inc="V")
        P.wait("scalar", "V")
        P.do("scalar", lambda E: E.activation(out=rstd[:], in_=sc1[:], func=AF.Sqrt), inc="SS")
        P.wait("vector", "SS")
        P.do("vector", lambda E: E.tensor_scalar(out=nmr[:], in0=mv[:, 0:1],
                                                 scalar1=rstd[:], scalar2=-1.0,
                                                 op0=ALU.mult, op1=ALU.mult), inc="V")
        P.wait("scalar", "V")
        if i >= 2:
            P.wait("scalar", "T")
        xb = xnf[b]
        ss_xn[i] = P.do("scalar", lambda E, b=b, xb=xb: E.activation(
            out=xb[:], in_=stg[b][:], func=AF.Identity, bias=nmr[:], scale=rstd[:]),
            inc="SS")
        P.wait("tensor", "SS")
        for c in range(8):
            half = c % 2
            reg = P3[:, 512 * half:512 * half + 128]
            if c >= 2:
                P.wait("tensor", "V", v_tr[i])
            P.do("tensor", lambda E, xb=xb, c=c, reg=reg: E.transpose(
                out=reg, in_=xb[:, 128 * c:128 * (c + 1)], identity=idsb[:]), inc="T")
            P.wait("vector", "T")
            v_tr[i] = P.do("vector", lambda E, c=c, i=i, reg=reg: E.tensor_copy(
                out=xnT[c][:, 128 * i:128 * (i + 1)], in_=reg), inc="V")
        if DEV["ffn"]:
            emit_pump(9)

    comb_lb = {}
    _steps = [
        (lambda E, r: E.tensor_add(out=r[:, 32:64], in0=r[:, 0:32], in1=ebt[:]),),
        tuple(
            (lambda E, r, a=a, bb=bb, j=j: E.tensor_add(
                out=r[:, 64 + 8 * j:72 + 8 * j],
                in0=r[:, 32:64].rearrange("p (g e) -> p g e", e=4)[:, :, a:a + 1].squeeze(2),
                in1=r[:, 32:64].rearrange("p (g e) -> p g e", e=4)[:, :, bb:bb + 1].squeeze(2)))
            for j, (a, bb) in enumerate([(0, 1), (0, 2), (0, 3), (1, 2), (1, 3), (2, 3)])),
        (lambda E, r: E.tensor_max(out=r[:, 112:120], in0=r[:, 64:72], in1=r[:, 72:80]),
         lambda E, r: E.tensor_max(out=r[:, 120:128], in0=r[:, 80:88], in1=r[:, 88:96]),
         lambda E, r: E.tensor_max(out=r[:, 128:136], in0=r[:, 96:104], in1=r[:, 104:112])),
        (lambda E, r: E.tensor_max(out=r[:, 112:120], in0=r[:, 112:120], in1=r[:, 120:128]),),
        (lambda E, r: E.tensor_max(out=r[:, 112:120], in0=r[:, 112:120], in1=r[:, 128:136]),),
        (lambda E, r: E.max(out=r[:, 120:128], in_=r[:, 112:120]),),
        (lambda E, r: E.memset(r[:, 124:128], NEG),),
        (lambda E, r: E.match_replace(out=r[:, 128:136], in_to_replace=r[:, 120:128],
                                      in_values=r[:, 112:120], imm_value=NEG),),
        (lambda E, r: E.tensor_tensor(out=r[:, 120:128], in0=r[:, 128:136],
                                      in1=r[:, 112:120], op=ALU.is_lt),),
        (lambda E, r: E.tensor_tensor(
            out=r[:, 136:168].rearrange("p (g e) -> p g e", e=4),
            in0=r[:, 32:64].rearrange("p (g e) -> p g e", e=4),
            in1=r[:, 120:128].unsqueeze(2).to_broadcast([128, 8, 4]),
            op=ALU.mult),),
        (lambda E, r: E.max(out=r[:, 168:176], in_=r[:, 136:168]),),
        (lambda E, r: E.memset(r[:, 174:176], NEG),),
        (lambda E, r: E.match_replace(out=r[:, 176:208], in_to_replace=r[:, 168:176],
                                      in_values=r[:, 136:168], imm_value=NEG),),
        (lambda E, r: E.tensor_tensor(out=r[:, 176:208], in0=r[:, 176:208],
                                      in1=r[:, 136:168], op=ALU.is_lt),),
        (lambda E, r: E.tensor_mul(out=r[:, 208:240], in0=r[:, 176:208], in1=r[:, 0:32]),),
        (lambda E, r: E.reduce_sum(out=r[:, 240:241], in_=r[:, 208:240], axis=AX.X),),
        (lambda E, r: E.tensor_scalar_add(out=r[:, 240:241], in0=r[:, 240:241],
                                          scalar1=1e-20),),
        (lambda E, r: E.reciprocal(out=r[:, 241:242], in_=r[:, 240:241]),),
    ]
    for q in range(NT // 4):
        tiles = [4 * q + t for t in range(4)]
        P.wait("tensor", "V")
        P.wait("tensor", "SS")
        for t, i in enumerate(tiles):
            for c in range(8):
                P.do("tensor", lambda E, c=c, i=i, t=t: E.matmul(
                    out=P3[:, 512 + 32 * t:544 + 32 * t],
                    lhsT=xnT[c][:, 128 * i:128 * (i + 1)],
                    rhs=gwb[:, 32 * c:32 * (c + 1)],
                    start=(c == 0), stop=(c == 7)), inc=("T" if c == 7 else None))
        P.wait("scalar", "T")
        for t in range(4):
            P.do("scalar", lambda E, t=t: E.activation(
                out=rtr4[:, 256 * t:256 * t + 32], in_=P3[:, 512 + 32 * t:544 + 32 * t],
                func=AF.Sigmoid), inc="SS")
        P.wait("vector", "SS")
        for step in _steps:
            for t in range(4):
                for fn in step:
                    P.do("vector", lambda E, fn=fn, t=t: fn(
                        E, rtr4[:, 256 * t:256 * (t + 1)]))
            P.drain("vector")
        P.wait("vector", "T")
        for t in range(4):
            r4 = rtr4[:, 256 * t:256 * (t + 1)]
            P.do("vector", lambda E, cb=comb4[t], r4=r4: E.tensor_scalar(
                out=cb[:, 0:32], in0=r4[:, 208:240],
                scalar1=r4[:, 241:242], scalar2=2.5,
                op0=ALU.mult, op1=ALU.mult))
            P.do("vector", lambda E, cb=comb4[t]: E.memset(cb[:, 32:36], 1.0))
        P.drain("vector")
        P.do("vector", lambda E: E.nop(), inc="V")
        for t, i in enumerate(tiles):
            P.wait("tensor", "V")
            P.do("tensor", lambda E, cb=comb4[t]: E.transpose(
                out=P3[0:36, 384:512], in_=cb[:], identity=idsb[:]), inc="T")
            P.wait("vector", "T")
            if i >= 2:
                P.wait("vector", "LB", comb_lb[i - 2])
            P.do("vector", lambda E, i=i: E.tensor_copy(
                out=cstg[i % 2][:], in_=P3[0:36, 384:512]), inc="V")
            P.wait("gpsimd", "V")
            comb_lb[i] = P.dma("gpsimd", "LB", combT_d[0:36, 128 * i:128 * (i + 1)],
                               cstg[i % 2][:])

    v_out = 0
    prev_last_exp = [0]
    t_attw_free = [0]
    v_stg_free = [0]
    for g in range(DEV["groups"]):
        KC = 4 * (g + 1)
        KT = 16 * (g + 1)

        P.wait("scalar", "T")
        P.wait("scalar", "V")
        for hchunk in range(8):
            sbuf_t = stg[hchunk % 2]
            if hchunk >= 2:
                P.wait("scalar", "V")
            P.dma("scalar", "LX0" if hchunk % 2 == 0 else "LX1", sbuf_t[:],
                  ap["pw_p"][:, 1024 * hchunk:1024 * (hchunk + 1)])
            P.wait("vector", "LX0" if hchunk % 2 == 0 else "LX1")
            dstsl = wsl[hchunk // 4]
            off = 1024 * (hchunk % 4)
            P.do("vector", lambda E, s=sbuf_t, d=dstsl, off=off: E.tensor_copy(
                out=d[:, off:off + 1024], in_=s[:]), inc="V")
        if g == 1:
            P.wait("vector", "O", v_out)
        P.wait("sync", "T")
        P.wait("sync", "V")
        P.wait("sync", "SS")
        P.dma("sync", "LX1", stg[1][0:1, 0:1024], ap["bvp"])
        P.wait("vector", "LX1")
        P.wait("vector", "LC")
        P.wait("vector", "T")
        P.do("vector", lambda E: E.tensor_copy(out=bvpb, in_=stg[1][0:1, 0:1024]), inc="V")
        P.wait("sync", "V")
        P.dma("sync", "LX1", stg[1][0:1, 0:1024], ap["pjb"])
        P.wait("vector", "LX1")
        P.do("vector", lambda E: E.tensor_copy(out=pjbb, in_=stg[1][0:1, 0:1024]), inc="V")
        v_stg_free[0] = P.cnt["V"]

        for p in range(DEV["pairs"] if DEV["attn"] else 0):
            P.wait("scalar", "T", t_attw_free[0])
            P.wait("scalar", "V", v_stg_free[0])
            for dc in range(8):
                P.dma("scalar", "LX0", stg[0][:, 128 * dc:128 * (dc + 1)],
                      ap["wq_p"][:, 1024 * dc + 128 * p:1024 * dc + 128 * (p + 1)])
            P.wait("vector", "LX0")
            P.wait("vector", "T", t_attw_free[0])
            v_attw = P.do("vector", lambda E: E.tensor_copy(out=attw[:, 0:1024], in_=stg[0][:]),
                          inc="V")
            P.wait("scalar", "V", v_attw)
            for dc in range(8):
                P.dma("scalar", "LX0", stg[0][:, 128 * dc:128 * (dc + 1)],
                      ap["wk_p"][:, 1024 * dc + 128 * p:1024 * dc + 128 * (p + 1)])
            P.wait("vector", "LX0")
            v_attw_k = P.do("vector", lambda E: E.tensor_copy(out=attw[:, 1024:2048], in_=stg[0][:]),
                            inc="V")

            ss_pre = {}
            ropev = []
            reg_of = {}
            ckc = [0]
            t_q_end = 0
            for kind in (("q",) if DEV.get("skipk") else ("q", "k")):
                if kind == "k":
                    t_q_end = P.cnt["T"]
                    P.wait("scalar", "V", v_attw_k)
                    for dc in range(8):
                        P.dma("scalar", "LX0", stg[0][:, 128 * dc:128 * (dc + 1)],
                              ap["wv_p"][:, 1024 * dc + 128 * p:1024 * dc + 128 * (p + 1)])
                    P.wait("vector", "LX0")
                    P.wait("vector", "T", t_q_end)
                    v_attw = P.do("vector", lambda E: E.tensor_copy(out=attw[:, 0:1024],
                                                                    in_=stg[0][:]), inc="V")
                half = 0 if kind == "q" else 1024
                nch = GC if kind == "q" else KC

                def emit_qk_mms(ci, kind=kind, p=p, half=half):
                    c = (GC * g + ci) if kind == "q" else ci
                    b = ci % 2
                    rot = ckc[0] % 3
                    ckc[0] += 1
                    reg_of[(kind, ci)] = rot
                    P.wait("sync", "V")
                    P.dma("sync", "CS0" if b == 0 else "CS1", csc[b][:], ap["cost"][:, 512 * c:512 * (c + 1)])
                    P.dma("sync", "CS0" if b == 0 else "CS1", css[b][:], ap["sint"][:, 512 * c:512 * (c + 1)])
                    qkreg = (P3, P0, P1)[rot][:, 0:512]
                    wv_need = v_attw if kind == "q" else v_attw_k
                    if len(ropev) >= 3:
                        P.wait("tensor", "V", max(wv_need, ropev[-3]))
                    else:
                        P.wait("tensor", "V", wv_need)
                    if (kind, ci - 3) in ss_pre:
                        P.wait("tensor", "SS", ss_pre[(kind, ci - 3)])
                    elif kind == "k" and ("q", GC - 3 + ci) in ss_pre:
                        P.wait("tensor", "SS", ss_pre[("q", GC - 3 + ci)])
                    for dc in range(8):
                        P.do("tensor", lambda E, dc=dc, c=c, qkreg=qkreg, half=half: E.matmul(
                            out=qkreg, lhsT=attw[:, half + 128 * dc:half + 128 * (dc + 1)],
                            rhs=xnT[dc][:, 512 * c:512 * (c + 1)],
                            start=(dc == 0), stop=(dc == 7)),
                            inc=("T" if dc == 7 else None))
                    return P.cnt["T"]

                def emit_qk_tail(ci, kind=kind, p=p):
                    b = ci % 2
                    rot = reg_of[(kind, ci)]
                    qkreg = (P3, P0, P1)[rot][:, 0:512]
                    rreg = (P3, P0, P1)[rot][:, 512:1024]
                    P.wait("scalar", "T", qk_t[(kind, ci)])
                    if len(ropev) >= 2:
                        P.wait("scalar", "V", ropev[-2])
                    bias = bqks[:, 2 * p:2 * p + 1] if kind == "q" else bqk[:, 2 * p + 1:2 * p + 2]
                    scale = 0.125 if kind == "q" else 1.0
                    ss_pre[(kind, ci)] = P.do(
                        "scalar", lambda E, b=b, bias=bias, scale=scale, qkreg=qkreg:
                        E.activation(out=qpre[b][:], in_=qkreg, func=AF.Identity,
                                     bias=bias, scale=scale), inc="SS")
                    P.wait("tensor", "SS", ss_pre[(kind, ci)])
                    if len(ropev) >= 2:
                        P.wait("tensor", "V", ropev[-2])
                    P.do("tensor", lambda E, b=b, rreg=rreg: E.matmul(
                        out=rreg, lhsT=mtb[:], rhs=qpre[b][:], start=True, stop=True),
                        inc="T")
                    dst = qrot if kind == "q" else krot
                    dof = 512 * ci if kind == "q" else 512 * ci
                    P.wait("vector", "T")
                    P.wait("vector", "CS0" if b == 0 else "CS1")
                    P.drain("vector")
                    P.do("vector", lambda E, b=b: E.tensor_mul(
                        out=vt1[:], in0=qpre[b][:], in1=csc[b][:]))
                    P.do("vector", lambda E, b=b, rreg=rreg: E.tensor_tensor(
                        out=vt2[:], in0=rreg, in1=css[b][:], op=ALU.mult))
                    P.drain("vector")
                    ropev.append(P.do("vector", lambda E, dst=dst, dof=dof: E.tensor_add(
                        out=dst[:, dof:dof + 512], in0=vt1[:], in1=vt2[:]), inc="V"))

                qk_t = getattr(P, "_qk_t", None)
                if qk_t is None:
                    qk_t = {}
                    P._qk_t = qk_t
                for ci in range(nch + 1):
                    if ci < nch:
                        qk_t[(kind, ci)] = emit_qk_mms(ci)
                    if ci - 1 >= 0:
                        emit_qk_tail(ci - 1)
            if DEV["astage"] < 2:
                continue
            P.wait("tensor", "V")
            v_ev = [0] * KT
            for tt in range(KT):
                if g == 0 and DEV["ffn"]:
                    emit_pump(6)
                vreg = (P2[:, 0:128], P2[:, 640:768],
                        P0[:, 0:128], P0[:, 640:768])[tt % 4]
                if tt >= 4:
                    P.wait("tensor", "V", v_ev[tt - 4])
                for dc in range(8):
                    P.do("tensor", lambda E, dc=dc, tt=tt, vreg=vreg: E.matmul(
                        out=vreg, lhsT=xnT[dc][:, 128 * tt:128 * (tt + 1)],
                        rhs=attw[:, 128 * dc:128 * (dc + 1)],
                        start=(dc == 0), stop=False))
                P.do("tensor", lambda E, p=p, vreg=vreg: E.matmul(
                    out=vreg, lhsT=onesb[0:1, 0:128],
                    rhs=bvpb[:, 128 * p:128 * (p + 1)],
                    start=False, stop=True), inc="T")
                P.wait("vector", "T")
                P.do("vector", lambda E, tt=tt, vreg=vreg: E.tensor_copy(
                    out=vbuf[:, 130 * tt:130 * tt + 64], in_=vreg[:, 0:64]))
                v_ev[tt] = P.do("vector", lambda E, tt=tt, vreg=vreg: E.tensor_copy(
                    out=vbuf[:, 130 * tt + 65:130 * tt + 129], in_=vreg[:, 64:128]), inc="V")
            t_attw_free[0] = P.cnt["T"]
            v_stg_free[0] = v_attw

            if DEV["astage"] < 3:
                continue
            if True:
              for kind, nch, buf in (("q", GC, qrot), ("k", KC, krot)):
                nmm_t = {}
                sq_v = {}
                red_v = {}
                for ci in range(nch + 2):
                    if ci < nch:
                        sqb = qpre[ci % 2]
                        P.wait("vector", "T", nmm_t.get(ci - 2, 0))
                        sq_v[ci] = P.do("vector", lambda E, buf=buf, ci=ci, sqb=sqb: E.tensor_mul(
                            out=sqb[:], in0=buf[:, 512 * ci:512 * (ci + 1)],
                            in1=buf[:, 512 * ci:512 * (ci + 1)]), inc="V")
                    if 0 <= ci - 1 < nch:
                        c2 = ci - 1
                        sqb = qpre[c2 % 2]
                        par = 512 * (c2 % 2)
                        P.wait("tensor", "V", sq_v[c2])
                        P.wait("tensor", "V", red_v.get(c2 - 2, 0))
                        P.do("tensor", lambda E, sqb=sqb, par=par: E.matmul(
                            out=P2[0:1, par:par + 512], lhsT=onesb[0:64, 0:1], rhs=sqb[0:64, :],
                            start=True, stop=True, tile_position=(0, 0)))
                        nmm_t[c2] = P.do("tensor", lambda E, sqb=sqb, par=par: E.matmul(
                            out=P1[0:1, par:par + 512], lhsT=onesb[64:128, 0:1], rhs=sqb[64:128, :],
                            start=True, stop=True, tile_position=(64, 0)), inc="T")
                    if 0 <= ci - 2 < nch:
                        c2 = ci - 2
                        par = 512 * (c2 % 2)
                        P.wait("vector", "T", nmm_t[c2])
                        ofA = (0 if kind == "q" else 16) + c2
                        ofB = (8 if kind == "q" else 32) + c2
                        P.do("vector", lambda E, ofA=ofA, par=par: E.reduce_max(
                            out=cbuf[0:1, ofA:ofA + 1], in_=P2[0:1, par:par + 512], axis=AX.X))
                        red_v[c2] = P.do("vector", lambda E, ofB=ofB, par=par: E.reduce_max(
                            out=cbuf[0:1, ofB:ofB + 1], in_=P1[0:1, par:par + 512], axis=AX.X), inc="V")
            P.drain("vector")
            P.do("vector", lambda E: E.reduce_max(out=cbuf[0:1, 40:41], in_=cbuf[0:1, 0:GC], axis=AX.X))
            P.do("vector", lambda E: E.reduce_max(out=cbuf[0:1, 41:42], in_=cbuf[0:1, 8:8 + GC], axis=AX.X))
            P.do("vector", lambda E, KC=KC: E.reduce_max(out=cbuf[0:1, 42:43], in_=cbuf[0:1, 16:16 + KC], axis=AX.X))
            P.do("vector", lambda E, KC=KC: E.reduce_max(out=cbuf[0:1, 43:44], in_=cbuf[0:1, 32:32 + KC], axis=AX.X))
            P.drain("vector")
            P.do("vector", lambda E: E.tensor_max(out=cbuf[0:1, 40:41], in0=cbuf[0:1, 40:41], in1=cbuf[0:1, 41:42]))
            P.do("vector", lambda E: E.tensor_max(out=cbuf[0:1, 42:43], in0=cbuf[0:1, 42:43], in1=cbuf[0:1, 43:44]))
            P.drain("vector")
            P.do("vector", lambda E: E.tensor_add(out=cbuf[0:1, 44:45], in0=cbuf[0:1, 40:41], in1=cbuf[0:1, 42:43]))
            P.drain("vector")
            P.wait("vector", "SS", prev_last_exp[0])
            P.do("vector", lambda E: E.tensor_scalar_mul(out=ncb[0:1, 0:1], in0=cbuf[0:1, 44:45], scalar1=-1.0 / 16.0), inc="V")
            P.wait("tensor", "V")
            P.do("tensor", lambda E: E.matmul(out=P3[:, 0:1], lhsT=onesb[0:1, 0:128],
                                              rhs=ncb[0:1, 0:1], start=True, stop=True), inc="T")
            P.wait("vector", "T")
            P.do("vector", lambda E: E.tensor_copy(out=negC[:, 0:1], in_=P3[:, 0:1]), inc="V")
            v_negc = P.cnt["V"]
            P.wait("scalar", "V")

            if DEV["astage"] < 4:
                continue
            exps_all = [0, 0, 0]
            ctx_all = [0, 0]
            v_probs = {}
            sc_t = {}
            ctx_of = {}
            gof = [0]
            prev_proj_v = [0]
            for sq in range(4):
                sjg = 4 * g + sq
                ntc = 4 * sjg + 4
                gb = gof[0]
                t_ctx_last = 0
                diag_of = {}

                def emit_sc(tc, sq=sq, gb=gb):
                    PS = (P0, P1, P3)[tc % 3]
                    if PS is P3:
                        P.wait("tensor", "V", v_negc)
                    P.wait("tensor", "SS", exps_all[-2])
                    P.do("tensor", lambda E, tc=tc, sq=sq, PS=PS: E.matmul(
                        out=PS[:, 0:512], lhsT=krot[0:64, 128 * tc:128 * (tc + 1)],
                        rhs=qrot[0:64, 512 * sq:512 * (sq + 1)],
                        start=True, stop=True, tile_position=(0, 0)))
                    sc_t[gb + tc] = P.do("tensor", lambda E, tc=tc, sq=sq, PS=PS: E.matmul(
                        out=PS[:, 512:1024], lhsT=krot[64:128, 128 * tc:128 * (tc + 1)],
                        rhs=qrot[64:128, 512 * sq:512 * (sq + 1)],
                        start=True, stop=True, tile_position=(64, 0)), inc="T")

                def emit_exp(tc, gb=gb):
                    PS = (P0, P1, P3)[tc % 3]
                    gt = gb + tc
                    pr = probs[gt % 4]
                    P.wait("scalar", "T", sc_t[gt])
                    P.wait("scalar", "T", ctx_of.get(gt - 4, 0))
                    if (gt - 4) in v_probs:
                        P.wait("scalar", "V", v_probs[gt - 4])
                    ev = P.do("scalar", lambda E, PS=PS, pr=pr: E.activation(
                        out=pr[:], in_=PS[:], func=AF.Exp, bias=negC[:, 0:1]), inc="SS")
                    exps_all.append(ev)
                    r = tc - 4 * sjg
                    if r >= 0:
                        diag_of[tc] = r
                        P.wait("vector", "SS", ev)
                        for half in (0, 512):
                            P.do("vector", lambda E, pr=pr, r=r, half=half: E.tensor_tensor(
                                out=pr[:, half + 128 * r:half + 128 * (r + 1)],
                                in0=pr[:, half + 128 * r:half + 128 * (r + 1)], in1=mskb[:],
                                op=ALU.mult))
                            if r > 0:
                                P.do("vector", lambda E, pr=pr, r=r, half=half: E.memset(
                                    pr[:, half:half + 128 * r], 0.0))
                        P.drain("vector")
                        P.do("vector", lambda E: E.nop(), inc="V")
                        v_probs[gt] = P.cnt["V"]

                exp_of = {}
                for i in range(ntc + 2):
                    if i < ntc:
                        emit_sc(i)
                    if 0 <= i - 1 < ntc:
                        emit_exp(i - 1)
                        exp_of[i - 1] = exps_all[-1]
                    if 0 <= i - 2 < ntc:
                        tcx = i - 2
                        gt = gb + tcx
                        pr = probs[gt % 4]
                        if tcx == 0:
                            P.wait("tensor", "V", prev_proj_v[0])
                        if tcx in diag_of:
                            P.wait("tensor", "V", v_probs[gt])
                        else:
                            P.wait("tensor", "SS", exp_of[tcx])
                        st = (tcx == 0)
                        sp = (tcx == ntc - 1)
                        P.do("tensor", lambda E, tc=tcx, pr=pr, st=st, sp=sp: E.matmul(
                            out=P2[0:65, 0:512], lhsT=vbuf[:, 130 * tc:130 * tc + 65],
                            rhs=pr[:, 0:512], start=st, stop=sp))
                        t_ctx_last = P.do("tensor", lambda E, tc=tcx, pr=pr, st=st, sp=sp: E.matmul(
                            out=P2[0:65, 512:1024], lhsT=vbuf[:, 130 * tc + 65:130 * tc + 130],
                            rhs=pr[:, 512:1024], start=st, stop=sp), inc="T")
                        ctx_all.append(t_ctx_last)
                        ctx_of[gt] = t_ctx_last
                P.wait("vector", "T", t_ctx_last)
                P.do("vector", lambda E: E.tensor_copy(out=tmpb, in_=P2[0:64, 512:1024]),
                     inc="V")
                P.wait("scalar", "T", t_ctx_last)
                P.do("scalar", lambda E: E.activation(out=stABw, in_=P2[64:65, 0:1024],
                                                      func=AF.Ln))
                P.drain("scalar")
                P.do("scalar", lambda E: E.activation(out=stAB[:, 0:1024], in_=stABw,
                                                      func=AF.Exp, scale=-1.0), inc="SS")
                P.wait("tensor", "V")
                P.wait("tensor", "SS")
                P.do("tensor", lambda E: E.matmul(out=P3[:, 0:512], lhsT=onesb[0:1, 0:128],
                                                  rhs=stAB[0:1, 0:512], start=True, stop=True))
                P.do("tensor", lambda E: E.matmul(out=P3[:, 512:1024], lhsT=onesb[0:1, 0:128],
                                                  rhs=stAB[0:1, 512:1024], start=True, stop=True))
                P.do("tensor", lambda E: E.matmul(out=P2[:, 512:1024], lhsT=eshb[:],
                                                  rhs=tmpb, start=True, stop=True), inc="T")
                P.wait("vector", "T")
                P.do("vector", lambda E: E.tensor_copy(out=bcsb[:], in_=P3[:, 0:1024]))
                P.drain("vector")
                P.do("vector", lambda E: E.tensor_mul(out=ctxab[0:64, :], in0=P2[0:64, 0:512],
                                                      in1=bcsb[0:64, 0:512]))
                v_ctx_scale_prev = P.do("vector", lambda E: E.tensor_mul(
                    out=ctxab[64:128, :], in0=P2[64:128, 512:1024],
                    in1=bcsb[64:128, 512:1024]), inc="V")
                P.wait("tensor", "V", v_ctx_scale_prev)
                for msub in range(4):
                    preg = P3 if msub % 2 == 0 else P2
                    jloc = 4 * sq + msub
                    if msub >= 2:
                        P.wait("tensor", "V")
                    lastm = None
                    for n in range(2):
                        last = (n == 1 and p > 0)
                        P.do("tensor", lambda E, p=p, n=n, msub=msub, preg=preg: E.matmul(
                            out=preg[:, 512 * n:512 * (n + 1)],
                            lhsT=ctxab[:, 128 * msub:128 * (msub + 1)],
                            rhs=wsl[p // 4][:, 1024 * (p % 4) + 512 * n:1024 * (p % 4) + 512 * (n + 1)],
                            start=True, stop=(p > 0)), inc=("T" if last else None))
                        if p == 0:
                            P.do("tensor", lambda E, n=n, preg=preg: E.matmul(
                                out=preg[:, 512 * n:512 * (n + 1)], lhsT=onesb[0:1, 0:128],
                                rhs=pjbb[:, 512 * n:512 * (n + 1)],
                                start=False, stop=True), inc=("T" if n == 1 else None))
                    P.wait("vector", "T")
                    if p == 0:
                        P.do("vector", lambda E, jloc=jloc, preg=preg: E.tensor_copy(
                            out=acc[:, 1024 * jloc:1024 * (jloc + 1)], in_=preg[:]), inc="V")
                    else:
                        P.do("vector", lambda E, jloc=jloc, preg=preg: E.tensor_tensor(
                            out=acc[:, 1024 * jloc:1024 * (jloc + 1)],
                            in0=acc[:, 1024 * jloc:1024 * (jloc + 1)], in1=preg[:],
                            op=ALU.add), inc="V")
                    prev_proj_v[0] = P.cnt["V"]
                gof[0] += ntc
            prev_last_exp[0] = exps_all[-1]
        def load_expert(e, which, tfree):
            sl = wsl[(e % 2) if which == "up" else 2 + (e % 2)]
            srcw = upw_bf if which == "up" else dnw_bf
            eng, sem = ("sync", "LW") if which == "up" else ("gpsimd", "LD")
            P.wait(eng, "T", tfree)
            need_h = 4 * e + (1 if which == "up" else 3)
            P.wait(eng, "LB", pump_state["out"][need_h])
            cnt = P.dma(eng, sem, sl[:], srcw[e])
            return [(sem, cnt)]

        t_att_end = P.cnt["T"]
        if g == 0 and DEV["ffn"]:
            emit_pump(NE * 2 * 8)
        P.wait("scalar", "O")
        P.wait("scalar", "LB")
        if not DEV["attn"] or DEV["pairs"] == 0:
            P.do("vector", lambda E: E.memset(acc[:], 0.0), inc="V")
        NE_RUN = DEV["experts"] if DEV["ffn"] else 0
        up_v = load_expert(0, "up", t_att_end) if NE_RUN else None
        dn_v = load_expert(0, "dn", t_att_end) if NE_RUN else None
        dn_waits = {0: dn_v}
        up_waits = {0: up_v}
        if NE_RUN:
            P.wait("scalar", "T")
            P.wait("scalar", "V")
            P.dma("scalar", "LC", crowb[:], combT_d[0:1, GS * g:GS * (g + 1)])

        t_up_end = [0] * NE
        t_dn_end = [0] * NE
        down_t_done = {}
        v_hb = {}
        v_dn = {}
        ss_sq = {}
        dslots = [P3[:, 0:512], P3[:, 512:1024], P2[:, 512:1024]]
        dslot_last = {}
        epr = None
        pend_down = []
        for e in range(NE_RUN):
            def emit_down(e, tc):
                for sm, vv in (dn_waits.get(e) or []):
                    P.wait("tensor", sm, vv)
                P.wait("tensor", "V", v_hb[(e, tc, 3)])
                for tt in range(4):
                    jloc = 4 * tc + tt
                    for n in range(2):
                        slot = dslots[(2 * (4 * tc + tt) + n) % 3]
                        key_prev = dslot_last.get((2 * (4 * tc + tt) + n) % 3)
                        if key_prev is not None:
                            P.wait("tensor", "V", key_prev)
                        ho = 2048 * (tc % 2)
                        for i in range(4):
                            P.do("tensor", lambda E, i=i, n=n, tt=tt, e=e, slot=slot, ho=ho: E.matmul(
                                out=slot,
                                lhsT=hb[:, ho + 512 * i + 128 * tt:ho + 512 * i + 128 * (tt + 1)],
                                rhs=wsl[2 + e % 2][:, 1024 * i + 512 * n:1024 * i + 512 * (n + 1)],
                                start=(i == 0), stop=(i == 3)),
                                inc=("T" if i == 3 else None))
                        P.wait("vector", "T")
                        vv = P.do("vector", lambda E, jloc=jloc, n=n, slot=slot: E.tensor_tensor(
                            out=acc[:, 1024 * jloc + 512 * n:1024 * jloc + 512 * (n + 1)],
                            in0=acc[:, 1024 * jloc + 512 * n:1024 * jloc + 512 * (n + 1)],
                            in1=slot, op=ALU.add), inc="V")
                        dslot_last[(2 * (4 * tc + tt) + n) % 3] = vv
                        v_dn[(e, tc, tt, n)] = vv
                v_dn[(e, tc)] = v_dn[(e, tc, 3, 1)]
                down_t_done[e] = P.cnt["T"]

            for tc in range(GC):
                TC = GS * g + 512 * tc
                pend_down.append((e, tc))
                t_dn_guard = 0
                if len(pend_down) >= 3:
                    de, dtc = pend_down.pop(0)
                    emit_down(de, dtc)
                    t_dn_guard = P.cnt["T"]
                t_up_i = {}
                for i in (2, 3, 0, 1):
                    reg = (P1 if i >= 2 else P0)[:, 512 * (i % 2):512 * (i % 2) + 512]
                    if (e, tc - 1, i) in ss_sq:
                        P.wait("tensor", "SS", ss_sq[(e, tc - 1, i)])
                    elif tc == 0 and (epr, GC - 1, i) in ss_sq:
                        P.wait("tensor", "SS", ss_sq[(epr, GC - 1, i)])
                    if tc == 0:
                        for sm, vv in (up_waits.get(e) or []):
                            P.wait("tensor", sm, vv)
                    for dc in range(8):
                        P.do("tensor", lambda E, dc=dc, i=i, TC=TC, reg=reg, e=e: E.matmul(
                            out=reg, lhsT=wsl[e % 2][:, 512 * dc + 128 * i:512 * dc + 128 * (i + 1)],
                            rhs=xnT[dc][:, TC:TC + 512],
                            start=(dc == 0), stop=(dc == 7)),
                            inc=("T" if dc == 7 else None))
                    t_up_i[i] = P.cnt["T"]
                t_up_this = P.cnt["T"]
                if tc == GC - 1:
                    t_up_end[e] = t_up_this
                if (e, tc - 1, 3) in v_hb:
                    P.wait("tensor", "V", v_hb[(e, tc - 1, 3)])
                elif tc == 0 and (epr, GC - 1, 3) in v_hb:
                    P.wait("tensor", "V", v_hb[(epr, GC - 1, 3)])
                P.wait("tensor", "LC")
                P.do("tensor", lambda E, tc=tc: E.matmul(
                    out=P2[:, 0:512], lhsT=onesb[0:1, 0:128],
                    rhs=crowb[0:1, 512 * tc:512 * (tc + 1)],
                    start=True, stop=True), inc="T")
                t_bc = P.cnt["T"]
                if tc == 0 and e + 1 < NE_RUN:
                    up_waits[e + 1] = load_expert(e + 1, "up", P.cnt["T"])
                if tc == 2 and e + 1 < NE_RUN:
                    dn_waits[e + 1] = load_expert(
                        e + 1, "dn", down_t_done.get(e - 1, 0) if e >= 1 else P.cnt["T"])
                P.wait("scalar", "T", t_dn_guard)
                if (e, tc - 2, 3) in v_hb:
                    P.wait("scalar", "V", v_hb[(e, tc - 2, 3)])
                elif tc < 2 and epr is not None and (epr, GC - 2 + tc, 3) in v_hb:
                    P.wait("scalar", "V", v_hb[(epr, GC - 2 + tc, 3)])
                ho = 2048 * (tc % 2)
                for ia, ib in ((2, 3), (0, 1)):
                    P.wait("scalar", "T", t_up_i[ib])
                    for i in (ia, ib):
                        reg = (P1 if i >= 2 else P0)[:, 512 * (i % 2):512 * (i % 2) + 512]
                        P.do("scalar", lambda E, i=i, reg=reg: E.activation(
                            out=rb[:, 512 * i:512 * (i + 1)], in_=reg, func=AF.Relu))
                    P.drain("scalar")
                    for i in (ia, ib):
                        ss_sq[(e, tc, i)] = P.do("scalar", lambda E, i=i, ho=ho: E.activation(
                            out=hb[:, ho + 512 * i:ho + 512 * (i + 1)], in_=rb[:, 512 * i:512 * (i + 1)],
                            func=AF.Square), inc="SS")
                P.wait("vector", "T", t_bc)
                P.do("vector", lambda E: E.tensor_copy(out=bcsb[:, 0:512], in_=P2[:, 0:512]),
                     inc="V")
                P.wait("vector", "SS")
                for i in range(4):
                    v_hb[(e, tc, i)] = P.do("vector", lambda E, i=i, ho=ho: E.tensor_tensor(
                        out=hb[:, ho + 512 * i:ho + 512 * (i + 1)],
                        in0=hb[:, ho + 512 * i:ho + 512 * (i + 1)],
                        in1=bcsb[:, 0:512], op=ALU.mult), inc="V")
            if e == NE_RUN - 1:
                while pend_down:
                    de, dtc = pend_down.pop(0)
                    emit_down(de, dtc)
            t_dn_end[e] = P.cnt["T"]
            epr = e
            if e + 1 < NE_RUN:
                P.wait("scalar", "T", t_dn_end[e])
                P.dma("scalar", "LC", crowb[:], combT_d[e + 1:e + 2, GS * g:GS * (g + 1)])

        o_st = [0] * GT
        for j in range(GT if DEV["out"] else 0):
            bj = j % 2
            P.wait("sync", "V")
            P.dma("sync", "LX0" if bj == 0 else "LX1", stg[bj][:], ap["x"][GS * g + 128 * j:GS * g + 128 * (j + 1), :])
            P.wait("vector", "LX0" if bj == 0 else "LX1")
            if j >= 1:
                P.wait("vector", "O", o_st[j - 1])
            P.do("vector", lambda E, bj=bj, j=j: E.tensor_tensor(
                out=outs, in0=stg[bj][:], in1=acc[:, 1024 * j:1024 * (j + 1)],
                op=ALU.add), inc="V")
            P.wait("gpsimd", "V")
            o_st[j] = P.dma("gpsimd", "O",
                            out_hbm[GS * g + 128 * j:GS * g + 128 * (j + 1), :], outs)
        v_out = P.cnt["O"]

    for tname in DEV["taps"]:
        t = tens[tname]
        shp = [t.shape[0], t.shape[1]]
        tap = nc.dram_tensor("tap_" + tname, shp, t[:].dtype, kind="ExternalOutput").ap()
        P.wait("gpsimd", "V")
        P.wait("gpsimd", "SS")
        P.wait("gpsimd", "T")
        P.dma("gpsimd", "O", tap, t[:])

    P.wait("gpsimd", "O")

    with nc.Block() as block:
        @block.sync
        def _(E):
            P.run(E, "sync", handles)

        @block.tensor
        def _(E):
            P.run(E, "tensor", handles)

        @block.scalar
        def _(E):
            P.run(E, "scalar", handles)

        @block.vector
        def _(E):
            P.run(E, "vector", handles)

        @block.gpsimd
        def _(E):
            P.run(E, "gpsimd", handles)

    es.close()
    return nc, P


def _host_prep(inputs):
    qkv_w = np.asarray(inputs["L0_attn_qkv_w"], np.float32)
    qkv_b = np.asarray(inputs["L0_attn_qkv_b"], np.float32)
    proj_w = np.asarray(inputs["L0_attn_proj_w"], np.float32)
    proj_b = np.asarray(inputs["L0_attn_proj_b"], np.float32)
    gate_w = np.asarray(inputs["L0_ffn_gate_w"], np.float32)
    e_bias = np.asarray(inputs["L0_ffn_e_bias"], np.float32)
    up_w = np.asarray(inputs["L0_ffn_up_w"], np.float32)
    down_w = np.asarray(inputs["L0_ffn_down_w"], np.float32)
    sup_w = np.asarray(inputs["L0_ffn_shared_up_w"], np.float32)
    sdown_w = np.asarray(inputs["L0_ffn_shared_down_w"], np.float32)

    d = {}
    Wq, Wk, Wv = qkv_w[0:D], qkv_w[D:2 * D], qkv_w[2 * D:3 * D]
    bq, bk, bv = qkv_b[0:D], qkv_b[D:2 * D], qkv_b[2 * D:3 * D]
    d["wq_p"] = _pack_k(np.ascontiguousarray(Wq.T))
    d["wk_p"] = _pack_k(np.ascontiguousarray(Wk.T))
    d["wv_p"] = _pack_k(np.ascontiguousarray(Wv.T))
    d["pw_p"] = _pack_k(np.ascontiguousarray(proj_w.T))
    bqk = np.zeros((128, 16), np.float32)
    for p in range(NP):
        bqk[:, 2 * p] = bq[128 * p:128 * (p + 1)]
        bqk[:, 2 * p + 1] = bk[128 * p:128 * (p + 1)]
    d["bqk"] = bqk
    d["bvp"] = np.ascontiguousarray(bv[None, :])
    d["pjb"] = np.ascontiguousarray(proj_b[None, :])
    d["gw_p"] = _pack_k(np.ascontiguousarray(gate_w.T))
    d["ebt"] = np.ascontiguousarray(np.broadcast_to(e_bias[None, :], (128, 32)).copy())
    upl, dnl = [], []
    for e in range(32):
        upl.append(_pack_k(np.ascontiguousarray(up_w[e].T)))
        dnl.append(_pack_k(np.ascontiguousarray(down_w[e].T)))
    supT = np.ascontiguousarray(sup_w.T)
    sdnT = np.ascontiguousarray(sdown_w.T)
    for c in range(4):
        upl.append(_pack_k(np.ascontiguousarray(supT[:, 512 * c:512 * (c + 1)])))
        dnl.append(_pack_k(np.ascontiguousarray(sdnT[512 * c:512 * (c + 1), :])))
    d["upw"] = np.ascontiguousarray(np.stack(upl))
    d["dnw"] = np.ascontiguousarray(np.stack(dnl))
    inv = 1.0 / (10000.0 ** (np.arange(0, DH, 2, dtype=np.float64) / DH))
    t = np.arange(S, dtype=np.float64)
    fr = np.concatenate([np.outer(t, inv), np.outer(t, inv)], axis=1)
    cosT = np.cos(fr).T.astype(np.float32)
    sinT = np.sin(fr).T.astype(np.float32)
    d["cost"] = np.ascontiguousarray(np.vstack([cosT, cosT]).astype(ml_dtypes.bfloat16))
    d["sint"] = np.ascontiguousarray(np.vstack([sinT, sinT]).astype(ml_dtypes.bfloat16))
    M64 = np.zeros((64, 64), np.float32)
    for i in range(32):
        M64[i, i + 32] = -1.0
        M64[i + 32, i] = 1.0
    MT = np.zeros((128, 128), np.float32)
    MT[0:64, 0:64] = M64.T
    MT[64:128, 64:128] = M64.T
    d["mt"] = MT
    d["idn"] = np.eye(128, dtype=np.float32)
    esh = np.zeros((64, 128), np.float32)
    for i in range(64):
        esh[i, 64 + i] = 1.0
    d["esh"] = esh
    d["msk"] = np.ascontiguousarray(
        (np.arange(128)[:, None] <= np.arange(128)[None, :]).astype(np.float32))
    return d


def kernel(**inputs):
    x = np.asarray(inputs["L0_attn_x"], np.float32)
    shared = _host_prep(inputs)
    nc, _ = build_nc()
    in_maps = []
    for c in range(B):
        m = dict(shared)
        m["x"] = np.ascontiguousarray(x[c])
        in_maps.append(m)
    res = run_bass_kernel_spmd(nc, in_maps, core_ids=list(range(B)))
    out = np.stack([np.asarray(r["out"], np.float32) for r in res.results])
    return out


# version 29
# speedup vs baseline: 1.0396x; 1.0396x over previous
import contextlib
import numpy as np
import ml_dtypes
import concourse.bass as bass
import concourse.mybir as mybir
from concourse.bass_utils import run_bass_kernel_spmd

F32 = mybir.dt.float32
BF16 = mybir.dt.bfloat16
AF = mybir.ActivationFunctionType
ALU = mybir.AluOpType
AX = mybir.AxisListType

B, S, D = 8, 4096, 1024
H, DH = 16, 64
NP = 8
NE = 36
NT = S // 128
NG = 2
GS = S // NG
GT = GS // 128
GC = GS // 512
NEG = -1e30

ENGS = ("sync", "tensor", "scalar", "vector", "gpsimd")

DEV = {"pairs": NP, "experts": NE, "groups": NG, "attn": True, "ffn": True,
       "out": True, "taps": [], "astage": 4}


class Plan:
    def __init__(self):
        self.ops = {e: [] for e in ENGS}
        self.cnt = {s: 0 for s in ("LI", "LX0", "LX1", "CS0", "CS1", "LC", "LD", "LB", "LW", "LA", "PU0", "PU1", "T", "SS", "V", "O")}
        self.lastwait = {e: {} for e in ENGS}

    def do(self, eng, fn, inc=None, amt=1):
        if inc is not None:
            self.cnt[inc] += amt
            self.ops[eng].append((fn, inc, amt))
            return self.cnt[inc]
        self.ops[eng].append((fn, None, 0))
        return None

    def dma(self, eng, sem, dst, src):
        return self.do(eng, (lambda E, d=dst, s=src: E.dma_start(out=d, in_=s)),
                       inc=sem, amt=16)

    def wait(self, eng, sem, val=None):
        v = self.cnt[sem] if val is None else val
        if v <= 0:
            return
        if self.lastwait[eng].get(sem, -1) >= v:
            return
        self.lastwait[eng][sem] = v
        self.ops[eng].append((("wait", sem, v), None, 0))

    def drain(self, eng):
        self.ops[eng].append((("drain",), None, 0))

    def run(self, E, eng, handles):
        for fn, inc, amt in self.ops[eng]:
            if isinstance(fn, tuple):
                if fn[0] == "wait":
                    E.wait_ge(handles[fn[1]], fn[2])
                else:
                    E.drain()
                continue
            r = fn(E)
            if inc is not None:
                r.then_inc(handles[inc], amt)

    def n_ops(self):
        return {e: len(v) for e, v in self.ops.items()}


def _pack_k(w):
    K, N = w.shape
    assert K % 128 == 0
    return np.ascontiguousarray(
        w.reshape(K // 128, 128, N).transpose(1, 0, 2).reshape(128, -1))


def build_nc():
    nc = bass.Bass("TRN2", target_bir_lowering=False, debug=False)
    ap = {}

    def din(name, shape, dt=F32):
        ap[name] = nc.dram_tensor(name, list(shape), dt, kind="ExternalInput").ap()

    din("x", (S, D))
    din("wq_p", (128, 8192)); din("wk_p", (128, 8192)); din("wv_p", (128, 8192))
    din("pw_p", (128, 8192))
    din("bqk", (128, 16)); din("bvp", (1, 1024)); din("pjb", (1, 1024))
    din("gw_p", (128, 256)); din("ebt", (128, 32))
    din("upw", (NE, 128, 4096)); din("dnw", (NE, 128, 4096))
    din("cost", (128, S), BF16); din("sint", (128, S), BF16)
    din("mt", (128, 128)); din("idn", (128, 128)); din("msk", (128, 128))
    din("esh", (64, 128))
    out_hbm = nc.dram_tensor("out", [S, D], F32, kind="ExternalOutput").ap()
    upw_bf = nc.dram_tensor("upw_bf", [NE, 128, 4096], BF16, kind="Internal").ap()
    dnw_bf = nc.dram_tensor("dnw_bf", [NE, 128, 4096], BF16, kind="Internal").ap()
    combT_d = nc.dram_tensor("combT_d", [36, S], BF16, kind="Internal").ap()

    es = contextlib.ExitStack()
    tens = {}

    def sb(name, shape, dt):
        t = es.enter_context(nc.sbuf_tensor(name, list(shape), dt))
        tens[name] = t
        return t

    def psm(name, shape, dt):
        return es.enter_context(nc.psum_tensor(name, list(shape), dt))

    xnT = [sb(f"xnT{c}", (128, S), BF16) for c in range(8)]
    acc = sb("acc", (128, GT * 1024), BF16)
    wsl = [sb(f"wsl{i}", (128, 4096), BF16) for i in range(4)]
    cstg = [sb(f"cstg{i}", (36, 128), BF16) for i in range(2)]
    stg = [sb(f"stg{i}", (128, 1024), F32) for i in range(2)]
    attw = sb("attw", (128, 2048), BF16)
    vbuf = sb("vbuf", (128, 4160), BF16)
    qrot = sb("qrot", (128, GS), BF16)
    krot = sb("krot", (128, S), BF16)
    qpre = [sb(f"qpre{i}", (128, 512), BF16) for i in range(2)]
    probs = [sb(f"probs{i}", (128, 1024), BF16) for i in range(4)]
    vt1 = sb("vt1", (128, 512), F32)
    vt2 = sb("vt2", (128, 512), F32)
    sqn = qpre[0][:]
    cbuf = sb("cbuf", (1, 64), F32)
    ncb = sb("ncb", (1, 4), BF16)
    negC = sb("negC", (128, 8), F32)
    stAB = sb("stAB", (1, 1024), BF16)
    stABw = stg[1][0:1, 0:1024]
    bcsb = sb("bcsb", (128, 1024), BF16)
    ctxab = sb("ctxab", (128, 512), BF16)
    tmpb = qpre[1][0:64, :]
    rb = sb("rb", (128, 2048), BF16)
    outs = rb[:].bitcast(F32)
    hb = sb("hb", (128, 4096), BF16)
    crowb = sb("crowb", (1, 2048), BF16)
    bvpb = crowb[0:1, 0:1024]
    pjbb = crowb[0:1, 1024:2048]
    comb4 = [sb(f"comb{i}", (128, 36), F32) for i in range(4)]
    rtr4 = sb("rtr4", (128, 1024), F32)
    accf = acc[:].bitcast(F32)
    xnf = [accf[:, 0:1024], accf[:, 1024:2048]]
    csc = [sb(f"csc{i}", (128, 512), BF16) for i in range(2)]
    css = [sb(f"css{i}", (128, 512), BF16) for i in range(2)]
    mskb = sb("mskb", (128, 128), BF16)
    mtb = sb("mtb", (128, 128), BF16)
    eshb = sb("eshb", (64, 128), BF16)
    idsb = sb("idsb", (128, 128), F32)
    onesb = sb("onesb", (128, 128), BF16)
    gwb = sb("gwb", (128, 256), BF16)
    ebt = sb("ebt_s", (128, 32), F32)
    bqk = sb("bqk_s", (128, 16), F32)
    bqks = sb("bqks", (128, 16), F32)
    bnst = sb("bnst", (128, 12), F32)
    mv = sb("mv", (128, 2), F32)
    sc1 = sb("sc1", (128, 1), F32)
    rstd = sb("rstd", (128, 1), F32)
    nmr = sb("nmr", (128, 1), F32)

    P0 = psm("P0", (128, 1024), F32)
    P1 = psm("P1", (128, 1024), F32)
    P2 = psm("P2", (128, 1024), F32)
    P3 = psm("P3", (128, 1024), F32)

    handles = {}
    for sname in ("LI", "LX0", "LX1", "CS0", "CS1", "LC", "LD", "LB", "LW", "LA", "PU0", "PU1", "T", "SS", "V", "O"):
        handles[sname] = es.enter_context(nc.semaphore(sname))

    P = Plan()

    P.dma("sync", "LI", idsb[:], ap["idn"])
    P.dma("sync", "LI", stg[0][:, 0:128], ap["msk"])
    P.dma("sync", "LI", stg[0][:, 128:256], ap["mt"])
    P.dma("sync", "LI", stg[0][:, 256:272], ap["bqk"])
    P.dma("sync", "LI", ebt[:], ap["ebt"])
    P.dma("sync", "LI", stg[0][:, 272:528], ap["gw_p"])
    P.dma("sync", "LI", stg[0][0:64, 528:656], ap["esh"])
    P.wait("vector", "LI")
    P.do("vector", lambda E: E.tensor_copy(out=mskb[:], in_=stg[0][:, 0:128]))
    P.do("vector", lambda E: E.tensor_copy(out=mtb[:], in_=stg[0][:, 128:256]))
    P.do("vector", lambda E: E.tensor_copy(out=bqk[:], in_=stg[0][:, 256:272]))
    P.do("vector", lambda E: E.tensor_copy(out=gwb[:], in_=stg[0][:, 272:528]))
    P.do("vector", lambda E: E.tensor_copy(out=eshb[:], in_=stg[0][0:64, 528:656]))
    P.do("vector", lambda E: E.memset(onesb[:], 1.0))
    P.do("vector", lambda E: E.memset(vbuf[:], 1.0))
    P.do("vector", lambda E: E.memset(rb[:], 0.0))
    P.drain("vector")
    P.do("vector", lambda E: E.tensor_copy(out=bqks[:], in_=bqk[:]))
    P.drain("vector")
    bqv = bqks[:].rearrange("p (a b) -> p a b", b=2)
    P.do("vector", lambda E: E.tensor_scalar_mul(
        out=bqv[:, :, 0:1], in0=bqv[:, :, 0:1], scalar1=0.125), inc="V")

    pump_state = {"k": 0, "cast": {}, "out": {}}
    hbf = hb[:].bitcast(F32)

    PUMP_TOT = NE * 2 * 4
    PUMP_SLOTS = 6
    obuf = [rb[:, 0:2048], acc[:, 12288:14336], acc[:, 14336:16384]]

    def _pump_slot(j):
        s = j % PUMP_SLOTS
        if s < 2:
            return hbf[:, 1024 * s:1024 * (s + 1)]
        return accf[:, 2048 + 1024 * (s - 2):2048 + 1024 * (s - 1)]

    def _pump_issue_in(j):
        if j >= PUMP_TOT:
            return
        mat = j // 4
        jc = j % 4
        e = mat // 2
        srcw = ap["upw"] if mat % 2 == 0 else ap["dnw"]
        if j >= PUMP_SLOTS:
            P.wait("scalar", "V", pump_state["cast"][j - PUMP_SLOTS])
        P.dma("scalar", "PU0", _pump_slot(j), srcw[e, :, 1024 * jc:1024 * (jc + 1)])

    def emit_pump(nchunks):
        for _ in range(nchunks):
            k = pump_state["k"]
            if k >= PUMP_TOT:
                return
            pump_state["k"] = k + 1
            mat = k // 4
            e = mat // 2
            dstw = upw_bf if mat % 2 == 0 else dnw_bf
            h = k // 2
            ob = obuf[h % 3]
            if k == 0:
                P.wait("scalar", "V", 1)
                for j in range(PUMP_SLOTS - 1):
                    _pump_issue_in(j)
            _pump_issue_in(k + PUMP_SLOTS - 1)
            P.wait("vector", "PU0", 16 * (k + 1))
            if h >= 3:
                P.wait("vector", "LB", pump_state["out"][h - 3])
            pump_state["cast"][k] = P.do(
                "vector", lambda E, k=k, ob=ob: E.tensor_copy(
                    out=ob[:, 1024 * (k % 2):1024 * (k % 2) + 1024], in_=_pump_slot(k)),
                inc="V")
            if k % 2 == 1:
                P.wait("gpsimd", "V", pump_state["cast"][k])
                pump_state["out"][h] = P.dma(
                    "gpsimd", "LB", dstw[e, :, 2048 * (h % 2):2048 * (h % 2) + 2048],
                    ob[:])

    ss_xn = [0] * NT
    v_tr = [0] * NT
    for i in range(NT):
        b = i % 2
        if i < 2:
            P.wait("sync", "V")
        else:
            P.wait("sync", "SS", ss_xn[i - 2])
        P.dma("sync", "LX0" if b == 0 else "LX1", stg[b][:], ap["x"][128 * i:128 * (i + 1), :])
        P.wait("vector", "LX0" if b == 0 else "LX1")
        P.do("vector", lambda E, b=b: E.bn_stats(out=bnst[:, 0:6], in_=stg[b][:, 0:512]))
        P.do("vector", lambda E, b=b: E.bn_stats(out=bnst[:, 6:12], in_=stg[b][:, 512:1024]))
        P.drain("vector")
        P.do("vector", lambda E: E.bn_aggr(out=mv[:], in_=bnst[:]))
        P.drain("vector")
        P.do("vector", lambda E: E.tensor_scalar_add(out=sc1[:], in0=mv[:, 1:2], scalar1=1e-5))
        P.drain("vector")
        P.do("vector", lambda E: E.reciprocal(out=sc1[:], in_=sc1[:]), inc="V")
        P.wait("scalar", "V")
        P.do("scalar", lambda E: E.activation(out=rstd[:], in_=sc1[:], func=AF.Sqrt), inc="SS")
        P.wait("vector", "SS")
        P.do("vector", lambda E: E.tensor_scalar(out=nmr[:], in0=mv[:, 0:1],
                                                 scalar1=rstd[:], scalar2=-1.0,
                                                 op0=ALU.mult, op1=ALU.mult), inc="V")
        P.wait("scalar", "V")
        if i >= 2:
            P.wait("scalar", "T")
        xb = xnf[b]
        ss_xn[i] = P.do("scalar", lambda E, b=b, xb=xb: E.activation(
            out=xb[:], in_=stg[b][:], func=AF.Identity, bias=nmr[:], scale=rstd[:]),
            inc="SS")
        P.wait("tensor", "SS")
        for c in range(8):
            half = c % 2
            reg = P3[:, 512 * half:512 * half + 128]
            if c >= 2:
                P.wait("tensor", "V", v_tr[i])
            P.do("tensor", lambda E, xb=xb, c=c, reg=reg: E.transpose(
                out=reg, in_=xb[:, 128 * c:128 * (c + 1)], identity=idsb[:]), inc="T")
            P.wait("vector", "T")
            v_tr[i] = P.do("vector", lambda E, c=c, i=i, reg=reg: E.tensor_copy(
                out=xnT[c][:, 128 * i:128 * (i + 1)], in_=reg), inc="V")
        if DEV["ffn"]:
            emit_pump(9)

    comb_lb = {}
    _steps = [
        (lambda E, r: E.tensor_add(out=r[:, 32:64], in0=r[:, 0:32], in1=ebt[:]),),
        tuple(
            (lambda E, r, a=a, bb=bb, j=j: E.tensor_add(
                out=r[:, 64 + 8 * j:72 + 8 * j],
                in0=r[:, 32:64].rearrange("p (g e) -> p g e", e=4)[:, :, a:a + 1].squeeze(2),
                in1=r[:, 32:64].rearrange("p (g e) -> p g e", e=4)[:, :, bb:bb + 1].squeeze(2)))
            for j, (a, bb) in enumerate([(0, 1), (0, 2), (0, 3), (1, 2), (1, 3), (2, 3)])),
        (lambda E, r: E.tensor_max(out=r[:, 112:120], in0=r[:, 64:72], in1=r[:, 72:80]),
         lambda E, r: E.tensor_max(out=r[:, 120:128], in0=r[:, 80:88], in1=r[:, 88:96]),
         lambda E, r: E.tensor_max(out=r[:, 128:136], in0=r[:, 96:104], in1=r[:, 104:112])),
        (lambda E, r: E.tensor_max(out=r[:, 112:120], in0=r[:, 112:120], in1=r[:, 120:128]),),
        (lambda E, r: E.tensor_max(out=r[:, 112:120], in0=r[:, 112:120], in1=r[:, 128:136]),),
        (lambda E, r: E.max(out=r[:, 120:128], in_=r[:, 112:120]),),
        (lambda E, r: E.memset(r[:, 124:128], NEG),),
        (lambda E, r: E.match_replace(out=r[:, 128:136], in_to_replace=r[:, 120:128],
                                      in_values=r[:, 112:120], imm_value=NEG),),
        (lambda E, r: E.tensor_tensor(out=r[:, 120:128], in0=r[:, 128:136],
                                      in1=r[:, 112:120], op=ALU.is_lt),),
        (lambda E, r: E.tensor_tensor(
            out=r[:, 136:168].rearrange("p (g e) -> p g e", e=4),
            in0=r[:, 32:64].rearrange("p (g e) -> p g e", e=4),
            in1=r[:, 120:128].unsqueeze(2).to_broadcast([128, 8, 4]),
            op=ALU.mult),),
        (lambda E, r: E.max(out=r[:, 168:176], in_=r[:, 136:168]),),
        (lambda E, r: E.memset(r[:, 174:176], NEG),),
        (lambda E, r: E.match_replace(out=r[:, 176:208], in_to_replace=r[:, 168:176],
                                      in_values=r[:, 136:168], imm_value=NEG),),
        (lambda E, r: E.tensor_tensor(out=r[:, 176:208], in0=r[:, 176:208],
                                      in1=r[:, 136:168], op=ALU.is_lt),),
        (lambda E, r: E.tensor_mul(out=r[:, 208:240], in0=r[:, 176:208], in1=r[:, 0:32]),),
        (lambda E, r: E.reduce_sum(out=r[:, 240:241], in_=r[:, 208:240], axis=AX.X),),
        (lambda E, r: E.tensor_scalar_add(out=r[:, 240:241], in0=r[:, 240:241],
                                          scalar1=1e-20),),
        (lambda E, r: E.reciprocal(out=r[:, 241:242], in_=r[:, 240:241]),),
    ]
    for q in range(NT // 4):
        tiles = [4 * q + t for t in range(4)]
        P.wait("tensor", "V")
        P.wait("tensor", "SS")
        for t, i in enumerate(tiles):
            for c in range(8):
                P.do("tensor", lambda E, c=c, i=i, t=t: E.matmul(
                    out=P3[:, 512 + 32 * t:544 + 32 * t],
                    lhsT=xnT[c][:, 128 * i:128 * (i + 1)],
                    rhs=gwb[:, 32 * c:32 * (c + 1)],
                    start=(c == 0), stop=(c == 7)), inc=("T" if c == 7 else None))
        P.wait("scalar", "T")
        for t in range(4):
            P.do("scalar", lambda E, t=t: E.activation(
                out=rtr4[:, 256 * t:256 * t + 32], in_=P3[:, 512 + 32 * t:544 + 32 * t],
                func=AF.Sigmoid), inc="SS")
        P.wait("vector", "SS")
        for step in _steps:
            for t in range(4):
                for fn in step:
                    P.do("vector", lambda E, fn=fn, t=t: fn(
                        E, rtr4[:, 256 * t:256 * (t + 1)]))
            P.drain("vector")
        P.wait("vector", "T")
        for t in range(4):
            r4 = rtr4[:, 256 * t:256 * (t + 1)]
            P.do("vector", lambda E, cb=comb4[t], r4=r4: E.tensor_scalar(
                out=cb[:, 0:32], in0=r4[:, 208:240],
                scalar1=r4[:, 241:242], scalar2=2.5,
                op0=ALU.mult, op1=ALU.mult))
            P.do("vector", lambda E, cb=comb4[t]: E.memset(cb[:, 32:36], 1.0))
        P.drain("vector")
        P.do("vector", lambda E: E.nop(), inc="V")
        for t, i in enumerate(tiles):
            P.wait("tensor", "V")
            P.do("tensor", lambda E, cb=comb4[t]: E.transpose(
                out=P3[0:36, 384:512], in_=cb[:], identity=idsb[:]), inc="T")
            P.wait("vector", "T")
            if i >= 2:
                P.wait("vector", "LB", comb_lb[i - 2])
            P.do("vector", lambda E, i=i: E.tensor_copy(
                out=cstg[i % 2][:], in_=P3[0:36, 384:512]), inc="V")
            P.wait("gpsimd", "V")
            comb_lb[i] = P.dma("gpsimd", "LB", combT_d[0:36, 128 * i:128 * (i + 1)],
                               cstg[i % 2][:])

    v_out = 0
    prev_last_exp = [0]
    t_attw_free = [0]
    v_stg_free = [0]
    att_end_prev = [0]
    prev_up_end = [None]
    krotf = krot[:].bitcast(F32)
    for g in range(DEV["groups"]):
        KC = 4 * (g + 1)
        KT = 16 * (g + 1)

        P.wait("scalar", "T", att_end_prev[0])
        v_pw = {}
        for hchunk in range(8):
            sl = krotf[:, 1024 * (hchunk % 2):1024 * (hchunk % 2) + 1024]
            if hchunk >= 2:
                P.wait("scalar", "V", v_pw[hchunk - 2])
            P.dma("scalar", "PU1", sl,
                  ap["pw_p"][:, 1024 * hchunk:1024 * (hchunk + 1)])
            P.wait("vector", "PU1")
            if prev_up_end[0] is not None:
                P.wait("vector", "T", prev_up_end[0][hchunk // 4])
            dstsl = wsl[hchunk // 4]
            off = 1024 * (hchunk % 4)
            v_pw[hchunk] = P.do("vector", lambda E, s=sl, d=dstsl, off=off: E.tensor_copy(
                out=d[:, off:off + 1024], in_=s[:]), inc="V")
        if g == 1:
            P.wait("vector", "O", v_out)
        P.wait("sync", "T")
        P.wait("sync", "V")
        P.wait("sync", "SS")
        P.dma("sync", "LX1", stg[1][0:1, 0:1024], ap["bvp"])
        P.wait("vector", "LX1")
        P.wait("vector", "LC")
        P.wait("vector", "T")
        P.do("vector", lambda E: E.tensor_copy(out=bvpb, in_=stg[1][0:1, 0:1024]), inc="V")
        P.wait("sync", "V")
        P.dma("sync", "LX1", stg[1][0:1, 0:1024], ap["pjb"])
        P.wait("vector", "LX1")
        P.do("vector", lambda E: E.tensor_copy(out=pjbb, in_=stg[1][0:1, 0:1024]), inc="V")
        v_stg_free[0] = P.cnt["V"]

        for p in range(DEV["pairs"] if DEV["attn"] else 0):
            P.wait("scalar", "T", t_attw_free[0])
            P.wait("scalar", "V", v_stg_free[0])
            for dc in range(8):
                P.dma("scalar", "LA", stg[0][:, 128 * dc:128 * (dc + 1)],
                      ap["wq_p"][:, 1024 * dc + 128 * p:1024 * dc + 128 * (p + 1)])
            P.wait("vector", "LA")
            P.wait("vector", "T", t_attw_free[0])
            v_attw = P.do("vector", lambda E: E.tensor_copy(out=attw[:, 0:1024], in_=stg[0][:]),
                          inc="V")
            P.wait("scalar", "V", v_attw)
            for dc in range(8):
                P.dma("scalar", "LA", stg[0][:, 128 * dc:128 * (dc + 1)],
                      ap["wk_p"][:, 1024 * dc + 128 * p:1024 * dc + 128 * (p + 1)])
            P.wait("vector", "LA")
            v_attw_k = P.do("vector", lambda E: E.tensor_copy(out=attw[:, 1024:2048], in_=stg[0][:]),
                            inc="V")

            ss_pre = {}
            ropev = []
            reg_of = {}
            ckc = [0]
            t_q_end = 0
            for kind in (("q",) if DEV.get("skipk") else ("q", "k")):
                if kind == "k":
                    t_q_end = P.cnt["T"]
                    P.wait("scalar", "V", v_attw_k)
                    for dc in range(8):
                        P.dma("scalar", "LA", stg[0][:, 128 * dc:128 * (dc + 1)],
                              ap["wv_p"][:, 1024 * dc + 128 * p:1024 * dc + 128 * (p + 1)])
                    P.wait("vector", "LA")
                    P.wait("vector", "T", t_q_end)
                    v_attw = P.do("vector", lambda E: E.tensor_copy(out=attw[:, 0:1024],
                                                                    in_=stg[0][:]), inc="V")
                half = 0 if kind == "q" else 1024
                nch = GC if kind == "q" else KC

                def emit_qk_mms(ci, kind=kind, p=p, half=half):
                    c = (GC * g + ci) if kind == "q" else ci
                    b = ci % 2
                    rot = ckc[0] % 3
                    ckc[0] += 1
                    reg_of[(kind, ci)] = rot
                    P.wait("sync", "V")
                    P.dma("sync", "CS0" if b == 0 else "CS1", csc[b][:], ap["cost"][:, 512 * c:512 * (c + 1)])
                    P.dma("sync", "CS0" if b == 0 else "CS1", css[b][:], ap["sint"][:, 512 * c:512 * (c + 1)])
                    qkreg = (P3, P0, P1)[rot][:, 0:512]
                    wv_need = v_attw if kind == "q" else v_attw_k
                    if len(ropev) >= 3:
                        P.wait("tensor", "V", max(wv_need, ropev[-3]))
                    else:
                        P.wait("tensor", "V", wv_need)
                    if (kind, ci - 3) in ss_pre:
                        P.wait("tensor", "SS", ss_pre[(kind, ci - 3)])
                    elif kind == "k" and ("q", GC - 3 + ci) in ss_pre:
                        P.wait("tensor", "SS", ss_pre[("q", GC - 3 + ci)])
                    for dc in range(8):
                        P.do("tensor", lambda E, dc=dc, c=c, qkreg=qkreg, half=half: E.matmul(
                            out=qkreg, lhsT=attw[:, half + 128 * dc:half + 128 * (dc + 1)],
                            rhs=xnT[dc][:, 512 * c:512 * (c + 1)],
                            start=(dc == 0), stop=(dc == 7)),
                            inc=("T" if dc == 7 else None))
                    return P.cnt["T"]

                def emit_qk_tail(ci, kind=kind, p=p):
                    b = ci % 2
                    rot = reg_of[(kind, ci)]
                    qkreg = (P3, P0, P1)[rot][:, 0:512]
                    rreg = (P3, P0, P1)[rot][:, 512:1024]
                    P.wait("scalar", "T", qk_t[(kind, ci)])
                    if len(ropev) >= 2:
                        P.wait("scalar", "V", ropev[-2])
                    bias = bqks[:, 2 * p:2 * p + 1] if kind == "q" else bqk[:, 2 * p + 1:2 * p + 2]
                    scale = 0.125 if kind == "q" else 1.0
                    ss_pre[(kind, ci)] = P.do(
                        "scalar", lambda E, b=b, bias=bias, scale=scale, qkreg=qkreg:
                        E.activation(out=qpre[b][:], in_=qkreg, func=AF.Identity,
                                     bias=bias, scale=scale), inc="SS")
                    P.wait("tensor", "SS", ss_pre[(kind, ci)])
                    if len(ropev) >= 2:
                        P.wait("tensor", "V", ropev[-2])
                    P.do("tensor", lambda E, b=b, rreg=rreg: E.matmul(
                        out=rreg, lhsT=mtb[:], rhs=qpre[b][:], start=True, stop=True),
                        inc="T")
                    dst = qrot if kind == "q" else krot
                    dof = 512 * ci if kind == "q" else 512 * ci
                    P.wait("vector", "T")
                    P.wait("vector", "CS0" if b == 0 else "CS1")
                    P.drain("vector")
                    P.do("vector", lambda E, b=b: E.tensor_mul(
                        out=vt1[:], in0=qpre[b][:], in1=csc[b][:]))
                    P.do("vector", lambda E, b=b, rreg=rreg: E.tensor_tensor(
                        out=vt2[:], in0=rreg, in1=css[b][:], op=ALU.mult))
                    P.drain("vector")
                    ropev.append(P.do("vector", lambda E, dst=dst, dof=dof: E.tensor_add(
                        out=dst[:, dof:dof + 512], in0=vt1[:], in1=vt2[:]), inc="V"))

                qk_t = getattr(P, "_qk_t", None)
                if qk_t is None:
                    qk_t = {}
                    P._qk_t = qk_t
                for ci in range(nch + 1):
                    if ci < nch:
                        qk_t[(kind, ci)] = emit_qk_mms(ci)
                    if ci - 1 >= 0:
                        emit_qk_tail(ci - 1)
            if DEV["astage"] < 2:
                continue
            P.wait("tensor", "V")
            v_ev = [0] * KT
            for tt in range(KT):
                if g == 0 and DEV["ffn"]:
                    emit_pump(6)
                vreg = (P2[:, 0:128], P2[:, 640:768],
                        P0[:, 0:128], P0[:, 640:768])[tt % 4]
                if tt >= 4:
                    P.wait("tensor", "V", v_ev[tt - 4])
                for dc in range(8):
                    P.do("tensor", lambda E, dc=dc, tt=tt, vreg=vreg: E.matmul(
                        out=vreg, lhsT=xnT[dc][:, 128 * tt:128 * (tt + 1)],
                        rhs=attw[:, 128 * dc:128 * (dc + 1)],
                        start=(dc == 0), stop=False))
                P.do("tensor", lambda E, p=p, vreg=vreg: E.matmul(
                    out=vreg, lhsT=onesb[0:1, 0:128],
                    rhs=bvpb[:, 128 * p:128 * (p + 1)],
                    start=False, stop=True), inc="T")
                P.wait("vector", "T")
                P.do("vector", lambda E, tt=tt, vreg=vreg: E.tensor_copy(
                    out=vbuf[:, 130 * tt:130 * tt + 64], in_=vreg[:, 0:64]))
                v_ev[tt] = P.do("vector", lambda E, tt=tt, vreg=vreg: E.tensor_copy(
                    out=vbuf[:, 130 * tt + 65:130 * tt + 129], in_=vreg[:, 64:128]), inc="V")
            t_attw_free[0] = P.cnt["T"]
            v_stg_free[0] = v_attw

            if DEV["astage"] < 3:
                continue
            if True:
              for kind, nch, buf in (("q", GC, qrot), ("k", KC, krot)):
                nmm_t = {}
                sq_v = {}
                red_v = {}
                for ci in range(nch + 2):
                    if ci < nch:
                        sqb = qpre[ci % 2]
                        P.wait("vector", "T", nmm_t.get(ci - 2, 0))
                        sq_v[ci] = P.do("vector", lambda E, buf=buf, ci=ci, sqb=sqb: E.tensor_mul(
                            out=sqb[:], in0=buf[:, 512 * ci:512 * (ci + 1)],
                            in1=buf[:, 512 * ci:512 * (ci + 1)]), inc="V")
                    if 0 <= ci - 1 < nch:
                        c2 = ci - 1
                        sqb = qpre[c2 % 2]
                        par = 512 * (c2 % 2)
                        P.wait("tensor", "V", sq_v[c2])
                        P.wait("tensor", "V", red_v.get(c2 - 2, 0))
                        P.do("tensor", lambda E, sqb=sqb, par=par: E.matmul(
                            out=P2[0:1, par:par + 512], lhsT=onesb[0:64, 0:1], rhs=sqb[0:64, :],
                            start=True, stop=True, tile_position=(0, 0)))
                        nmm_t[c2] = P.do("tensor", lambda E, sqb=sqb, par=par: E.matmul(
                            out=P1[0:1, par:par + 512], lhsT=onesb[64:128, 0:1], rhs=sqb[64:128, :],
                            start=True, stop=True, tile_position=(64, 0)), inc="T")
                    if 0 <= ci - 2 < nch:
                        c2 = ci - 2
                        par = 512 * (c2 % 2)
                        P.wait("vector", "T", nmm_t[c2])
                        ofA = (0 if kind == "q" else 16) + c2
                        ofB = (8 if kind == "q" else 32) + c2
                        P.do("vector", lambda E, ofA=ofA, par=par: E.reduce_max(
                            out=cbuf[0:1, ofA:ofA + 1], in_=P2[0:1, par:par + 512], axis=AX.X))
                        red_v[c2] = P.do("vector", lambda E, ofB=ofB, par=par: E.reduce_max(
                            out=cbuf[0:1, ofB:ofB + 1], in_=P1[0:1, par:par + 512], axis=AX.X), inc="V")
            P.drain("vector")
            P.do("vector", lambda E: E.reduce_max(out=cbuf[0:1, 40:41], in_=cbuf[0:1, 0:GC], axis=AX.X))
            P.do("vector", lambda E: E.reduce_max(out=cbuf[0:1, 41:42], in_=cbuf[0:1, 8:8 + GC], axis=AX.X))
            P.do("vector", lambda E, KC=KC: E.reduce_max(out=cbuf[0:1, 42:43], in_=cbuf[0:1, 16:16 + KC], axis=AX.X))
            P.do("vector", lambda E, KC=KC: E.reduce_max(out=cbuf[0:1, 43:44], in_=cbuf[0:1, 32:32 + KC], axis=AX.X))
            P.drain("vector")
            P.do("vector", lambda E: E.tensor_max(out=cbuf[0:1, 40:41], in0=cbuf[0:1, 40:41], in1=cbuf[0:1, 41:42]))
            P.do("vector", lambda E: E.tensor_max(out=cbuf[0:1, 42:43], in0=cbuf[0:1, 42:43], in1=cbuf[0:1, 43:44]))
            P.drain("vector")
            P.do("vector", lambda E: E.tensor_add(out=cbuf[0:1, 44:45], in0=cbuf[0:1, 40:41], in1=cbuf[0:1, 42:43]))
            P.drain("vector")
            P.wait("vector", "SS", prev_last_exp[0])
            P.do("vector", lambda E: E.tensor_scalar_mul(out=ncb[0:1, 0:1], in0=cbuf[0:1, 44:45], scalar1=-1.0 / 16.0), inc="V")
            P.wait("tensor", "V")
            P.do("tensor", lambda E: E.matmul(out=P3[:, 0:1], lhsT=onesb[0:1, 0:128],
                                              rhs=ncb[0:1, 0:1], start=True, stop=True), inc="T")
            P.wait("vector", "T")
            P.do("vector", lambda E: E.tensor_copy(out=negC[:, 0:1], in_=P3[:, 0:1]), inc="V")
            v_negc = P.cnt["V"]
            P.wait("scalar", "V")

            if DEV["astage"] < 4:
                continue
            exps_all = [0, 0, 0]
            ctx_all = [0, 0]
            v_probs = {}
            sc_t = {}
            ctx_of = {}
            gof = [0]
            prev_proj_v = [0]
            for sq in range(4):
                sjg = 4 * g + sq
                ntc = 4 * sjg + 4
                gb = gof[0]
                t_ctx_last = 0
                diag_of = {}

                def emit_sc(tc, sq=sq, gb=gb):
                    PS = (P0, P1, P3)[tc % 3]
                    if PS is P3:
                        P.wait("tensor", "V", v_negc)
                    P.wait("tensor", "SS", exps_all[-2])
                    P.do("tensor", lambda E, tc=tc, sq=sq, PS=PS: E.matmul(
                        out=PS[:, 0:512], lhsT=krot[0:64, 128 * tc:128 * (tc + 1)],
                        rhs=qrot[0:64, 512 * sq:512 * (sq + 1)],
                        start=True, stop=True, tile_position=(0, 0)))
                    sc_t[gb + tc] = P.do("tensor", lambda E, tc=tc, sq=sq, PS=PS: E.matmul(
                        out=PS[:, 512:1024], lhsT=krot[64:128, 128 * tc:128 * (tc + 1)],
                        rhs=qrot[64:128, 512 * sq:512 * (sq + 1)],
                        start=True, stop=True, tile_position=(64, 0)), inc="T")

                def emit_exp(tc, gb=gb):
                    PS = (P0, P1, P3)[tc % 3]
                    gt = gb + tc
                    pr = probs[gt % 4]
                    P.wait("scalar", "T", sc_t[gt])
                    P.wait("scalar", "T", ctx_of.get(gt - 4, 0))
                    if (gt - 4) in v_probs:
                        P.wait("scalar", "V", v_probs[gt - 4])
                    ev = P.do("scalar", lambda E, PS=PS, pr=pr: E.activation(
                        out=pr[:], in_=PS[:], func=AF.Exp, bias=negC[:, 0:1]), inc="SS")
                    exps_all.append(ev)
                    r = tc - 4 * sjg
                    if r >= 0:
                        diag_of[tc] = r
                        P.wait("vector", "SS", ev)
                        for half in (0, 512):
                            P.do("vector", lambda E, pr=pr, r=r, half=half: E.tensor_tensor(
                                out=pr[:, half + 128 * r:half + 128 * (r + 1)],
                                in0=pr[:, half + 128 * r:half + 128 * (r + 1)], in1=mskb[:],
                                op=ALU.mult))
                            if r > 0:
                                P.do("vector", lambda E, pr=pr, r=r, half=half: E.memset(
                                    pr[:, half:half + 128 * r], 0.0))
                        P.drain("vector")
                        P.do("vector", lambda E: E.nop(), inc="V")
                        v_probs[gt] = P.cnt["V"]

                exp_of = {}
                for i in range(ntc + 2):
                    if i < ntc:
                        emit_sc(i)
                    if 0 <= i - 1 < ntc:
                        emit_exp(i - 1)
                        exp_of[i - 1] = exps_all[-1]
                    if 0 <= i - 2 < ntc:
                        tcx = i - 2
                        gt = gb + tcx
                        pr = probs[gt % 4]
                        if tcx == 0:
                            P.wait("tensor", "V", prev_proj_v[0])
                        if tcx in diag_of:
                            P.wait("tensor", "V", v_probs[gt])
                        else:
                            P.wait("tensor", "SS", exp_of[tcx])
                        st = (tcx == 0)
                        sp = (tcx == ntc - 1)
                        P.do("tensor", lambda E, tc=tcx, pr=pr, st=st, sp=sp: E.matmul(
                            out=P2[0:65, 0:512], lhsT=vbuf[:, 130 * tc:130 * tc + 65],
                            rhs=pr[:, 0:512], start=st, stop=sp))
                        t_ctx_last = P.do("tensor", lambda E, tc=tcx, pr=pr, st=st, sp=sp: E.matmul(
                            out=P2[0:65, 512:1024], lhsT=vbuf[:, 130 * tc + 65:130 * tc + 130],
                            rhs=pr[:, 512:1024], start=st, stop=sp), inc="T")
                        ctx_all.append(t_ctx_last)
                        ctx_of[gt] = t_ctx_last
                P.wait("vector", "T", t_ctx_last)
                P.do("vector", lambda E: E.tensor_copy(out=tmpb, in_=P2[0:64, 512:1024]),
                     inc="V")
                P.wait("scalar", "T", t_ctx_last)
                P.do("scalar", lambda E: E.activation(out=stABw, in_=P2[64:65, 0:1024],
                                                      func=AF.Ln))
                P.drain("scalar")
                P.do("scalar", lambda E: E.activation(out=stAB[:, 0:1024], in_=stABw,
                                                      func=AF.Exp, scale=-1.0), inc="SS")
                P.wait("tensor", "V")
                P.wait("tensor", "SS")
                P.do("tensor", lambda E: E.matmul(out=P3[:, 0:512], lhsT=onesb[0:1, 0:128],
                                                  rhs=stAB[0:1, 0:512], start=True, stop=True))
                P.do("tensor", lambda E: E.matmul(out=P3[:, 512:1024], lhsT=onesb[0:1, 0:128],
                                                  rhs=stAB[0:1, 512:1024], start=True, stop=True))
                P.do("tensor", lambda E: E.matmul(out=P2[:, 512:1024], lhsT=eshb[:],
                                                  rhs=tmpb, start=True, stop=True), inc="T")
                P.wait("vector", "T")
                P.do("vector", lambda E: E.tensor_copy(out=bcsb[:], in_=P3[:, 0:1024]))
                P.drain("vector")
                P.do("vector", lambda E: E.tensor_mul(out=ctxab[0:64, :], in0=P2[0:64, 0:512],
                                                      in1=bcsb[0:64, 0:512]))
                v_ctx_scale_prev = P.do("vector", lambda E: E.tensor_mul(
                    out=ctxab[64:128, :], in0=P2[64:128, 512:1024],
                    in1=bcsb[64:128, 512:1024]), inc="V")
                P.wait("tensor", "V", v_ctx_scale_prev)
                for msub in range(4):
                    preg = P3 if msub % 2 == 0 else P2
                    jloc = 4 * sq + msub
                    if msub >= 2:
                        P.wait("tensor", "V")
                    lastm = None
                    for n in range(2):
                        last = (n == 1 and p > 0)
                        P.do("tensor", lambda E, p=p, n=n, msub=msub, preg=preg: E.matmul(
                            out=preg[:, 512 * n:512 * (n + 1)],
                            lhsT=ctxab[:, 128 * msub:128 * (msub + 1)],
                            rhs=wsl[p // 4][:, 1024 * (p % 4) + 512 * n:1024 * (p % 4) + 512 * (n + 1)],
                            start=True, stop=(p > 0)), inc=("T" if last else None))
                        if p == 0:
                            P.do("tensor", lambda E, n=n, preg=preg: E.matmul(
                                out=preg[:, 512 * n:512 * (n + 1)], lhsT=onesb[0:1, 0:128],
                                rhs=pjbb[:, 512 * n:512 * (n + 1)],
                                start=False, stop=True), inc=("T" if n == 1 else None))
                    P.wait("vector", "T")
                    if p == 0:
                        P.do("vector", lambda E, jloc=jloc, preg=preg: E.tensor_copy(
                            out=acc[:, 1024 * jloc:1024 * (jloc + 1)], in_=preg[:]), inc="V")
                    else:
                        P.do("vector", lambda E, jloc=jloc, preg=preg: E.tensor_tensor(
                            out=acc[:, 1024 * jloc:1024 * (jloc + 1)],
                            in0=acc[:, 1024 * jloc:1024 * (jloc + 1)], in1=preg[:],
                            op=ALU.add), inc="V")
                    prev_proj_v[0] = P.cnt["V"]
                gof[0] += ntc
            prev_last_exp[0] = exps_all[-1]
        def load_expert(e, which, tfree):
            sl = wsl[(e % 2) if which == "up" else 2 + (e % 2)]
            srcw = upw_bf if which == "up" else dnw_bf
            eng, sem = ("sync", "LW") if which == "up" else ("gpsimd", "LD")
            P.wait(eng, "T", tfree)
            need_h = 4 * e + (1 if which == "up" else 3)
            P.wait(eng, "LB", pump_state["out"][need_h])
            cnt = P.dma(eng, sem, sl[:], srcw[e])
            return [(sem, cnt)]

        t_att_end = P.cnt["T"]
        att_end_prev[0] = t_att_end
        if g == 0 and DEV["ffn"]:
            emit_pump(NE * 2 * 8)
        P.wait("scalar", "O")
        P.wait("scalar", "LB")
        if not DEV["attn"] or DEV["pairs"] == 0:
            P.do("vector", lambda E: E.memset(acc[:], 0.0), inc="V")
        NE_RUN = DEV["experts"] if DEV["ffn"] else 0
        up_v = load_expert(0, "up", t_att_end) if NE_RUN else None
        dn_v = load_expert(0, "dn", t_att_end) if NE_RUN else None
        dn_waits = {0: dn_v}
        up_waits = {0: up_v}
        if NE_RUN:
            P.wait("scalar", "T")
            P.wait("scalar", "V")
            P.dma("scalar", "LC", crowb[:], combT_d[0:1, GS * g:GS * (g + 1)])

        t_up_end = [0] * NE
        t_dn_end = [0] * NE
        down_t_done = {}
        v_hb = {}
        v_dn = {}
        ss_sq = {}
        dslots = [P3[:, 0:512], P3[:, 512:1024], P2[:, 512:1024]]
        dslot_last = {}
        epr = None
        pend_down = []
        for e in range(NE_RUN):
            def emit_down(e, tc):
                for sm, vv in (dn_waits.get(e) or []):
                    P.wait("tensor", sm, vv)
                P.wait("tensor", "V", v_hb[(e, tc, 3)])
                for tt in range(4):
                    jloc = 4 * tc + tt
                    for n in range(2):
                        slot = dslots[(2 * (4 * tc + tt) + n) % 3]
                        key_prev = dslot_last.get((2 * (4 * tc + tt) + n) % 3)
                        if key_prev is not None:
                            P.wait("tensor", "V", key_prev)
                        ho = 2048 * (tc % 2)
                        for i in range(4):
                            P.do("tensor", lambda E, i=i, n=n, tt=tt, e=e, slot=slot, ho=ho: E.matmul(
                                out=slot,
                                lhsT=hb[:, ho + 512 * i + 128 * tt:ho + 512 * i + 128 * (tt + 1)],
                                rhs=wsl[2 + e % 2][:, 1024 * i + 512 * n:1024 * i + 512 * (n + 1)],
                                start=(i == 0), stop=(i == 3)),
                                inc=("T" if i == 3 else None))
                        P.wait("vector", "T")
                        vv = P.do("vector", lambda E, jloc=jloc, n=n, slot=slot: E.tensor_tensor(
                            out=acc[:, 1024 * jloc + 512 * n:1024 * jloc + 512 * (n + 1)],
                            in0=acc[:, 1024 * jloc + 512 * n:1024 * jloc + 512 * (n + 1)],
                            in1=slot, op=ALU.add), inc="V")
                        dslot_last[(2 * (4 * tc + tt) + n) % 3] = vv
                        v_dn[(e, tc, tt, n)] = vv
                v_dn[(e, tc)] = v_dn[(e, tc, 3, 1)]
                down_t_done[e] = P.cnt["T"]

            for tc in range(GC):
                TC = GS * g + 512 * tc
                pend_down.append((e, tc))
                t_dn_guard = 0
                if len(pend_down) >= 3:
                    de, dtc = pend_down.pop(0)
                    emit_down(de, dtc)
                    t_dn_guard = P.cnt["T"]
                t_up_i = {}
                for i in (2, 3, 0, 1):
                    reg = (P1 if i >= 2 else P0)[:, 512 * (i % 2):512 * (i % 2) + 512]
                    if (e, tc - 1, i) in ss_sq:
                        P.wait("tensor", "SS", ss_sq[(e, tc - 1, i)])
                    elif tc == 0 and (epr, GC - 1, i) in ss_sq:
                        P.wait("tensor", "SS", ss_sq[(epr, GC - 1, i)])
                    if tc == 0:
                        for sm, vv in (up_waits.get(e) or []):
                            P.wait("tensor", sm, vv)
                    for dc in range(8):
                        P.do("tensor", lambda E, dc=dc, i=i, TC=TC, reg=reg, e=e: E.matmul(
                            out=reg, lhsT=wsl[e % 2][:, 512 * dc + 128 * i:512 * dc + 128 * (i + 1)],
                            rhs=xnT[dc][:, TC:TC + 512],
                            start=(dc == 0), stop=(dc == 7)),
                            inc=("T" if dc == 7 else None))
                    t_up_i[i] = P.cnt["T"]
                t_up_this = P.cnt["T"]
                if tc == GC - 1:
                    t_up_end[e] = t_up_this
                if (e, tc - 1, 3) in v_hb:
                    P.wait("tensor", "V", v_hb[(e, tc - 1, 3)])
                elif tc == 0 and (epr, GC - 1, 3) in v_hb:
                    P.wait("tensor", "V", v_hb[(epr, GC - 1, 3)])
                P.wait("tensor", "LC")
                P.do("tensor", lambda E, tc=tc: E.matmul(
                    out=P2[:, 0:512], lhsT=onesb[0:1, 0:128],
                    rhs=crowb[0:1, 512 * tc:512 * (tc + 1)],
                    start=True, stop=True), inc="T")
                t_bc = P.cnt["T"]
                if tc == 0 and e + 1 < NE_RUN:
                    up_waits[e + 1] = load_expert(e + 1, "up", P.cnt["T"])
                if tc == 2 and e + 1 < NE_RUN:
                    dn_waits[e + 1] = load_expert(
                        e + 1, "dn", down_t_done.get(e - 1, 0) if e >= 1 else P.cnt["T"])
                P.wait("scalar", "T", t_dn_guard)
                if (e, tc - 2, 3) in v_hb:
                    P.wait("scalar", "V", v_hb[(e, tc - 2, 3)])
                elif tc < 2 and epr is not None and (epr, GC - 2 + tc, 3) in v_hb:
                    P.wait("scalar", "V", v_hb[(epr, GC - 2 + tc, 3)])
                ho = 2048 * (tc % 2)
                for ia, ib in ((2, 3), (0, 1)):
                    P.wait("scalar", "T", t_up_i[ib])
                    for i in (ia, ib):
                        reg = (P1 if i >= 2 else P0)[:, 512 * (i % 2):512 * (i % 2) + 512]
                        P.do("scalar", lambda E, i=i, reg=reg: E.activation(
                            out=rb[:, 512 * i:512 * (i + 1)], in_=reg, func=AF.Relu))
                    P.drain("scalar")
                    for i in (ia, ib):
                        ss_sq[(e, tc, i)] = P.do("scalar", lambda E, i=i, ho=ho: E.activation(
                            out=hb[:, ho + 512 * i:ho + 512 * (i + 1)], in_=rb[:, 512 * i:512 * (i + 1)],
                            func=AF.Square), inc="SS")
                P.wait("vector", "T", t_bc)
                P.do("vector", lambda E: E.tensor_copy(out=bcsb[:, 0:512], in_=P2[:, 0:512]),
                     inc="V")
                P.wait("vector", "SS")
                for i in range(4):
                    v_hb[(e, tc, i)] = P.do("vector", lambda E, i=i, ho=ho: E.tensor_tensor(
                        out=hb[:, ho + 512 * i:ho + 512 * (i + 1)],
                        in0=hb[:, ho + 512 * i:ho + 512 * (i + 1)],
                        in1=bcsb[:, 0:512], op=ALU.mult), inc="V")
            if e == NE_RUN - 1:
                while pend_down:
                    de, dtc = pend_down.pop(0)
                    emit_down(de, dtc)
            t_dn_end[e] = P.cnt["T"]
            epr = e
            if e + 1 < NE_RUN:
                P.wait("scalar", "T", t_dn_end[e])
                P.dma("scalar", "LC", crowb[:], combT_d[e + 1:e + 2, GS * g:GS * (g + 1)])

        prev_up_end[0] = (t_up_end[NE_RUN - 2], t_up_end[NE_RUN - 1]) if NE_RUN >= 2 else None
        o_st = [0] * GT
        for j in range(GT if DEV["out"] else 0):
            bj = j % 2
            ob2 = outs if j % 2 == 0 else hbf[:, 0:1024]
            P.wait("sync", "V")
            P.dma("sync", "LX0" if bj == 0 else "LX1", stg[bj][:], ap["x"][GS * g + 128 * j:GS * g + 128 * (j + 1), :])
            P.wait("vector", "LX0" if bj == 0 else "LX1")
            if j >= 2:
                P.wait("vector", "O", o_st[j - 2])
            P.do("vector", lambda E, bj=bj, j=j, ob2=ob2: E.tensor_tensor(
                out=ob2, in0=stg[bj][:], in1=acc[:, 1024 * j:1024 * (j + 1)],
                op=ALU.add), inc="V")
            P.wait("gpsimd", "V")
            o_st[j] = P.dma("gpsimd", "O",
                            out_hbm[GS * g + 128 * j:GS * g + 128 * (j + 1), :], ob2)
        v_out = P.cnt["O"]
        v_stg_free[0] = P.cnt["V"]

    for tname in DEV["taps"]:
        t = tens[tname]
        shp = [t.shape[0], t.shape[1]]
        tap = nc.dram_tensor("tap_" + tname, shp, t[:].dtype, kind="ExternalOutput").ap()
        P.wait("gpsimd", "V")
        P.wait("gpsimd", "SS")
        P.wait("gpsimd", "T")
        P.dma("gpsimd", "O", tap, t[:])

    P.wait("gpsimd", "O")

    with nc.Block() as block:
        @block.sync
        def _(E):
            P.run(E, "sync", handles)

        @block.tensor
        def _(E):
            P.run(E, "tensor", handles)

        @block.scalar
        def _(E):
            P.run(E, "scalar", handles)

        @block.vector
        def _(E):
            P.run(E, "vector", handles)

        @block.gpsimd
        def _(E):
            P.run(E, "gpsimd", handles)

    es.close()
    return nc, P


def _host_prep(inputs):
    qkv_w = np.asarray(inputs["L0_attn_qkv_w"], np.float32)
    qkv_b = np.asarray(inputs["L0_attn_qkv_b"], np.float32)
    proj_w = np.asarray(inputs["L0_attn_proj_w"], np.float32)
    proj_b = np.asarray(inputs["L0_attn_proj_b"], np.float32)
    gate_w = np.asarray(inputs["L0_ffn_gate_w"], np.float32)
    e_bias = np.asarray(inputs["L0_ffn_e_bias"], np.float32)
    up_w = np.asarray(inputs["L0_ffn_up_w"], np.float32)
    down_w = np.asarray(inputs["L0_ffn_down_w"], np.float32)
    sup_w = np.asarray(inputs["L0_ffn_shared_up_w"], np.float32)
    sdown_w = np.asarray(inputs["L0_ffn_shared_down_w"], np.float32)

    d = {}
    Wq, Wk, Wv = qkv_w[0:D], qkv_w[D:2 * D], qkv_w[2 * D:3 * D]
    bq, bk, bv = qkv_b[0:D], qkv_b[D:2 * D], qkv_b[2 * D:3 * D]
    d["wq_p"] = _pack_k(np.ascontiguousarray(Wq.T))
    d["wk_p"] = _pack_k(np.ascontiguousarray(Wk.T))
    d["wv_p"] = _pack_k(np.ascontiguousarray(Wv.T))
    d["pw_p"] = _pack_k(np.ascontiguousarray(proj_w.T))
    bqk = np.zeros((128, 16), np.float32)
    for p in range(NP):
        bqk[:, 2 * p] = bq[128 * p:128 * (p + 1)]
        bqk[:, 2 * p + 1] = bk[128 * p:128 * (p + 1)]
    d["bqk"] = bqk
    d["bvp"] = np.ascontiguousarray(bv[None, :])
    d["pjb"] = np.ascontiguousarray(proj_b[None, :])
    d["gw_p"] = _pack_k(np.ascontiguousarray(gate_w.T))
    d["ebt"] = np.ascontiguousarray(np.broadcast_to(e_bias[None, :], (128, 32)).copy())
    upl, dnl = [], []
    for e in range(32):
        upl.append(_pack_k(np.ascontiguousarray(up_w[e].T)))
        dnl.append(_pack_k(np.ascontiguousarray(down_w[e].T)))
    supT = np.ascontiguousarray(sup_w.T)
    sdnT = np.ascontiguousarray(sdown_w.T)
    for c in range(4):
        upl.append(_pack_k(np.ascontiguousarray(supT[:, 512 * c:512 * (c + 1)])))
        dnl.append(_pack_k(np.ascontiguousarray(sdnT[512 * c:512 * (c + 1), :])))
    d["upw"] = np.ascontiguousarray(np.stack(upl))
    d["dnw"] = np.ascontiguousarray(np.stack(dnl))
    inv = 1.0 / (10000.0 ** (np.arange(0, DH, 2, dtype=np.float64) / DH))
    t = np.arange(S, dtype=np.float64)
    fr = np.concatenate([np.outer(t, inv), np.outer(t, inv)], axis=1)
    cosT = np.cos(fr).T.astype(np.float32)
    sinT = np.sin(fr).T.astype(np.float32)
    d["cost"] = np.ascontiguousarray(np.vstack([cosT, cosT]).astype(ml_dtypes.bfloat16))
    d["sint"] = np.ascontiguousarray(np.vstack([sinT, sinT]).astype(ml_dtypes.bfloat16))
    M64 = np.zeros((64, 64), np.float32)
    for i in range(32):
        M64[i, i + 32] = -1.0
        M64[i + 32, i] = 1.0
    MT = np.zeros((128, 128), np.float32)
    MT[0:64, 0:64] = M64.T
    MT[64:128, 64:128] = M64.T
    d["mt"] = MT
    d["idn"] = np.eye(128, dtype=np.float32)
    esh = np.zeros((64, 128), np.float32)
    for i in range(64):
        esh[i, 64 + i] = 1.0
    d["esh"] = esh
    d["msk"] = np.ascontiguousarray(
        (np.arange(128)[:, None] <= np.arange(128)[None, :]).astype(np.float32))
    return d


def kernel(**inputs):
    x = np.asarray(inputs["L0_attn_x"], np.float32)
    shared = _host_prep(inputs)
    nc, _ = build_nc()
    in_maps = []
    for c in range(B):
        m = dict(shared)
        m["x"] = np.ascontiguousarray(x[c])
        in_maps.append(m)
    res = run_bass_kernel_spmd(nc, in_maps, core_ids=list(range(B)))
    out = np.stack([np.asarray(r["out"], np.float32) for r in res.results])
    return out


# version 30
# speedup vs baseline: 1.1074x; 1.0652x over previous
import contextlib
import numpy as np
import ml_dtypes
import concourse.bass as bass
import concourse.mybir as mybir
from concourse.bass_utils import run_bass_kernel_spmd

F32 = mybir.dt.float32
BF16 = mybir.dt.bfloat16
AF = mybir.ActivationFunctionType
ALU = mybir.AluOpType
AX = mybir.AxisListType

B, S, D = 8, 4096, 1024
H, DH = 16, 64
NP = 8
NE = 36
NT = S // 128
NG = 2
GS = S // NG
GT = GS // 128
GC = GS // 512
NEG = -1e30

ENGS = ("sync", "tensor", "scalar", "vector", "gpsimd")

DEV = {"pairs": NP, "experts": NE, "groups": NG, "attn": True, "ffn": True,
       "out": True, "taps": [], "astage": 4}


class Plan:
    def __init__(self):
        self.ops = {e: [] for e in ENGS}
        self.cnt = {s: 0 for s in ("LI", "LX0", "LX1", "CS0", "CS1", "LC", "LD", "LB", "LW", "LA", "PU0", "PU1", "T", "SS", "V", "O", "PS0", "PS1", "PS2", "PS3", "PS4", "PS5", "LB1", "LB2", "RB0", "RB1", "LW1", "LD1", "PJ0", "PJ1", "O1")}
        self.lastwait = {e: {} for e in ENGS}

    def do(self, eng, fn, inc=None, amt=1):
        if inc is not None:
            self.cnt[inc] += amt
            self.ops[eng].append((fn, inc, amt))
            return self.cnt[inc]
        self.ops[eng].append((fn, None, 0))
        return None

    def dma(self, eng, sem, dst, src):
        return self.do(eng, (lambda E, d=dst, s=src: E.dma_start(out=d, in_=s)),
                       inc=sem, amt=16)

    def wait(self, eng, sem, val=None):
        v = self.cnt[sem] if val is None else val
        if v <= 0:
            return
        if self.lastwait[eng].get(sem, -1) >= v:
            return
        self.lastwait[eng][sem] = v
        self.ops[eng].append((("wait", sem, v), None, 0))

    def drain(self, eng):
        self.ops[eng].append((("drain",), None, 0))

    def run(self, E, eng, handles):
        for fn, inc, amt in self.ops[eng]:
            if isinstance(fn, tuple):
                if fn[0] == "wait":
                    E.wait_ge(handles[fn[1]], fn[2])
                else:
                    E.drain()
                continue
            r = fn(E)
            if inc is not None:
                r.then_inc(handles[inc], amt)

    def n_ops(self):
        return {e: len(v) for e, v in self.ops.items()}


def _pack_k(w):
    K, N = w.shape
    assert K % 128 == 0
    return np.ascontiguousarray(
        w.reshape(K // 128, 128, N).transpose(1, 0, 2).reshape(128, -1))


def build_nc():
    nc = bass.Bass("TRN2", target_bir_lowering=False, debug=False)
    ap = {}

    def din(name, shape, dt=F32):
        ap[name] = nc.dram_tensor(name, list(shape), dt, kind="ExternalInput").ap()

    din("x", (S, D))
    din("wq_p", (128, 8192)); din("wk_p", (128, 8192)); din("wv_p", (128, 8192))
    din("pw_p", (128, 8192))
    din("bqk", (128, 16)); din("bvp", (1, 1024)); din("pjb", (1, 1024))
    din("gw_p", (128, 256)); din("ebt", (128, 32))
    din("upw", (NE, 128, 4096)); din("dnw", (NE, 128, 4096))
    din("cost", (128, S), BF16); din("sint", (128, S), BF16)
    din("mt", (128, 128)); din("idn", (128, 128)); din("msk", (128, 128))
    din("esh", (64, 128))
    out_hbm = nc.dram_tensor("out", [S, D], F32, kind="ExternalOutput").ap()
    upw_bf = nc.dram_tensor("upw_bf", [NE, 128, 4096], BF16, kind="Internal").ap()
    dnw_bf = nc.dram_tensor("dnw_bf", [NE, 128, 4096], BF16, kind="Internal").ap()
    combT_d = nc.dram_tensor("combT_d", [36, S], BF16, kind="Internal").ap()

    es = contextlib.ExitStack()
    tens = {}

    def sb(name, shape, dt):
        t = es.enter_context(nc.sbuf_tensor(name, list(shape), dt))
        tens[name] = t
        return t

    def psm(name, shape, dt):
        return es.enter_context(nc.psum_tensor(name, list(shape), dt))

    xnT = [sb(f"xnT{c}", (128, S), BF16) for c in range(8)]
    acc = sb("acc", (128, GT * 1024), BF16)
    wsl = [sb(f"wsl{i}", (128, 4096), BF16) for i in range(4)]
    cstg = [sb(f"cstg{i}", (36, 128), BF16) for i in range(2)]
    stg = [sb(f"stg{i}", (128, 1024), F32) for i in range(2)]
    attw = sb("attw", (128, 2048), BF16)
    vbuf = sb("vbuf", (128, 4160), BF16)
    qrot = sb("qrot", (128, GS), BF16)
    krot = sb("krot", (128, S), BF16)
    qpre = [sb(f"qpre{i}", (128, 512), BF16) for i in range(2)]
    probs = [sb(f"probs{i}", (128, 1024), BF16) for i in range(4)]
    vt1 = sb("vt1", (128, 512), F32)
    vt2 = sb("vt2", (128, 512), F32)
    sqn = qpre[0][:]
    cbuf = sb("cbuf", (1, 64), F32)
    ncb = sb("ncb", (1, 4), BF16)
    negC = sb("negC", (128, 8), F32)
    stAB = sb("stAB", (1, 1024), BF16)
    stABw = stg[1][0:1, 0:1024]
    bcsb = sb("bcsb", (128, 1024), BF16)
    ctxab = sb("ctxab", (128, 512), BF16)
    tmpb = qpre[1][0:64, :]
    rb = sb("rb", (128, 2048), BF16)
    outs = rb[:].bitcast(F32)
    hb = sb("hb", (128, 4096), BF16)
    crowb = sb("crowb", (1, 2048), BF16)
    bvpb = crowb[0:1, 0:1024]
    pjbb = crowb[0:1, 1024:2048]
    comb4 = [sb(f"comb{i}", (128, 36), F32) for i in range(4)]
    rtr4 = sb("rtr4", (128, 1024), F32)
    accf = acc[:].bitcast(F32)
    xnf = [accf[:, 0:1024], accf[:, 1024:2048]]
    csc = [sb(f"csc{i}", (128, 512), BF16) for i in range(2)]
    css = [sb(f"css{i}", (128, 512), BF16) for i in range(2)]
    mskb = sb("mskb", (128, 128), BF16)
    mtb = sb("mtb", (128, 128), BF16)
    eshb = sb("eshb", (64, 128), BF16)
    idsb = sb("idsb", (128, 128), F32)
    onesb = sb("onesb", (128, 128), BF16)
    gwb = sb("gwb", (128, 256), BF16)
    ebt = sb("ebt_s", (128, 32), F32)
    bqk = sb("bqk_s", (128, 16), F32)
    bqks = sb("bqks", (128, 16), F32)
    bnst = sb("bnst", (128, 12), F32)
    mv = sb("mv", (128, 2), F32)
    sc1 = sb("sc1", (128, 1), F32)
    rstd = sb("rstd", (128, 1), F32)
    nmr = sb("nmr", (128, 1), F32)

    P0 = psm("P0", (128, 1024), F32)
    P1 = psm("P1", (128, 1024), F32)
    P2 = psm("P2", (128, 1024), F32)
    P3 = psm("P3", (128, 1024), F32)

    handles = {}
    for sname in ("LI", "LX0", "LX1", "CS0", "CS1", "LC", "LD", "LB", "LW", "LA", "PU0", "PU1", "T", "SS", "V", "O", "PS0", "PS1", "PS2", "PS3", "PS4", "PS5", "LB1", "LB2", "RB0", "RB1", "LW1", "LD1", "PJ0", "PJ1", "O1"):
        handles[sname] = es.enter_context(nc.semaphore(sname))

    P = Plan()

    P.dma("sync", "LI", idsb[:], ap["idn"])
    P.dma("sync", "LI", stg[0][:, 0:128], ap["msk"])
    P.dma("sync", "LI", stg[0][:, 128:256], ap["mt"])
    P.dma("sync", "LI", stg[0][:, 256:272], ap["bqk"])
    P.dma("sync", "LI", ebt[:], ap["ebt"])
    P.dma("sync", "LI", stg[0][:, 272:528], ap["gw_p"])
    P.dma("sync", "LI", stg[0][0:64, 528:656], ap["esh"])
    P.wait("vector", "LI")
    P.do("vector", lambda E: E.tensor_copy(out=mskb[:], in_=stg[0][:, 0:128]))
    P.do("vector", lambda E: E.tensor_copy(out=mtb[:], in_=stg[0][:, 128:256]))
    P.do("vector", lambda E: E.tensor_copy(out=bqk[:], in_=stg[0][:, 256:272]))
    P.do("vector", lambda E: E.tensor_copy(out=gwb[:], in_=stg[0][:, 272:528]))
    P.do("vector", lambda E: E.tensor_copy(out=eshb[:], in_=stg[0][0:64, 528:656]))
    P.do("vector", lambda E: E.memset(onesb[:], 1.0))
    P.do("vector", lambda E: E.memset(vbuf[:], 1.0))
    P.do("vector", lambda E: E.memset(rb[:], 0.0))
    P.drain("vector")
    P.do("vector", lambda E: E.tensor_copy(out=bqks[:], in_=bqk[:]))
    P.drain("vector")
    bqv = bqks[:].rearrange("p (a b) -> p a b", b=2)
    P.do("vector", lambda E: E.tensor_scalar_mul(
        out=bqv[:, :, 0:1], in0=bqv[:, :, 0:1], scalar1=0.125), inc="V")

    pump_state = {"k": 0, "cast": {}, "out": {}}
    hbf = hb[:].bitcast(F32)

    PUMP_TOT = NE * 2 * 4
    PUMP_SLOTS = 6
    obuf = [rb[:, 0:2048], acc[:, 12288:14336], acc[:, 14336:16384]]

    def _pump_slot(j):
        s = j % PUMP_SLOTS
        if s < 2:
            return hbf[:, 1024 * s:1024 * (s + 1)]
        return accf[:, 2048 + 1024 * (s - 2):2048 + 1024 * (s - 1)]

    def _pump_issue_in(j):
        if j >= PUMP_TOT:
            return
        mat = j // 4
        jc = j % 4
        e = mat // 2
        srcw = ap["upw"] if mat % 2 == 0 else ap["dnw"]
        if j >= PUMP_SLOTS:
            P.wait("scalar", "V", pump_state["cast"][j - PUMP_SLOTS])
        P.dma("scalar", "PS%d" % (j % PUMP_SLOTS), _pump_slot(j),
              srcw[e, :, 1024 * jc:1024 * (jc + 1)])

    def emit_pump(nchunks):
        for _ in range(nchunks):
            k = pump_state["k"]
            if k >= PUMP_TOT:
                return
            pump_state["k"] = k + 1
            mat = k // 4
            e = mat // 2
            dstw = upw_bf if mat % 2 == 0 else dnw_bf
            h = k // 2
            ob = obuf[h % 3]
            if k == 0:
                P.wait("scalar", "V", 1)
                for j in range(PUMP_SLOTS - 1):
                    _pump_issue_in(j)
            _pump_issue_in(k + PUMP_SLOTS - 1)
            P.wait("vector", "PS%d" % (k % PUMP_SLOTS))
            if h >= 3:
                P.wait("vector", *pump_state["out"][h - 3])
            pump_state["cast"][k] = P.do(
                "vector", lambda E, k=k, ob=ob: E.tensor_copy(
                    out=ob[:, 1024 * (k % 2):1024 * (k % 2) + 1024], in_=_pump_slot(k)),
                inc="V")
            if k % 2 == 1:
                P.wait("gpsimd", "V", pump_state["cast"][k])
                osem = ("LB", "LB1", "LB2")[h % 3]
                pump_state["out"][h] = (osem, P.dma(
                    "gpsimd", osem, dstw[e, :, 2048 * (h % 2):2048 * (h % 2) + 2048],
                    ob[:]))

    ss_xn = [0] * NT
    v_tr = [0] * NT
    for i in range(NT):
        b = i % 2
        if i < 2:
            P.wait("sync", "V")
        else:
            P.wait("sync", "SS", ss_xn[i - 2])
        P.dma("sync", "LX0" if b == 0 else "LX1", stg[b][:], ap["x"][128 * i:128 * (i + 1), :])
        P.wait("vector", "LX0" if b == 0 else "LX1")
        P.do("vector", lambda E, b=b: E.bn_stats(out=bnst[:, 0:6], in_=stg[b][:, 0:512]))
        P.do("vector", lambda E, b=b: E.bn_stats(out=bnst[:, 6:12], in_=stg[b][:, 512:1024]))
        P.drain("vector")
        P.do("vector", lambda E: E.bn_aggr(out=mv[:], in_=bnst[:]))
        P.drain("vector")
        P.do("vector", lambda E: E.tensor_scalar_add(out=sc1[:], in0=mv[:, 1:2], scalar1=1e-5))
        P.drain("vector")
        P.do("vector", lambda E: E.reciprocal(out=sc1[:], in_=sc1[:]), inc="V")
        P.wait("scalar", "V")
        P.do("scalar", lambda E: E.activation(out=rstd[:], in_=sc1[:], func=AF.Sqrt), inc="SS")
        P.wait("vector", "SS")
        P.do("vector", lambda E: E.tensor_scalar(out=nmr[:], in0=mv[:, 0:1],
                                                 scalar1=rstd[:], scalar2=-1.0,
                                                 op0=ALU.mult, op1=ALU.mult), inc="V")
        P.wait("scalar", "V")
        if i >= 2:
            P.wait("scalar", "T")
        xb = xnf[b]
        ss_xn[i] = P.do("scalar", lambda E, b=b, xb=xb: E.activation(
            out=xb[:], in_=stg[b][:], func=AF.Identity, bias=nmr[:], scale=rstd[:]),
            inc="SS")
        P.wait("tensor", "SS")
        for c in range(8):
            half = c % 2
            reg = P3[:, 512 * half:512 * half + 128]
            if c >= 2:
                P.wait("tensor", "V", v_tr[i])
            P.do("tensor", lambda E, xb=xb, c=c, reg=reg: E.transpose(
                out=reg, in_=xb[:, 128 * c:128 * (c + 1)], identity=idsb[:]), inc="T")
            P.wait("vector", "T")
            v_tr[i] = P.do("vector", lambda E, c=c, i=i, reg=reg: E.tensor_copy(
                out=xnT[c][:, 128 * i:128 * (i + 1)], in_=reg), inc="V")
        if DEV["ffn"]:
            emit_pump(9)

    comb_lb = {}
    _steps = [
        (lambda E, r: E.tensor_add(out=r[:, 32:64], in0=r[:, 0:32], in1=ebt[:]),),
        tuple(
            (lambda E, r, a=a, bb=bb, j=j: E.tensor_add(
                out=r[:, 64 + 8 * j:72 + 8 * j],
                in0=r[:, 32:64].rearrange("p (g e) -> p g e", e=4)[:, :, a:a + 1].squeeze(2),
                in1=r[:, 32:64].rearrange("p (g e) -> p g e", e=4)[:, :, bb:bb + 1].squeeze(2)))
            for j, (a, bb) in enumerate([(0, 1), (0, 2), (0, 3), (1, 2), (1, 3), (2, 3)])),
        (lambda E, r: E.tensor_max(out=r[:, 112:120], in0=r[:, 64:72], in1=r[:, 72:80]),
         lambda E, r: E.tensor_max(out=r[:, 120:128], in0=r[:, 80:88], in1=r[:, 88:96]),
         lambda E, r: E.tensor_max(out=r[:, 128:136], in0=r[:, 96:104], in1=r[:, 104:112])),
        (lambda E, r: E.tensor_max(out=r[:, 112:120], in0=r[:, 112:120], in1=r[:, 120:128]),),
        (lambda E, r: E.tensor_max(out=r[:, 112:120], in0=r[:, 112:120], in1=r[:, 128:136]),),
        (lambda E, r: E.max(out=r[:, 120:128], in_=r[:, 112:120]),),
        (lambda E, r: E.memset(r[:, 124:128], NEG),),
        (lambda E, r: E.match_replace(out=r[:, 128:136], in_to_replace=r[:, 120:128],
                                      in_values=r[:, 112:120], imm_value=NEG),),
        (lambda E, r: E.tensor_tensor(out=r[:, 120:128], in0=r[:, 128:136],
                                      in1=r[:, 112:120], op=ALU.is_lt),),
        (lambda E, r: E.tensor_tensor(
            out=r[:, 136:168].rearrange("p (g e) -> p g e", e=4),
            in0=r[:, 32:64].rearrange("p (g e) -> p g e", e=4),
            in1=r[:, 120:128].unsqueeze(2).to_broadcast([128, 8, 4]),
            op=ALU.mult),),
        (lambda E, r: E.max(out=r[:, 168:176], in_=r[:, 136:168]),),
        (lambda E, r: E.memset(r[:, 174:176], NEG),),
        (lambda E, r: E.match_replace(out=r[:, 176:208], in_to_replace=r[:, 168:176],
                                      in_values=r[:, 136:168], imm_value=NEG),),
        (lambda E, r: E.tensor_tensor(out=r[:, 176:208], in0=r[:, 176:208],
                                      in1=r[:, 136:168], op=ALU.is_lt),),
        (lambda E, r: E.tensor_mul(out=r[:, 208:240], in0=r[:, 176:208], in1=r[:, 0:32]),),
        (lambda E, r: E.reduce_sum(out=r[:, 240:241], in_=r[:, 208:240], axis=AX.X),),
        (lambda E, r: E.tensor_scalar_add(out=r[:, 240:241], in0=r[:, 240:241],
                                          scalar1=1e-20),),
        (lambda E, r: E.reciprocal(out=r[:, 241:242], in_=r[:, 240:241]),),
    ]
    for q in range(NT // 4):
        tiles = [4 * q + t for t in range(4)]
        P.wait("tensor", "V")
        P.wait("tensor", "SS")
        for t, i in enumerate(tiles):
            for c in range(8):
                P.do("tensor", lambda E, c=c, i=i, t=t: E.matmul(
                    out=P3[:, 512 + 32 * t:544 + 32 * t],
                    lhsT=xnT[c][:, 128 * i:128 * (i + 1)],
                    rhs=gwb[:, 32 * c:32 * (c + 1)],
                    start=(c == 0), stop=(c == 7)), inc=("T" if c == 7 else None))
        P.wait("scalar", "T")
        for t in range(4):
            P.do("scalar", lambda E, t=t: E.activation(
                out=rtr4[:, 256 * t:256 * t + 32], in_=P3[:, 512 + 32 * t:544 + 32 * t],
                func=AF.Sigmoid), inc="SS")
        P.wait("vector", "SS")
        for step in _steps:
            for t in range(4):
                for fn in step:
                    P.do("vector", lambda E, fn=fn, t=t: fn(
                        E, rtr4[:, 256 * t:256 * (t + 1)]))
            P.drain("vector")
        P.wait("vector", "T")
        for t in range(4):
            r4 = rtr4[:, 256 * t:256 * (t + 1)]
            P.do("vector", lambda E, cb=comb4[t], r4=r4: E.tensor_scalar(
                out=cb[:, 0:32], in0=r4[:, 208:240],
                scalar1=r4[:, 241:242], scalar2=2.5,
                op0=ALU.mult, op1=ALU.mult))
            P.do("vector", lambda E, cb=comb4[t]: E.memset(cb[:, 32:36], 1.0))
        P.drain("vector")
        P.do("vector", lambda E: E.nop(), inc="V")
        for t, i in enumerate(tiles):
            P.wait("tensor", "V")
            P.do("tensor", lambda E, cb=comb4[t]: E.transpose(
                out=P3[0:36, 384:512], in_=cb[:], identity=idsb[:]), inc="T")
            P.wait("vector", "T")
            if i >= 2:
                P.wait("vector", "RB%d" % (i % 2), comb_lb[i - 2])
            P.do("vector", lambda E, i=i: E.tensor_copy(
                out=cstg[i % 2][:], in_=P3[0:36, 384:512]), inc="V")
            P.wait("gpsimd", "V")
            comb_lb[i] = P.dma("gpsimd", "RB%d" % (i % 2),
                               combT_d[0:36, 128 * i:128 * (i + 1)], cstg[i % 2][:])

    v_out = (0, 0)
    prev_last_exp = [0]
    t_attw_free = [0]
    v_stg_free = [0]
    att_end_prev = [0]
    prev_up_end = [None]
    krotf = krot[:].bitcast(F32)
    for g in range(DEV["groups"]):
        KC = 4 * (g + 1)
        KT = 16 * (g + 1)

        P.wait("scalar", "T", att_end_prev[0])
        v_pw = {}
        for hchunk in range(8):
            sl = krotf[:, 1024 * (hchunk % 2):1024 * (hchunk % 2) + 1024]
            if hchunk >= 2:
                P.wait("scalar", "V", v_pw[hchunk - 2])
            P.dma("scalar", "PJ%d" % (hchunk % 2), sl,
                  ap["pw_p"][:, 1024 * hchunk:1024 * (hchunk + 1)])
            P.wait("vector", "PJ%d" % (hchunk % 2))
            if prev_up_end[0] is not None:
                P.wait("vector", "T", prev_up_end[0][hchunk // 4])
            dstsl = wsl[hchunk // 4]
            off = 1024 * (hchunk % 4)
            v_pw[hchunk] = P.do("vector", lambda E, s=sl, d=dstsl, off=off: E.tensor_copy(
                out=d[:, off:off + 1024], in_=s[:]), inc="V")
        if g == 1:
            P.wait("vector", "O", v_out[0])
            P.wait("vector", "O1", v_out[1])
        P.wait("sync", "T")
        P.wait("sync", "V")
        P.wait("sync", "SS")
        P.dma("sync", "LX1", stg[1][0:1, 0:1024], ap["bvp"])
        P.wait("vector", "LX1")
        P.wait("vector", "LC")
        P.wait("vector", "T")
        P.do("vector", lambda E: E.tensor_copy(out=bvpb, in_=stg[1][0:1, 0:1024]), inc="V")
        P.wait("sync", "V")
        P.dma("sync", "LX1", stg[1][0:1, 0:1024], ap["pjb"])
        P.wait("vector", "LX1")
        P.do("vector", lambda E: E.tensor_copy(out=pjbb, in_=stg[1][0:1, 0:1024]), inc="V")
        v_stg_free[0] = P.cnt["V"]

        for p in range(DEV["pairs"] if DEV["attn"] else 0):
            P.wait("scalar", "T", t_attw_free[0])
            P.wait("scalar", "V", v_stg_free[0])
            for dc in range(8):
                P.dma("scalar", "LA", stg[0][:, 128 * dc:128 * (dc + 1)],
                      ap["wq_p"][:, 1024 * dc + 128 * p:1024 * dc + 128 * (p + 1)])
            P.wait("vector", "LA")
            P.wait("vector", "T", t_attw_free[0])
            v_attw = P.do("vector", lambda E: E.tensor_copy(out=attw[:, 0:1024], in_=stg[0][:]),
                          inc="V")
            P.wait("scalar", "V", v_attw)
            for dc in range(8):
                P.dma("scalar", "LA", stg[0][:, 128 * dc:128 * (dc + 1)],
                      ap["wk_p"][:, 1024 * dc + 128 * p:1024 * dc + 128 * (p + 1)])
            P.wait("vector", "LA")
            v_attw_k = P.do("vector", lambda E: E.tensor_copy(out=attw[:, 1024:2048], in_=stg[0][:]),
                            inc="V")

            ss_pre = {}
            ropev = []
            reg_of = {}
            ckc = [0]
            t_q_end = 0
            for kind in (("q",) if DEV.get("skipk") else ("q", "k")):
                if kind == "k":
                    t_q_end = P.cnt["T"]
                    P.wait("scalar", "V", v_attw_k)
                    for dc in range(8):
                        P.dma("scalar", "LA", stg[0][:, 128 * dc:128 * (dc + 1)],
                              ap["wv_p"][:, 1024 * dc + 128 * p:1024 * dc + 128 * (p + 1)])
                    P.wait("vector", "LA")
                    P.wait("vector", "T", t_q_end)
                    v_attw = P.do("vector", lambda E: E.tensor_copy(out=attw[:, 0:1024],
                                                                    in_=stg[0][:]), inc="V")
                half = 0 if kind == "q" else 1024
                nch = GC if kind == "q" else KC

                def emit_qk_mms(ci, kind=kind, p=p, half=half):
                    c = (GC * g + ci) if kind == "q" else ci
                    b = ci % 2
                    rot = ckc[0] % 3
                    ckc[0] += 1
                    reg_of[(kind, ci)] = rot
                    P.wait("sync", "V")
                    P.dma("sync", "CS0" if b == 0 else "CS1", csc[b][:], ap["cost"][:, 512 * c:512 * (c + 1)])
                    P.dma("sync", "CS0" if b == 0 else "CS1", css[b][:], ap["sint"][:, 512 * c:512 * (c + 1)])
                    qkreg = (P3, P0, P1)[rot][:, 0:512]
                    wv_need = v_attw if kind == "q" else v_attw_k
                    if len(ropev) >= 3:
                        P.wait("tensor", "V", max(wv_need, ropev[-3]))
                    else:
                        P.wait("tensor", "V", wv_need)
                    if (kind, ci - 3) in ss_pre:
                        P.wait("tensor", "SS", ss_pre[(kind, ci - 3)])
                    elif kind == "k" and ("q", GC - 3 + ci) in ss_pre:
                        P.wait("tensor", "SS", ss_pre[("q", GC - 3 + ci)])
                    for dc in range(8):
                        P.do("tensor", lambda E, dc=dc, c=c, qkreg=qkreg, half=half: E.matmul(
                            out=qkreg, lhsT=attw[:, half + 128 * dc:half + 128 * (dc + 1)],
                            rhs=xnT[dc][:, 512 * c:512 * (c + 1)],
                            start=(dc == 0), stop=(dc == 7)),
                            inc=("T" if dc == 7 else None))
                    return P.cnt["T"]

                def emit_qk_tail(ci, kind=kind, p=p):
                    b = ci % 2
                    rot = reg_of[(kind, ci)]
                    qkreg = (P3, P0, P1)[rot][:, 0:512]
                    rreg = (P3, P0, P1)[rot][:, 512:1024]
                    P.wait("scalar", "T", qk_t[(kind, ci)])
                    if len(ropev) >= 2:
                        P.wait("scalar", "V", ropev[-2])
                    bias = bqks[:, 2 * p:2 * p + 1] if kind == "q" else bqk[:, 2 * p + 1:2 * p + 2]
                    scale = 0.125 if kind == "q" else 1.0
                    ss_pre[(kind, ci)] = P.do(
                        "scalar", lambda E, b=b, bias=bias, scale=scale, qkreg=qkreg:
                        E.activation(out=qpre[b][:], in_=qkreg, func=AF.Identity,
                                     bias=bias, scale=scale), inc="SS")
                    P.wait("tensor", "SS", ss_pre[(kind, ci)])
                    if len(ropev) >= 2:
                        P.wait("tensor", "V", ropev[-2])
                    P.do("tensor", lambda E, b=b, rreg=rreg: E.matmul(
                        out=rreg, lhsT=mtb[:], rhs=qpre[b][:], start=True, stop=True),
                        inc="T")
                    dst = qrot if kind == "q" else krot
                    dof = 512 * ci if kind == "q" else 512 * ci
                    P.wait("vector", "T")
                    P.wait("vector", "CS0" if b == 0 else "CS1")
                    P.drain("vector")
                    P.do("vector", lambda E, b=b: E.tensor_mul(
                        out=vt1[:], in0=qpre[b][:], in1=csc[b][:]))
                    P.do("vector", lambda E, b=b, rreg=rreg: E.tensor_tensor(
                        out=vt2[:], in0=rreg, in1=css[b][:], op=ALU.mult))
                    P.drain("vector")
                    ropev.append(P.do("vector", lambda E, dst=dst, dof=dof: E.tensor_add(
                        out=dst[:, dof:dof + 512], in0=vt1[:], in1=vt2[:]), inc="V"))

                qk_t = getattr(P, "_qk_t", None)
                if qk_t is None:
                    qk_t = {}
                    P._qk_t = qk_t
                for ci in range(nch + 1):
                    if ci < nch:
                        qk_t[(kind, ci)] = emit_qk_mms(ci)
                    if ci - 1 >= 0:
                        emit_qk_tail(ci - 1)
            if DEV["astage"] < 2:
                continue
            P.wait("tensor", "V")
            v_ev = [0] * KT
            for tt in range(KT):
                if g == 0 and DEV["ffn"]:
                    emit_pump(6)
                vreg = (P2[:, 0:128], P2[:, 640:768],
                        P0[:, 0:128], P0[:, 640:768])[tt % 4]
                if tt >= 4:
                    P.wait("tensor", "V", v_ev[tt - 4])
                for dc in range(8):
                    P.do("tensor", lambda E, dc=dc, tt=tt, vreg=vreg: E.matmul(
                        out=vreg, lhsT=xnT[dc][:, 128 * tt:128 * (tt + 1)],
                        rhs=attw[:, 128 * dc:128 * (dc + 1)],
                        start=(dc == 0), stop=False))
                P.do("tensor", lambda E, p=p, vreg=vreg: E.matmul(
                    out=vreg, lhsT=onesb[0:1, 0:128],
                    rhs=bvpb[:, 128 * p:128 * (p + 1)],
                    start=False, stop=True), inc="T")
                P.wait("vector", "T")
                P.do("vector", lambda E, tt=tt, vreg=vreg: E.tensor_copy(
                    out=vbuf[:, 130 * tt:130 * tt + 64], in_=vreg[:, 0:64]))
                v_ev[tt] = P.do("vector", lambda E, tt=tt, vreg=vreg: E.tensor_copy(
                    out=vbuf[:, 130 * tt + 65:130 * tt + 129], in_=vreg[:, 64:128]), inc="V")
            t_attw_free[0] = P.cnt["T"]
            v_stg_free[0] = v_attw

            if DEV["astage"] < 3:
                continue
            if True:
              for kind, nch, buf in (("q", GC, qrot), ("k", KC, krot)):
                nmm_t = {}
                sq_v = {}
                red_v = {}
                for ci in range(nch + 2):
                    if ci < nch:
                        sqb = qpre[ci % 2]
                        P.wait("vector", "T", nmm_t.get(ci - 2, 0))
                        sq_v[ci] = P.do("vector", lambda E, buf=buf, ci=ci, sqb=sqb: E.tensor_mul(
                            out=sqb[:], in0=buf[:, 512 * ci:512 * (ci + 1)],
                            in1=buf[:, 512 * ci:512 * (ci + 1)]), inc="V")
                    if 0 <= ci - 1 < nch:
                        c2 = ci - 1
                        sqb = qpre[c2 % 2]
                        par = 512 * (c2 % 2)
                        P.wait("tensor", "V", sq_v[c2])
                        P.wait("tensor", "V", red_v.get(c2 - 2, 0))
                        P.do("tensor", lambda E, sqb=sqb, par=par: E.matmul(
                            out=P2[0:1, par:par + 512], lhsT=onesb[0:64, 0:1], rhs=sqb[0:64, :],
                            start=True, stop=True, tile_position=(0, 0)))
                        nmm_t[c2] = P.do("tensor", lambda E, sqb=sqb, par=par: E.matmul(
                            out=P1[0:1, par:par + 512], lhsT=onesb[64:128, 0:1], rhs=sqb[64:128, :],
                            start=True, stop=True, tile_position=(64, 0)), inc="T")
                    if 0 <= ci - 2 < nch:
                        c2 = ci - 2
                        par = 512 * (c2 % 2)
                        P.wait("vector", "T", nmm_t[c2])
                        ofA = (0 if kind == "q" else 16) + c2
                        ofB = (8 if kind == "q" else 32) + c2
                        P.do("vector", lambda E, ofA=ofA, par=par: E.reduce_max(
                            out=cbuf[0:1, ofA:ofA + 1], in_=P2[0:1, par:par + 512], axis=AX.X))
                        red_v[c2] = P.do("vector", lambda E, ofB=ofB, par=par: E.reduce_max(
                            out=cbuf[0:1, ofB:ofB + 1], in_=P1[0:1, par:par + 512], axis=AX.X), inc="V")
            P.drain("vector")
            P.do("vector", lambda E: E.reduce_max(out=cbuf[0:1, 40:41], in_=cbuf[0:1, 0:GC], axis=AX.X))
            P.do("vector", lambda E: E.reduce_max(out=cbuf[0:1, 41:42], in_=cbuf[0:1, 8:8 + GC], axis=AX.X))
            P.do("vector", lambda E, KC=KC: E.reduce_max(out=cbuf[0:1, 42:43], in_=cbuf[0:1, 16:16 + KC], axis=AX.X))
            P.do("vector", lambda E, KC=KC: E.reduce_max(out=cbuf[0:1, 43:44], in_=cbuf[0:1, 32:32 + KC], axis=AX.X))
            P.drain("vector")
            P.do("vector", lambda E: E.tensor_max(out=cbuf[0:1, 40:41], in0=cbuf[0:1, 40:41], in1=cbuf[0:1, 41:42]))
            P.do("vector", lambda E: E.tensor_max(out=cbuf[0:1, 42:43], in0=cbuf[0:1, 42:43], in1=cbuf[0:1, 43:44]))
            P.drain("vector")
            P.do("vector", lambda E: E.tensor_add(out=cbuf[0:1, 44:45], in0=cbuf[0:1, 40:41], in1=cbuf[0:1, 42:43]))
            P.drain("vector")
            P.wait("vector", "SS", prev_last_exp[0])
            P.do("vector", lambda E: E.tensor_scalar_mul(out=ncb[0:1, 0:1], in0=cbuf[0:1, 44:45], scalar1=-1.0 / 16.0), inc="V")
            P.wait("tensor", "V")
            P.do("tensor", lambda E: E.matmul(out=P3[:, 0:1], lhsT=onesb[0:1, 0:128],
                                              rhs=ncb[0:1, 0:1], start=True, stop=True), inc="T")
            P.wait("vector", "T")
            P.do("vector", lambda E: E.tensor_copy(out=negC[:, 0:1], in_=P3[:, 0:1]), inc="V")
            v_negc = P.cnt["V"]
            P.wait("scalar", "V")

            if DEV["astage"] < 4:
                continue
            exps_all = [0, 0, 0]
            ctx_all = [0, 0]
            v_probs = {}
            sc_t = {}
            ctx_of = {}
            gof = [0]
            prev_proj_v = [0]
            for sq in range(4):
                sjg = 4 * g + sq
                ntc = 4 * sjg + 4
                gb = gof[0]
                t_ctx_last = 0
                diag_of = {}

                def emit_sc(tc, sq=sq, gb=gb):
                    PS = (P0, P1, P3)[tc % 3]
                    if PS is P3:
                        P.wait("tensor", "V", v_negc)
                    P.wait("tensor", "SS", exps_all[-2])
                    P.do("tensor", lambda E, tc=tc, sq=sq, PS=PS: E.matmul(
                        out=PS[:, 0:512], lhsT=krot[0:64, 128 * tc:128 * (tc + 1)],
                        rhs=qrot[0:64, 512 * sq:512 * (sq + 1)],
                        start=True, stop=True, tile_position=(0, 0)))
                    sc_t[gb + tc] = P.do("tensor", lambda E, tc=tc, sq=sq, PS=PS: E.matmul(
                        out=PS[:, 512:1024], lhsT=krot[64:128, 128 * tc:128 * (tc + 1)],
                        rhs=qrot[64:128, 512 * sq:512 * (sq + 1)],
                        start=True, stop=True, tile_position=(64, 0)), inc="T")

                def emit_exp(tc, gb=gb):
                    PS = (P0, P1, P3)[tc % 3]
                    gt = gb + tc
                    pr = probs[gt % 4]
                    P.wait("scalar", "T", sc_t[gt])
                    P.wait("scalar", "T", ctx_of.get(gt - 4, 0))
                    if (gt - 4) in v_probs:
                        P.wait("scalar", "V", v_probs[gt - 4])
                    ev = P.do("scalar", lambda E, PS=PS, pr=pr: E.activation(
                        out=pr[:], in_=PS[:], func=AF.Exp, bias=negC[:, 0:1]), inc="SS")
                    exps_all.append(ev)
                    r = tc - 4 * sjg
                    if r >= 0:
                        diag_of[tc] = r
                        P.wait("vector", "SS", ev)
                        for half in (0, 512):
                            P.do("vector", lambda E, pr=pr, r=r, half=half: E.tensor_tensor(
                                out=pr[:, half + 128 * r:half + 128 * (r + 1)],
                                in0=pr[:, half + 128 * r:half + 128 * (r + 1)], in1=mskb[:],
                                op=ALU.mult))
                            if r > 0:
                                P.do("vector", lambda E, pr=pr, r=r, half=half: E.memset(
                                    pr[:, half:half + 128 * r], 0.0))
                        P.drain("vector")
                        P.do("vector", lambda E: E.nop(), inc="V")
                        v_probs[gt] = P.cnt["V"]

                exp_of = {}
                for i in range(ntc + 2):
                    if i < ntc:
                        emit_sc(i)
                    if 0 <= i - 1 < ntc:
                        emit_exp(i - 1)
                        exp_of[i - 1] = exps_all[-1]
                    if 0 <= i - 2 < ntc:
                        tcx = i - 2
                        gt = gb + tcx
                        pr = probs[gt % 4]
                        if tcx == 0:
                            P.wait("tensor", "V", prev_proj_v[0])
                        if tcx in diag_of:
                            P.wait("tensor", "V", v_probs[gt])
                        else:
                            P.wait("tensor", "SS", exp_of[tcx])
                        st = (tcx == 0)
                        sp = (tcx == ntc - 1)
                        P.do("tensor", lambda E, tc=tcx, pr=pr, st=st, sp=sp: E.matmul(
                            out=P2[0:65, 0:512], lhsT=vbuf[:, 130 * tc:130 * tc + 65],
                            rhs=pr[:, 0:512], start=st, stop=sp))
                        t_ctx_last = P.do("tensor", lambda E, tc=tcx, pr=pr, st=st, sp=sp: E.matmul(
                            out=P2[0:65, 512:1024], lhsT=vbuf[:, 130 * tc + 65:130 * tc + 130],
                            rhs=pr[:, 512:1024], start=st, stop=sp), inc="T")
                        ctx_all.append(t_ctx_last)
                        ctx_of[gt] = t_ctx_last
                P.wait("vector", "T", t_ctx_last)
                P.do("vector", lambda E: E.tensor_copy(out=tmpb, in_=P2[0:64, 512:1024]),
                     inc="V")
                P.wait("scalar", "T", t_ctx_last)
                P.do("scalar", lambda E: E.activation(out=stABw, in_=P2[64:65, 0:1024],
                                                      func=AF.Ln))
                P.drain("scalar")
                P.do("scalar", lambda E: E.activation(out=stAB[:, 0:1024], in_=stABw,
                                                      func=AF.Exp, scale=-1.0), inc="SS")
                P.wait("tensor", "V")
                P.wait("tensor", "SS")
                P.do("tensor", lambda E: E.matmul(out=P3[:, 0:512], lhsT=onesb[0:1, 0:128],
                                                  rhs=stAB[0:1, 0:512], start=True, stop=True))
                P.do("tensor", lambda E: E.matmul(out=P3[:, 512:1024], lhsT=onesb[0:1, 0:128],
                                                  rhs=stAB[0:1, 512:1024], start=True, stop=True))
                P.do("tensor", lambda E: E.matmul(out=P2[:, 512:1024], lhsT=eshb[:],
                                                  rhs=tmpb, start=True, stop=True), inc="T")
                P.wait("vector", "T")
                P.do("vector", lambda E: E.tensor_copy(out=bcsb[:], in_=P3[:, 0:1024]))
                P.drain("vector")
                P.do("vector", lambda E: E.tensor_mul(out=ctxab[0:64, :], in0=P2[0:64, 0:512],
                                                      in1=bcsb[0:64, 0:512]))
                v_ctx_scale_prev = P.do("vector", lambda E: E.tensor_mul(
                    out=ctxab[64:128, :], in0=P2[64:128, 512:1024],
                    in1=bcsb[64:128, 512:1024]), inc="V")
                P.wait("tensor", "V", v_ctx_scale_prev)
                for msub in range(4):
                    preg = P3 if msub % 2 == 0 else P2
                    jloc = 4 * sq + msub
                    if msub >= 2:
                        P.wait("tensor", "V")
                    lastm = None
                    for n in range(2):
                        last = (n == 1 and p > 0)
                        P.do("tensor", lambda E, p=p, n=n, msub=msub, preg=preg: E.matmul(
                            out=preg[:, 512 * n:512 * (n + 1)],
                            lhsT=ctxab[:, 128 * msub:128 * (msub + 1)],
                            rhs=wsl[p // 4][:, 1024 * (p % 4) + 512 * n:1024 * (p % 4) + 512 * (n + 1)],
                            start=True, stop=(p > 0)), inc=("T" if last else None))
                        if p == 0:
                            P.do("tensor", lambda E, n=n, preg=preg: E.matmul(
                                out=preg[:, 512 * n:512 * (n + 1)], lhsT=onesb[0:1, 0:128],
                                rhs=pjbb[:, 512 * n:512 * (n + 1)],
                                start=False, stop=True), inc=("T" if n == 1 else None))
                    P.wait("vector", "T")
                    if p == 0:
                        P.do("vector", lambda E, jloc=jloc, preg=preg: E.tensor_copy(
                            out=acc[:, 1024 * jloc:1024 * (jloc + 1)], in_=preg[:]), inc="V")
                    else:
                        P.do("vector", lambda E, jloc=jloc, preg=preg: E.tensor_tensor(
                            out=acc[:, 1024 * jloc:1024 * (jloc + 1)],
                            in0=acc[:, 1024 * jloc:1024 * (jloc + 1)], in1=preg[:],
                            op=ALU.add), inc="V")
                    prev_proj_v[0] = P.cnt["V"]
                gof[0] += ntc
            prev_last_exp[0] = exps_all[-1]
        def load_expert(e, which, tfree):
            sl = wsl[(e % 2) if which == "up" else 2 + (e % 2)]
            srcw = upw_bf if which == "up" else dnw_bf
            if which == "up":
                eng, sem = "sync", ("LW" if e % 2 == 0 else "LW1")
            else:
                eng, sem = "gpsimd", ("LD" if e % 2 == 0 else "LD1")
            P.wait(eng, "T", tfree)
            need_h = 4 * e + (1 if which == "up" else 3)
            P.wait(eng, *pump_state["out"][need_h])
            cnt = P.dma(eng, sem, sl[:], srcw[e])
            return [(sem, cnt)]

        t_att_end = P.cnt["T"]
        att_end_prev[0] = t_att_end
        if g == 0 and DEV["ffn"]:
            emit_pump(NE * 2 * 8)
        P.wait("scalar", "O")
        P.wait("scalar", "O1")
        P.wait("scalar", "LB")
        P.wait("scalar", "LB1")
        P.wait("scalar", "LB2")
        P.wait("scalar", "RB0")
        P.wait("scalar", "RB1")
        if not DEV["attn"] or DEV["pairs"] == 0:
            P.do("vector", lambda E: E.memset(acc[:], 0.0), inc="V")
        NE_RUN = DEV["experts"] if DEV["ffn"] else 0
        up_v = load_expert(0, "up", t_att_end) if NE_RUN else None
        dn_v = load_expert(0, "dn", t_att_end) if NE_RUN else None
        dn_waits = {0: dn_v}
        up_waits = {0: up_v}
        if NE_RUN:
            P.wait("scalar", "T")
            P.wait("scalar", "V")
            P.dma("scalar", "LC", crowb[:], combT_d[0:1, GS * g:GS * (g + 1)])

        t_up_end = [0] * NE
        t_dn_end = [0] * NE
        down_t_done = {}
        v_hb = {}
        v_dn = {}
        ss_sq = {}
        dslots = [P3[:, 0:512], P3[:, 512:1024], P2[:, 512:1024]]
        dslot_last = {}
        epr = None
        pend_down = []
        pend_scale = []
        v_bcc = {}
        for e in range(NE_RUN):
            def emit_down(e, tc):
                for sm, vv in (dn_waits.get(e) or []):
                    P.wait("tensor", sm, vv)
                P.wait("tensor", "V", v_hb[(e, tc, 3)])
                for tt in range(4):
                    jloc = 4 * tc + tt
                    for n in range(2):
                        slot = dslots[(2 * (4 * tc + tt) + n) % 3]
                        key_prev = dslot_last.get((2 * (4 * tc + tt) + n) % 3)
                        if key_prev is not None:
                            P.wait("tensor", "V", key_prev)
                        ho = 2048 * (tc % 2)
                        for i in range(4):
                            P.do("tensor", lambda E, i=i, n=n, tt=tt, e=e, slot=slot, ho=ho: E.matmul(
                                out=slot,
                                lhsT=hb[:, ho + 512 * i + 128 * tt:ho + 512 * i + 128 * (tt + 1)],
                                rhs=wsl[2 + e % 2][:, 1024 * i + 512 * n:1024 * i + 512 * (n + 1)],
                                start=(i == 0), stop=(i == 3)),
                                inc=("T" if i == 3 else None))
                        P.wait("vector", "T")
                        vv = P.do("vector", lambda E, jloc=jloc, n=n, slot=slot: E.tensor_tensor(
                            out=acc[:, 1024 * jloc + 512 * n:1024 * jloc + 512 * (n + 1)],
                            in0=acc[:, 1024 * jloc + 512 * n:1024 * jloc + 512 * (n + 1)],
                            in1=slot, op=ALU.add), inc="V")
                        dslot_last[(2 * (4 * tc + tt) + n) % 3] = vv
                        v_dn[(e, tc, tt, n)] = vv
                v_dn[(e, tc)] = v_dn[(e, tc, 3, 1)]
                down_t_done[e] = P.cnt["T"]

            for tc in range(GC):
                TC = GS * g + 512 * tc
                pend_down.append((e, tc))
                t_dn_guard = 0
                if len(pend_down) >= 3:
                    de, dtc = pend_down.pop(0)
                    emit_down(de, dtc)
                    t_dn_guard = P.cnt["T"]
                if pend_scale:
                    se, stc = pend_scale.pop(0)
                    sho = 2048 * (stc % 2)
                    P.wait("vector", "SS", ss_sq[(se, stc, 1)])
                    for i in range(4):
                        v_hb[(se, stc, i)] = P.do(
                            "vector", lambda E, i=i, sho=sho, stc=stc: E.tensor_tensor(
                                out=hb[:, sho + 512 * i:sho + 512 * (i + 1)],
                                in0=hb[:, sho + 512 * i:sho + 512 * (i + 1)],
                                in1=bcsb[:, 512 * (stc % 2):512 * (stc % 2) + 512],
                                op=ALU.mult), inc="V")
                t_up_i = {}
                for i in (2, 3, 0, 1):
                    reg = (P1 if i >= 2 else P0)[:, 512 * (i % 2):512 * (i % 2) + 512]
                    if (e, tc - 1, i) in ss_sq:
                        P.wait("tensor", "SS", ss_sq[(e, tc - 1, i)])
                    elif tc == 0 and (epr, GC - 1, i) in ss_sq:
                        P.wait("tensor", "SS", ss_sq[(epr, GC - 1, i)])
                    if tc == 0:
                        for sm, vv in (up_waits.get(e) or []):
                            P.wait("tensor", sm, vv)
                    for dc in range(8):
                        P.do("tensor", lambda E, dc=dc, i=i, TC=TC, reg=reg, e=e: E.matmul(
                            out=reg, lhsT=wsl[e % 2][:, 512 * dc + 128 * i:512 * dc + 128 * (i + 1)],
                            rhs=xnT[dc][:, TC:TC + 512],
                            start=(dc == 0), stop=(dc == 7)),
                            inc=("T" if dc == 7 else None))
                    t_up_i[i] = P.cnt["T"]
                t_up_this = P.cnt["T"]
                if tc == GC - 1:
                    t_up_end[e] = t_up_this
                if (e, tc - 1) in v_bcc:
                    P.wait("tensor", "V", v_bcc[(e, tc - 1)])
                elif tc == 0 and (epr, GC - 1) in v_bcc:
                    P.wait("tensor", "V", v_bcc[(epr, GC - 1)])
                P.wait("tensor", "LC")
                P.do("tensor", lambda E, tc=tc: E.matmul(
                    out=P2[:, 0:512], lhsT=onesb[0:1, 0:128],
                    rhs=crowb[0:1, 512 * tc:512 * (tc + 1)],
                    start=True, stop=True), inc="T")
                t_bc = P.cnt["T"]
                if tc == 0 and e + 1 < NE_RUN:
                    up_waits[e + 1] = load_expert(e + 1, "up", P.cnt["T"])
                if tc == 2 and e + 1 < NE_RUN:
                    dn_waits[e + 1] = load_expert(
                        e + 1, "dn", down_t_done.get(e - 1, 0) if e >= 1 else P.cnt["T"])
                P.wait("scalar", "T", t_dn_guard)
                if (e, tc - 2, 3) in v_hb:
                    P.wait("scalar", "V", v_hb[(e, tc - 2, 3)])
                elif tc < 2 and epr is not None and (epr, GC - 2 + tc, 3) in v_hb:
                    P.wait("scalar", "V", v_hb[(epr, GC - 2 + tc, 3)])
                ho = 2048 * (tc % 2)
                for ia, ib in ((2, 3), (0, 1)):
                    P.wait("scalar", "T", t_up_i[ib])
                    for i in (ia, ib):
                        reg = (P1 if i >= 2 else P0)[:, 512 * (i % 2):512 * (i % 2) + 512]
                        P.do("scalar", lambda E, i=i, reg=reg: E.activation(
                            out=rb[:, 512 * i:512 * (i + 1)], in_=reg, func=AF.Relu))
                    P.drain("scalar")
                    for i in (ia, ib):
                        ss_sq[(e, tc, i)] = P.do("scalar", lambda E, i=i, ho=ho: E.activation(
                            out=hb[:, ho + 512 * i:ho + 512 * (i + 1)], in_=rb[:, 512 * i:512 * (i + 1)],
                            func=AF.Square), inc="SS")
                P.wait("vector", "T", t_bc)
                v_bcc[(e, tc)] = P.do("vector", lambda E, tc=tc: E.tensor_copy(
                    out=bcsb[:, 512 * (tc % 2):512 * (tc % 2) + 512], in_=P2[:, 0:512]),
                    inc="V")
                pend_scale.append((e, tc))
            if e == NE_RUN - 1:
                while pend_scale:
                    se, stc = pend_scale.pop(0)
                    sho = 2048 * (stc % 2)
                    P.wait("vector", "SS", ss_sq[(se, stc, 1)])
                    for i in range(4):
                        v_hb[(se, stc, i)] = P.do(
                            "vector", lambda E, i=i, sho=sho, stc=stc: E.tensor_tensor(
                                out=hb[:, sho + 512 * i:sho + 512 * (i + 1)],
                                in0=hb[:, sho + 512 * i:sho + 512 * (i + 1)],
                                in1=bcsb[:, 512 * (stc % 2):512 * (stc % 2) + 512],
                                op=ALU.mult), inc="V")
                while pend_down:
                    de, dtc = pend_down.pop(0)
                    emit_down(de, dtc)
            t_dn_end[e] = P.cnt["T"]
            epr = e
            if e + 1 < NE_RUN:
                P.wait("scalar", "T", t_dn_end[e])
                P.dma("scalar", "LC", crowb[:], combT_d[e + 1:e + 2, GS * g:GS * (g + 1)])

        prev_up_end[0] = (t_up_end[NE_RUN - 2], t_up_end[NE_RUN - 1]) if NE_RUN >= 2 else None
        o_st = [0] * GT
        for j in range(GT if DEV["out"] else 0):
            bj = j % 2
            ob2 = outs if j % 2 == 0 else hbf[:, 0:1024]
            P.wait("sync", "V")
            P.dma("sync", "LX0" if bj == 0 else "LX1", stg[bj][:], ap["x"][GS * g + 128 * j:GS * g + 128 * (j + 1), :])
            P.wait("vector", "LX0" if bj == 0 else "LX1")
            if j >= 2:
                P.wait("vector", "O" if j % 2 == 0 else "O1", o_st[j - 2])
            P.do("vector", lambda E, bj=bj, j=j, ob2=ob2: E.tensor_tensor(
                out=ob2, in0=stg[bj][:], in1=acc[:, 1024 * j:1024 * (j + 1)],
                op=ALU.add), inc="V")
            P.wait("gpsimd", "V")
            o_st[j] = P.dma("gpsimd", "O" if j % 2 == 0 else "O1",
                            out_hbm[GS * g + 128 * j:GS * g + 128 * (j + 1), :], ob2)
        v_out = (P.cnt["O"], P.cnt["O1"])
        v_stg_free[0] = P.cnt["V"]

    for tname in DEV["taps"]:
        t = tens[tname]
        shp = [t.shape[0], t.shape[1]]
        tap = nc.dram_tensor("tap_" + tname, shp, t[:].dtype, kind="ExternalOutput").ap()
        P.wait("gpsimd", "V")
        P.wait("gpsimd", "SS")
        P.wait("gpsimd", "T")
        P.dma("gpsimd", "O", tap, t[:])

    P.wait("gpsimd", "O")
    P.wait("gpsimd", "O1")

    with nc.Block() as block:
        @block.sync
        def _(E):
            P.run(E, "sync", handles)

        @block.tensor
        def _(E):
            P.run(E, "tensor", handles)

        @block.scalar
        def _(E):
            P.run(E, "scalar", handles)

        @block.vector
        def _(E):
            P.run(E, "vector", handles)

        @block.gpsimd
        def _(E):
            P.run(E, "gpsimd", handles)

    es.close()
    return nc, P


def _host_prep(inputs):
    qkv_w = np.asarray(inputs["L0_attn_qkv_w"], np.float32)
    qkv_b = np.asarray(inputs["L0_attn_qkv_b"], np.float32)
    proj_w = np.asarray(inputs["L0_attn_proj_w"], np.float32)
    proj_b = np.asarray(inputs["L0_attn_proj_b"], np.float32)
    gate_w = np.asarray(inputs["L0_ffn_gate_w"], np.float32)
    e_bias = np.asarray(inputs["L0_ffn_e_bias"], np.float32)
    up_w = np.asarray(inputs["L0_ffn_up_w"], np.float32)
    down_w = np.asarray(inputs["L0_ffn_down_w"], np.float32)
    sup_w = np.asarray(inputs["L0_ffn_shared_up_w"], np.float32)
    sdown_w = np.asarray(inputs["L0_ffn_shared_down_w"], np.float32)

    d = {}
    Wq, Wk, Wv = qkv_w[0:D], qkv_w[D:2 * D], qkv_w[2 * D:3 * D]
    bq, bk, bv = qkv_b[0:D], qkv_b[D:2 * D], qkv_b[2 * D:3 * D]
    d["wq_p"] = _pack_k(np.ascontiguousarray(Wq.T))
    d["wk_p"] = _pack_k(np.ascontiguousarray(Wk.T))
    d["wv_p"] = _pack_k(np.ascontiguousarray(Wv.T))
    d["pw_p"] = _pack_k(np.ascontiguousarray(proj_w.T))
    bqk = np.zeros((128, 16), np.float32)
    for p in range(NP):
        bqk[:, 2 * p] = bq[128 * p:128 * (p + 1)]
        bqk[:, 2 * p + 1] = bk[128 * p:128 * (p + 1)]
    d["bqk"] = bqk
    d["bvp"] = np.ascontiguousarray(bv[None, :])
    d["pjb"] = np.ascontiguousarray(proj_b[None, :])
    d["gw_p"] = _pack_k(np.ascontiguousarray(gate_w.T))
    d["ebt"] = np.ascontiguousarray(np.broadcast_to(e_bias[None, :], (128, 32)).copy())
    upl, dnl = [], []
    for e in range(32):
        upl.append(_pack_k(np.ascontiguousarray(up_w[e].T)))
        dnl.append(_pack_k(np.ascontiguousarray(down_w[e].T)))
    supT = np.ascontiguousarray(sup_w.T)
    sdnT = np.ascontiguousarray(sdown_w.T)
    for c in range(4):
        upl.append(_pack_k(np.ascontiguousarray(supT[:, 512 * c:512 * (c + 1)])))
        dnl.append(_pack_k(np.ascontiguousarray(sdnT[512 * c:512 * (c + 1), :])))
    d["upw"] = np.ascontiguousarray(np.stack(upl))
    d["dnw"] = np.ascontiguousarray(np.stack(dnl))
    inv = 1.0 / (10000.0 ** (np.arange(0, DH, 2, dtype=np.float64) / DH))
    t = np.arange(S, dtype=np.float64)
    fr = np.concatenate([np.outer(t, inv), np.outer(t, inv)], axis=1)
    cosT = np.cos(fr).T.astype(np.float32)
    sinT = np.sin(fr).T.astype(np.float32)
    d["cost"] = np.ascontiguousarray(np.vstack([cosT, cosT]).astype(ml_dtypes.bfloat16))
    d["sint"] = np.ascontiguousarray(np.vstack([sinT, sinT]).astype(ml_dtypes.bfloat16))
    M64 = np.zeros((64, 64), np.float32)
    for i in range(32):
        M64[i, i + 32] = -1.0
        M64[i + 32, i] = 1.0
    MT = np.zeros((128, 128), np.float32)
    MT[0:64, 0:64] = M64.T
    MT[64:128, 64:128] = M64.T
    d["mt"] = MT
    d["idn"] = np.eye(128, dtype=np.float32)
    esh = np.zeros((64, 128), np.float32)
    for i in range(64):
        esh[i, 64 + i] = 1.0
    d["esh"] = esh
    d["msk"] = np.ascontiguousarray(
        (np.arange(128)[:, None] <= np.arange(128)[None, :]).astype(np.float32))
    return d


def kernel(**inputs):
    x = np.asarray(inputs["L0_attn_x"], np.float32)
    shared = _host_prep(inputs)
    nc, _ = build_nc()
    in_maps = []
    for c in range(B):
        m = dict(shared)
        m["x"] = np.ascontiguousarray(x[c])
        in_maps.append(m)
    res = run_bass_kernel_spmd(nc, in_maps, core_ids=list(range(B)))
    out = np.stack([np.asarray(r["out"], np.float32) for r in res.results])
    return out


# version 31
# speedup vs baseline: 1.1078x; 1.0004x over previous
import contextlib
import numpy as np
import ml_dtypes
import concourse.bass as bass
import concourse.mybir as mybir
from concourse.bass_utils import run_bass_kernel_spmd

F32 = mybir.dt.float32
BF16 = mybir.dt.bfloat16
AF = mybir.ActivationFunctionType
ALU = mybir.AluOpType
AX = mybir.AxisListType

B, S, D = 8, 4096, 1024
H, DH = 16, 64
NP = 8
NE = 36
NT = S // 128
NG = 2
GS = S // NG
GT = GS // 128
GC = GS // 512
NEG = -1e30

ENGS = ("sync", "tensor", "scalar", "vector", "gpsimd")

DEV = {"pairs": NP, "experts": NE, "groups": NG, "attn": True, "ffn": True,
       "out": True, "taps": [], "astage": 4}


class Plan:
    def __init__(self):
        self.ops = {e: [] for e in ENGS}
        self.cnt = {s: 0 for s in ("LI", "LX0", "LX1", "CS0", "CS1", "LC", "LD", "LB", "LW", "LA", "PU0", "PU1", "T", "SS", "V", "O", "PS0", "PS1", "PS2", "PS3", "PS4", "PS5", "LB1", "LB2", "RB0", "RB1", "LW1", "LD1", "PJ0", "PJ1", "O1")}
        self.lastwait = {e: {} for e in ENGS}

    def do(self, eng, fn, inc=None, amt=1):
        if inc is not None:
            self.cnt[inc] += amt
            self.ops[eng].append((fn, inc, amt))
            return self.cnt[inc]
        self.ops[eng].append((fn, None, 0))
        return None

    def dma(self, eng, sem, dst, src):
        return self.do(eng, (lambda E, d=dst, s=src: E.dma_start(out=d, in_=s)),
                       inc=sem, amt=16)

    def wait(self, eng, sem, val=None):
        v = self.cnt[sem] if val is None else val
        if v <= 0:
            return
        if self.lastwait[eng].get(sem, -1) >= v:
            return
        self.lastwait[eng][sem] = v
        self.ops[eng].append((("wait", sem, v), None, 0))

    def drain(self, eng):
        self.ops[eng].append((("drain",), None, 0))

    def run(self, E, eng, handles):
        for fn, inc, amt in self.ops[eng]:
            if isinstance(fn, tuple):
                if fn[0] == "wait":
                    E.wait_ge(handles[fn[1]], fn[2])
                else:
                    E.drain()
                continue
            r = fn(E)
            if inc is not None:
                r.then_inc(handles[inc], amt)

    def n_ops(self):
        return {e: len(v) for e, v in self.ops.items()}


def _pack_k(w):
    K, N = w.shape
    assert K % 128 == 0
    return np.ascontiguousarray(
        w.reshape(K // 128, 128, N).transpose(1, 0, 2).reshape(128, -1))


def build_nc():
    nc = bass.Bass("TRN2", target_bir_lowering=False, debug=False)
    ap = {}

    def din(name, shape, dt=F32):
        ap[name] = nc.dram_tensor(name, list(shape), dt, kind="ExternalInput").ap()

    din("x", (S, D))
    din("wq_p", (128, 8192)); din("wk_p", (128, 8192)); din("wv_p", (128, 8192))
    din("pw_p", (128, 8192))
    din("bqk", (128, 16)); din("bvp", (1, 1024)); din("pjb", (1, 1024))
    din("gw_p", (128, 256)); din("ebt", (128, 32))
    din("upw", (NE, 128, 4096)); din("dnw", (NE, 128, 4096))
    din("cost", (128, S), BF16); din("sint", (128, S), BF16)
    din("mt", (128, 128)); din("idn", (128, 128)); din("msk", (128, 128))
    din("esh", (64, 128))
    out_hbm = nc.dram_tensor("out", [S, D], F32, kind="ExternalOutput").ap()
    upw_bf = nc.dram_tensor("upw_bf", [NE, 128, 4096], BF16, kind="Internal").ap()
    dnw_bf = nc.dram_tensor("dnw_bf", [NE, 128, 4096], BF16, kind="Internal").ap()
    combT_d = nc.dram_tensor("combT_d", [36, S], BF16, kind="Internal").ap()

    es = contextlib.ExitStack()
    tens = {}

    def sb(name, shape, dt):
        t = es.enter_context(nc.sbuf_tensor(name, list(shape), dt))
        tens[name] = t
        return t

    def psm(name, shape, dt):
        return es.enter_context(nc.psum_tensor(name, list(shape), dt))

    xnT = [sb(f"xnT{c}", (128, S), BF16) for c in range(8)]
    acc = sb("acc", (128, GT * 1024), BF16)
    wsl = [sb(f"wsl{i}", (128, 4096), BF16) for i in range(4)]
    cstg = [sb(f"cstg{i}", (36, 128), BF16) for i in range(2)]
    stg = [sb(f"stg{i}", (128, 1024), F32) for i in range(2)]
    attw = sb("attw", (128, 2048), BF16)
    vbuf = sb("vbuf", (128, 4160), BF16)
    qrot = sb("qrot", (128, GS), BF16)
    krot = sb("krot", (128, S), BF16)
    qpre = [sb(f"qpre{i}", (128, 512), BF16) for i in range(2)]
    probs = [sb(f"probs{i}", (128, 1024), BF16) for i in range(4)]
    vt1 = sb("vt1", (128, 512), F32)
    vt2 = sb("vt2", (128, 512), F32)
    sqn = qpre[0][:]
    cbuf = sb("cbuf", (1, 64), F32)
    ncb = sb("ncb", (1, 4), BF16)
    negC = sb("negC", (128, 8), F32)
    stAB = sb("stAB", (1, 1024), BF16)
    stABw = stg[1][0:1, 0:1024]
    bcsb = sb("bcsb", (128, 1024), BF16)
    ctxab = sb("ctxab", (128, 512), BF16)
    tmpb = qpre[1][0:64, :]
    rb = sb("rb", (128, 2048), BF16)
    outs = rb[:].bitcast(F32)
    hb = sb("hb", (128, 4096), BF16)
    crowb = sb("crowb", (1, 2048), BF16)
    bvpb = crowb[0:1, 0:1024]
    pjbb = crowb[0:1, 1024:2048]
    comb4 = [sb(f"comb{i}", (128, 36), F32) for i in range(4)]
    rtr4 = sb("rtr4", (128, 1024), F32)
    accf = acc[:].bitcast(F32)
    xnf = [accf[:, 0:1024], accf[:, 1024:2048]]
    csc = [sb(f"csc{i}", (128, 512), BF16) for i in range(2)]
    css = [sb(f"css{i}", (128, 512), BF16) for i in range(2)]
    mskb = sb("mskb", (128, 128), BF16)
    mtb = sb("mtb", (128, 128), BF16)
    eshb = sb("eshb", (64, 128), BF16)
    idsb = sb("idsb", (128, 128), F32)
    onesb = sb("onesb", (128, 128), BF16)
    gwb = sb("gwb", (128, 256), BF16)
    ebt = sb("ebt_s", (128, 32), F32)
    bqk = sb("bqk_s", (128, 16), F32)
    bqks = sb("bqks", (128, 16), F32)
    bnst = sb("bnst", (128, 12), F32)
    mv = sb("mv", (128, 2), F32)
    sc1 = sb("sc1", (128, 1), F32)
    rstd = sb("rstd", (128, 1), F32)
    nmr = sb("nmr", (128, 1), F32)

    P0 = psm("P0", (128, 1024), F32)
    P1 = psm("P1", (128, 1024), F32)
    P2 = psm("P2", (128, 1024), F32)
    P3 = psm("P3", (128, 1024), F32)

    handles = {}
    for sname in ("LI", "LX0", "LX1", "CS0", "CS1", "LC", "LD", "LB", "LW", "LA", "PU0", "PU1", "T", "SS", "V", "O", "PS0", "PS1", "PS2", "PS3", "PS4", "PS5", "LB1", "LB2", "RB0", "RB1", "LW1", "LD1", "PJ0", "PJ1", "O1"):
        handles[sname] = es.enter_context(nc.semaphore(sname))

    P = Plan()

    P.dma("sync", "LI", idsb[:], ap["idn"])
    P.dma("sync", "LI", stg[0][:, 0:128], ap["msk"])
    P.dma("sync", "LI", stg[0][:, 128:256], ap["mt"])
    P.dma("sync", "LI", stg[0][:, 256:272], ap["bqk"])
    P.dma("sync", "LI", ebt[:], ap["ebt"])
    P.dma("sync", "LI", stg[0][:, 272:528], ap["gw_p"])
    P.dma("sync", "LI", stg[0][0:64, 528:656], ap["esh"])
    P.wait("vector", "LI")
    P.do("vector", lambda E: E.tensor_copy(out=mskb[:], in_=stg[0][:, 0:128]))
    P.do("vector", lambda E: E.tensor_copy(out=mtb[:], in_=stg[0][:, 128:256]))
    P.do("vector", lambda E: E.tensor_copy(out=bqk[:], in_=stg[0][:, 256:272]))
    P.do("vector", lambda E: E.tensor_copy(out=gwb[:], in_=stg[0][:, 272:528]))
    P.do("vector", lambda E: E.tensor_copy(out=eshb[:], in_=stg[0][0:64, 528:656]))
    P.do("vector", lambda E: E.memset(onesb[:], 1.0))
    P.do("vector", lambda E: E.memset(vbuf[:], 1.0))
    P.do("vector", lambda E: E.memset(rb[:], 0.0))
    P.drain("vector")
    P.do("vector", lambda E: E.tensor_copy(out=bqks[:], in_=bqk[:]))
    P.drain("vector")
    bqv = bqks[:].rearrange("p (a b) -> p a b", b=2)
    P.do("vector", lambda E: E.tensor_scalar_mul(
        out=bqv[:, :, 0:1], in0=bqv[:, :, 0:1], scalar1=0.125), inc="V")

    pump_state = {"k": 0, "cast": {}, "out": {}}
    hbf = hb[:].bitcast(F32)

    PUMP_TOT = NE * 2 * 4
    PUMP_SLOTS = 6
    obuf = [rb[:, 0:2048], acc[:, 12288:14336], acc[:, 14336:16384]]

    def _pump_slot(j):
        s = j % PUMP_SLOTS
        if s < 2:
            return hbf[:, 1024 * s:1024 * (s + 1)]
        return accf[:, 2048 + 1024 * (s - 2):2048 + 1024 * (s - 1)]

    def _pump_issue_in(j):
        if j >= PUMP_TOT:
            return
        mat = j // 4
        jc = j % 4
        e = mat // 2
        srcw = ap["upw"] if mat % 2 == 0 else ap["dnw"]
        if j >= PUMP_SLOTS:
            P.wait("scalar", "V", pump_state["cast"][j - PUMP_SLOTS])
        P.dma("scalar", "PS%d" % (j % PUMP_SLOTS), _pump_slot(j),
              srcw[e, :, 1024 * jc:1024 * (jc + 1)])

    def emit_pump(nchunks):
        for _ in range(nchunks):
            k = pump_state["k"]
            if k >= PUMP_TOT:
                return
            pump_state["k"] = k + 1
            mat = k // 4
            e = mat // 2
            dstw = upw_bf if mat % 2 == 0 else dnw_bf
            h = k // 2
            ob = obuf[h % 3]
            if k == 0:
                P.wait("scalar", "V", 1)
                for j in range(PUMP_SLOTS - 1):
                    _pump_issue_in(j)
            _pump_issue_in(k + PUMP_SLOTS - 1)
            P.wait("vector", "PS%d" % (k % PUMP_SLOTS))
            if h >= 3:
                P.wait("vector", *pump_state["out"][h - 3])
            pump_state["cast"][k] = P.do(
                "vector", lambda E, k=k, ob=ob: E.tensor_copy(
                    out=ob[:, 1024 * (k % 2):1024 * (k % 2) + 1024], in_=_pump_slot(k)),
                inc="V")
            if k % 2 == 1:
                P.wait("gpsimd", "V", pump_state["cast"][k])
                osem = ("LB", "LB1", "LB2")[h % 3]
                pump_state["out"][h] = (osem, P.dma(
                    "gpsimd", osem, dstw[e, :, 2048 * (h % 2):2048 * (h % 2) + 2048],
                    ob[:]))

    ss_xn = [0] * NT
    v_tr = [0] * NT
    for i in range(NT):
        b = i % 2
        if i < 2:
            P.wait("sync", "V")
        else:
            P.wait("sync", "SS", ss_xn[i - 2])
        P.dma("sync", "LX0" if b == 0 else "LX1", stg[b][:], ap["x"][128 * i:128 * (i + 1), :])
        P.wait("vector", "LX0" if b == 0 else "LX1")
        P.do("vector", lambda E, b=b: E.bn_stats(out=bnst[:, 0:6], in_=stg[b][:, 0:512]))
        P.do("vector", lambda E, b=b: E.bn_stats(out=bnst[:, 6:12], in_=stg[b][:, 512:1024]))
        P.drain("vector")
        P.do("vector", lambda E: E.bn_aggr(out=mv[:], in_=bnst[:]))
        P.drain("vector")
        P.do("vector", lambda E: E.tensor_scalar_add(out=sc1[:], in0=mv[:, 1:2], scalar1=1e-5))
        P.drain("vector")
        P.do("vector", lambda E: E.reciprocal(out=sc1[:], in_=sc1[:]), inc="V")
        P.wait("scalar", "V")
        P.do("scalar", lambda E: E.activation(out=rstd[:], in_=sc1[:], func=AF.Sqrt), inc="SS")
        P.wait("vector", "SS")
        P.do("vector", lambda E: E.tensor_scalar(out=nmr[:], in0=mv[:, 0:1],
                                                 scalar1=rstd[:], scalar2=-1.0,
                                                 op0=ALU.mult, op1=ALU.mult), inc="V")
        P.wait("scalar", "V")
        if i >= 2:
            P.wait("scalar", "T")
        xb = xnf[b]
        ss_xn[i] = P.do("scalar", lambda E, b=b, xb=xb: E.activation(
            out=xb[:], in_=stg[b][:], func=AF.Identity, bias=nmr[:], scale=rstd[:]),
            inc="SS")
        P.wait("tensor", "SS")
        ss_tr = {}
        for c in range(8):
            reg = (P3[:, 0:128], P3[:, 512:640],
                   P0[:, 0:128], P0[:, 512:640])[c % 4]
            if c >= 4:
                P.wait("tensor", "SS", ss_tr[c - 4])
            P.do("tensor", lambda E, xb=xb, c=c, reg=reg: E.transpose(
                out=reg, in_=xb[:, 128 * c:128 * (c + 1)], identity=idsb[:]), inc="T")
            P.wait("scalar", "T")
            ss_tr[c] = P.do("scalar", lambda E, c=c, i=i, reg=reg: E.activation(
                out=xnT[c][:, 128 * i:128 * (i + 1)], in_=reg, func=AF.Identity),
                inc="SS")
        v_tr[i] = P.cnt["V"]
        if DEV["ffn"]:
            emit_pump(9)

    comb_lb = {}
    _steps = [
        (lambda E, r: E.tensor_add(out=r[:, 32:64], in0=r[:, 0:32], in1=ebt[:]),),
        tuple(
            (lambda E, r, a=a, bb=bb, j=j: E.tensor_add(
                out=r[:, 64 + 8 * j:72 + 8 * j],
                in0=r[:, 32:64].rearrange("p (g e) -> p g e", e=4)[:, :, a:a + 1].squeeze(2),
                in1=r[:, 32:64].rearrange("p (g e) -> p g e", e=4)[:, :, bb:bb + 1].squeeze(2)))
            for j, (a, bb) in enumerate([(0, 1), (0, 2), (0, 3), (1, 2), (1, 3), (2, 3)])),
        (lambda E, r: E.tensor_max(out=r[:, 112:120], in0=r[:, 64:72], in1=r[:, 72:80]),
         lambda E, r: E.tensor_max(out=r[:, 120:128], in0=r[:, 80:88], in1=r[:, 88:96]),
         lambda E, r: E.tensor_max(out=r[:, 128:136], in0=r[:, 96:104], in1=r[:, 104:112])),
        (lambda E, r: E.tensor_max(out=r[:, 112:120], in0=r[:, 112:120], in1=r[:, 120:128]),),
        (lambda E, r: E.tensor_max(out=r[:, 112:120], in0=r[:, 112:120], in1=r[:, 128:136]),),
        (lambda E, r: E.max(out=r[:, 120:128], in_=r[:, 112:120]),),
        (lambda E, r: E.memset(r[:, 124:128], NEG),),
        (lambda E, r: E.match_replace(out=r[:, 128:136], in_to_replace=r[:, 120:128],
                                      in_values=r[:, 112:120], imm_value=NEG),),
        (lambda E, r: E.tensor_tensor(out=r[:, 120:128], in0=r[:, 128:136],
                                      in1=r[:, 112:120], op=ALU.is_lt),),
        (lambda E, r: E.tensor_tensor(
            out=r[:, 136:168].rearrange("p (g e) -> p g e", e=4),
            in0=r[:, 32:64].rearrange("p (g e) -> p g e", e=4),
            in1=r[:, 120:128].unsqueeze(2).to_broadcast([128, 8, 4]),
            op=ALU.mult),),
        (lambda E, r: E.max(out=r[:, 168:176], in_=r[:, 136:168]),),
        (lambda E, r: E.memset(r[:, 174:176], NEG),),
        (lambda E, r: E.match_replace(out=r[:, 176:208], in_to_replace=r[:, 168:176],
                                      in_values=r[:, 136:168], imm_value=NEG),),
        (lambda E, r: E.tensor_tensor(out=r[:, 176:208], in0=r[:, 176:208],
                                      in1=r[:, 136:168], op=ALU.is_lt),),
        (lambda E, r: E.tensor_mul(out=r[:, 208:240], in0=r[:, 176:208], in1=r[:, 0:32]),),
        (lambda E, r: E.reduce_sum(out=r[:, 240:241], in_=r[:, 208:240], axis=AX.X),),
        (lambda E, r: E.tensor_scalar_add(out=r[:, 240:241], in0=r[:, 240:241],
                                          scalar1=1e-20),),
        (lambda E, r: E.reciprocal(out=r[:, 241:242], in_=r[:, 240:241]),),
    ]
    for q in range(NT // 4):
        tiles = [4 * q + t for t in range(4)]
        P.wait("tensor", "V")
        P.wait("tensor", "SS")
        for t, i in enumerate(tiles):
            for c in range(8):
                P.do("tensor", lambda E, c=c, i=i, t=t: E.matmul(
                    out=P3[:, 512 + 32 * t:544 + 32 * t],
                    lhsT=xnT[c][:, 128 * i:128 * (i + 1)],
                    rhs=gwb[:, 32 * c:32 * (c + 1)],
                    start=(c == 0), stop=(c == 7)), inc=("T" if c == 7 else None))
        P.wait("scalar", "T")
        for t in range(4):
            P.do("scalar", lambda E, t=t: E.activation(
                out=rtr4[:, 256 * t:256 * t + 32], in_=P3[:, 512 + 32 * t:544 + 32 * t],
                func=AF.Sigmoid), inc="SS")
        P.wait("vector", "SS")
        for step in _steps:
            for t in range(4):
                for fn in step:
                    P.do("vector", lambda E, fn=fn, t=t: fn(
                        E, rtr4[:, 256 * t:256 * (t + 1)]))
            P.drain("vector")
        P.wait("vector", "T")
        for t in range(4):
            r4 = rtr4[:, 256 * t:256 * (t + 1)]
            P.do("vector", lambda E, cb=comb4[t], r4=r4: E.tensor_scalar(
                out=cb[:, 0:32], in0=r4[:, 208:240],
                scalar1=r4[:, 241:242], scalar2=2.5,
                op0=ALU.mult, op1=ALU.mult))
            P.do("vector", lambda E, cb=comb4[t]: E.memset(cb[:, 32:36], 1.0))
        P.drain("vector")
        P.do("vector", lambda E: E.nop(), inc="V")
        for t, i in enumerate(tiles):
            P.wait("tensor", "V")
            P.do("tensor", lambda E, cb=comb4[t]: E.transpose(
                out=P3[0:36, 384:512], in_=cb[:], identity=idsb[:]), inc="T")
            P.wait("vector", "T")
            if i >= 2:
                P.wait("vector", "RB%d" % (i % 2), comb_lb[i - 2])
            P.do("vector", lambda E, i=i: E.tensor_copy(
                out=cstg[i % 2][:], in_=P3[0:36, 384:512]), inc="V")
            P.wait("gpsimd", "V")
            comb_lb[i] = P.dma("gpsimd", "RB%d" % (i % 2),
                               combT_d[0:36, 128 * i:128 * (i + 1)], cstg[i % 2][:])

    v_out = (0, 0)
    prev_last_exp = [0]
    t_attw_free = [0]
    v_stg_free = [0]
    att_end_prev = [0]
    prev_up_end = [None]
    krotf = krot[:].bitcast(F32)
    for g in range(DEV["groups"]):
        KC = 4 * (g + 1)
        KT = 16 * (g + 1)

        P.wait("scalar", "T", att_end_prev[0])
        v_pw = {}
        for hchunk in range(8):
            sl = krotf[:, 1024 * (hchunk % 2):1024 * (hchunk % 2) + 1024]
            if hchunk >= 2:
                P.wait("scalar", "V", v_pw[hchunk - 2])
            P.dma("scalar", "PJ%d" % (hchunk % 2), sl,
                  ap["pw_p"][:, 1024 * hchunk:1024 * (hchunk + 1)])
            P.wait("vector", "PJ%d" % (hchunk % 2))
            if prev_up_end[0] is not None:
                P.wait("vector", "T", prev_up_end[0][hchunk // 4])
            dstsl = wsl[hchunk // 4]
            off = 1024 * (hchunk % 4)
            v_pw[hchunk] = P.do("vector", lambda E, s=sl, d=dstsl, off=off: E.tensor_copy(
                out=d[:, off:off + 1024], in_=s[:]), inc="V")
        if g == 1:
            P.wait("vector", "O", v_out[0])
            P.wait("vector", "O1", v_out[1])
        P.wait("sync", "T")
        P.wait("sync", "V")
        P.wait("sync", "SS")
        P.dma("sync", "LX1", stg[1][0:1, 0:1024], ap["bvp"])
        P.wait("vector", "LX1")
        P.wait("vector", "LC")
        P.wait("vector", "T")
        P.do("vector", lambda E: E.tensor_copy(out=bvpb, in_=stg[1][0:1, 0:1024]), inc="V")
        P.wait("sync", "V")
        P.dma("sync", "LX1", stg[1][0:1, 0:1024], ap["pjb"])
        P.wait("vector", "LX1")
        P.do("vector", lambda E: E.tensor_copy(out=pjbb, in_=stg[1][0:1, 0:1024]), inc="V")
        v_stg_free[0] = P.cnt["V"]

        for p in range(DEV["pairs"] if DEV["attn"] else 0):
            P.wait("scalar", "T", t_attw_free[0])
            P.wait("scalar", "V", v_stg_free[0])
            for dc in range(8):
                P.dma("scalar", "LA", stg[0][:, 128 * dc:128 * (dc + 1)],
                      ap["wq_p"][:, 1024 * dc + 128 * p:1024 * dc + 128 * (p + 1)])
            P.wait("vector", "LA")
            P.wait("vector", "T", t_attw_free[0])
            v_attw = P.do("vector", lambda E: E.tensor_copy(out=attw[:, 0:1024], in_=stg[0][:]),
                          inc="V")
            P.wait("scalar", "V", v_attw)
            for dc in range(8):
                P.dma("scalar", "LA", stg[0][:, 128 * dc:128 * (dc + 1)],
                      ap["wk_p"][:, 1024 * dc + 128 * p:1024 * dc + 128 * (p + 1)])
            P.wait("vector", "LA")
            v_attw_k = P.do("vector", lambda E: E.tensor_copy(out=attw[:, 1024:2048], in_=stg[0][:]),
                            inc="V")

            ss_pre = {}
            ropev = []
            reg_of = {}
            ckc = [0]
            t_q_end = 0
            for kind in (("q",) if DEV.get("skipk") else ("q", "k")):
                if kind == "k":
                    t_q_end = P.cnt["T"]
                    P.wait("scalar", "V", v_attw_k)
                    for dc in range(8):
                        P.dma("scalar", "LA", stg[0][:, 128 * dc:128 * (dc + 1)],
                              ap["wv_p"][:, 1024 * dc + 128 * p:1024 * dc + 128 * (p + 1)])
                    P.wait("vector", "LA")
                    P.wait("vector", "T", t_q_end)
                    v_attw = P.do("vector", lambda E: E.tensor_copy(out=attw[:, 0:1024],
                                                                    in_=stg[0][:]), inc="V")
                half = 0 if kind == "q" else 1024
                nch = GC if kind == "q" else KC

                def emit_qk_mms(ci, kind=kind, p=p, half=half):
                    c = (GC * g + ci) if kind == "q" else ci
                    b = ci % 2
                    rot = ckc[0] % 3
                    ckc[0] += 1
                    reg_of[(kind, ci)] = rot
                    P.wait("sync", "V")
                    P.dma("sync", "CS0" if b == 0 else "CS1", csc[b][:], ap["cost"][:, 512 * c:512 * (c + 1)])
                    P.dma("sync", "CS0" if b == 0 else "CS1", css[b][:], ap["sint"][:, 512 * c:512 * (c + 1)])
                    qkreg = (P3, P0, P1)[rot][:, 0:512]
                    wv_need = v_attw if kind == "q" else v_attw_k
                    if len(ropev) >= 3:
                        P.wait("tensor", "V", max(wv_need, ropev[-3]))
                    else:
                        P.wait("tensor", "V", wv_need)
                    if (kind, ci - 3) in ss_pre:
                        P.wait("tensor", "SS", ss_pre[(kind, ci - 3)])
                    elif kind == "k" and ("q", GC - 3 + ci) in ss_pre:
                        P.wait("tensor", "SS", ss_pre[("q", GC - 3 + ci)])
                    for dc in range(8):
                        P.do("tensor", lambda E, dc=dc, c=c, qkreg=qkreg, half=half: E.matmul(
                            out=qkreg, lhsT=attw[:, half + 128 * dc:half + 128 * (dc + 1)],
                            rhs=xnT[dc][:, 512 * c:512 * (c + 1)],
                            start=(dc == 0), stop=(dc == 7)),
                            inc=("T" if dc == 7 else None))
                    return P.cnt["T"]

                def emit_qk_tail(ci, kind=kind, p=p):
                    b = ci % 2
                    rot = reg_of[(kind, ci)]
                    qkreg = (P3, P0, P1)[rot][:, 0:512]
                    rreg = (P3, P0, P1)[rot][:, 512:1024]
                    P.wait("scalar", "T", qk_t[(kind, ci)])
                    if len(ropev) >= 2:
                        P.wait("scalar", "V", ropev[-2])
                    bias = bqks[:, 2 * p:2 * p + 1] if kind == "q" else bqk[:, 2 * p + 1:2 * p + 2]
                    scale = 0.125 if kind == "q" else 1.0
                    ss_pre[(kind, ci)] = P.do(
                        "scalar", lambda E, b=b, bias=bias, scale=scale, qkreg=qkreg:
                        E.activation(out=qpre[b][:], in_=qkreg, func=AF.Identity,
                                     bias=bias, scale=scale), inc="SS")
                    P.wait("tensor", "SS", ss_pre[(kind, ci)])
                    if len(ropev) >= 2:
                        P.wait("tensor", "V", ropev[-2])
                    P.do("tensor", lambda E, b=b, rreg=rreg: E.matmul(
                        out=rreg, lhsT=mtb[:], rhs=qpre[b][:], start=True, stop=True),
                        inc="T")
                    dst = qrot if kind == "q" else krot
                    dof = 512 * ci if kind == "q" else 512 * ci
                    P.wait("vector", "T")
                    P.wait("vector", "CS0" if b == 0 else "CS1")
                    P.drain("vector")
                    P.do("vector", lambda E, b=b: E.tensor_mul(
                        out=vt1[:], in0=qpre[b][:], in1=csc[b][:]))
                    P.do("vector", lambda E, b=b, rreg=rreg: E.tensor_tensor(
                        out=vt2[:], in0=rreg, in1=css[b][:], op=ALU.mult))
                    P.drain("vector")
                    ropev.append(P.do("vector", lambda E, dst=dst, dof=dof: E.tensor_add(
                        out=dst[:, dof:dof + 512], in0=vt1[:], in1=vt2[:]), inc="V"))

                qk_t = getattr(P, "_qk_t", None)
                if qk_t is None:
                    qk_t = {}
                    P._qk_t = qk_t
                for ci in range(nch + 1):
                    if ci < nch:
                        qk_t[(kind, ci)] = emit_qk_mms(ci)
                    if ci - 1 >= 0:
                        emit_qk_tail(ci - 1)
            if DEV["astage"] < 2:
                continue
            P.wait("tensor", "V")
            v_ev = [0] * KT
            for tt in range(KT):
                if g == 0 and DEV["ffn"]:
                    emit_pump(6)
                vreg = (P2[:, 0:128], P2[:, 640:768],
                        P0[:, 0:128], P0[:, 640:768])[tt % 4]
                if tt >= 4:
                    P.wait("tensor", "V", v_ev[tt - 4])
                for dc in range(8):
                    P.do("tensor", lambda E, dc=dc, tt=tt, vreg=vreg: E.matmul(
                        out=vreg, lhsT=xnT[dc][:, 128 * tt:128 * (tt + 1)],
                        rhs=attw[:, 128 * dc:128 * (dc + 1)],
                        start=(dc == 0), stop=False))
                P.do("tensor", lambda E, p=p, vreg=vreg: E.matmul(
                    out=vreg, lhsT=onesb[0:1, 0:128],
                    rhs=bvpb[:, 128 * p:128 * (p + 1)],
                    start=False, stop=True), inc="T")
                P.wait("vector", "T")
                P.do("vector", lambda E, tt=tt, vreg=vreg: E.tensor_copy(
                    out=vbuf[:, 130 * tt:130 * tt + 64], in_=vreg[:, 0:64]))
                v_ev[tt] = P.do("vector", lambda E, tt=tt, vreg=vreg: E.tensor_copy(
                    out=vbuf[:, 130 * tt + 65:130 * tt + 129], in_=vreg[:, 64:128]), inc="V")
            t_attw_free[0] = P.cnt["T"]
            v_stg_free[0] = v_attw

            if DEV["astage"] < 3:
                continue
            if True:
              for kind, nch, buf in (("q", GC, qrot), ("k", KC, krot)):
                nmm_t = {}
                sq_v = {}
                red_v = {}
                for ci in range(nch + 2):
                    if ci < nch:
                        sqb = qpre[ci % 2]
                        P.wait("vector", "T", nmm_t.get(ci - 2, 0))
                        sq_v[ci] = P.do("vector", lambda E, buf=buf, ci=ci, sqb=sqb: E.tensor_mul(
                            out=sqb[:], in0=buf[:, 512 * ci:512 * (ci + 1)],
                            in1=buf[:, 512 * ci:512 * (ci + 1)]), inc="V")
                    if 0 <= ci - 1 < nch:
                        c2 = ci - 1
                        sqb = qpre[c2 % 2]
                        par = 512 * (c2 % 2)
                        P.wait("tensor", "V", sq_v[c2])
                        P.wait("tensor", "V", red_v.get(c2 - 2, 0))
                        P.do("tensor", lambda E, sqb=sqb, par=par: E.matmul(
                            out=P2[0:1, par:par + 512], lhsT=onesb[0:64, 0:1], rhs=sqb[0:64, :],
                            start=True, stop=True, tile_position=(0, 0)))
                        nmm_t[c2] = P.do("tensor", lambda E, sqb=sqb, par=par: E.matmul(
                            out=P1[0:1, par:par + 512], lhsT=onesb[64:128, 0:1], rhs=sqb[64:128, :],
                            start=True, stop=True, tile_position=(64, 0)), inc="T")
                    if 0 <= ci - 2 < nch:
                        c2 = ci - 2
                        par = 512 * (c2 % 2)
                        P.wait("vector", "T", nmm_t[c2])
                        ofA = (0 if kind == "q" else 16) + c2
                        ofB = (8 if kind == "q" else 32) + c2
                        P.do("vector", lambda E, ofA=ofA, par=par: E.reduce_max(
                            out=cbuf[0:1, ofA:ofA + 1], in_=P2[0:1, par:par + 512], axis=AX.X))
                        red_v[c2] = P.do("vector", lambda E, ofB=ofB, par=par: E.reduce_max(
                            out=cbuf[0:1, ofB:ofB + 1], in_=P1[0:1, par:par + 512], axis=AX.X), inc="V")
            P.drain("vector")
            P.do("vector", lambda E: E.reduce_max(out=cbuf[0:1, 40:41], in_=cbuf[0:1, 0:GC], axis=AX.X))
            P.do("vector", lambda E: E.reduce_max(out=cbuf[0:1, 41:42], in_=cbuf[0:1, 8:8 + GC], axis=AX.X))
            P.do("vector", lambda E, KC=KC: E.reduce_max(out=cbuf[0:1, 42:43], in_=cbuf[0:1, 16:16 + KC], axis=AX.X))
            P.do("vector", lambda E, KC=KC: E.reduce_max(out=cbuf[0:1, 43:44], in_=cbuf[0:1, 32:32 + KC], axis=AX.X))
            P.drain("vector")
            P.do("vector", lambda E: E.tensor_max(out=cbuf[0:1, 40:41], in0=cbuf[0:1, 40:41], in1=cbuf[0:1, 41:42]))
            P.do("vector", lambda E: E.tensor_max(out=cbuf[0:1, 42:43], in0=cbuf[0:1, 42:43], in1=cbuf[0:1, 43:44]))
            P.drain("vector")
            P.do("vector", lambda E: E.tensor_add(out=cbuf[0:1, 44:45], in0=cbuf[0:1, 40:41], in1=cbuf[0:1, 42:43]))
            P.drain("vector")
            P.wait("vector", "SS", prev_last_exp[0])
            P.do("vector", lambda E: E.tensor_scalar_mul(out=ncb[0:1, 0:1], in0=cbuf[0:1, 44:45], scalar1=-1.0 / 16.0), inc="V")
            P.wait("tensor", "V")
            P.do("tensor", lambda E: E.matmul(out=P3[:, 0:1], lhsT=onesb[0:1, 0:128],
                                              rhs=ncb[0:1, 0:1], start=True, stop=True), inc="T")
            P.wait("vector", "T")
            P.do("vector", lambda E: E.tensor_copy(out=negC[:, 0:1], in_=P3[:, 0:1]), inc="V")
            v_negc = P.cnt["V"]
            P.wait("scalar", "V")

            if DEV["astage"] < 4:
                continue
            exps_all = [0, 0, 0]
            ctx_all = [0, 0]
            v_probs = {}
            sc_t = {}
            ctx_of = {}
            gof = [0]
            prev_proj_v = [0]
            for sq in range(4):
                sjg = 4 * g + sq
                ntc = 4 * sjg + 4
                gb = gof[0]
                t_ctx_last = 0
                diag_of = {}

                def emit_sc(tc, sq=sq, gb=gb):
                    PS = (P0, P1, P3)[tc % 3]
                    if PS is P3:
                        P.wait("tensor", "V", v_negc)
                    P.wait("tensor", "SS", exps_all[-2])
                    P.do("tensor", lambda E, tc=tc, sq=sq, PS=PS: E.matmul(
                        out=PS[:, 0:512], lhsT=krot[0:64, 128 * tc:128 * (tc + 1)],
                        rhs=qrot[0:64, 512 * sq:512 * (sq + 1)],
                        start=True, stop=True, tile_position=(0, 0)))
                    sc_t[gb + tc] = P.do("tensor", lambda E, tc=tc, sq=sq, PS=PS: E.matmul(
                        out=PS[:, 512:1024], lhsT=krot[64:128, 128 * tc:128 * (tc + 1)],
                        rhs=qrot[64:128, 512 * sq:512 * (sq + 1)],
                        start=True, stop=True, tile_position=(64, 0)), inc="T")

                def emit_exp(tc, gb=gb):
                    PS = (P0, P1, P3)[tc % 3]
                    gt = gb + tc
                    pr = probs[gt % 4]
                    P.wait("scalar", "T", sc_t[gt])
                    P.wait("scalar", "T", ctx_of.get(gt - 4, 0))
                    if (gt - 4) in v_probs:
                        P.wait("scalar", "V", v_probs[gt - 4])
                    ev = P.do("scalar", lambda E, PS=PS, pr=pr: E.activation(
                        out=pr[:], in_=PS[:], func=AF.Exp, bias=negC[:, 0:1]), inc="SS")
                    exps_all.append(ev)
                    r = tc - 4 * sjg
                    if r >= 0:
                        diag_of[tc] = r
                        P.wait("vector", "SS", ev)
                        for half in (0, 512):
                            P.do("vector", lambda E, pr=pr, r=r, half=half: E.tensor_tensor(
                                out=pr[:, half + 128 * r:half + 128 * (r + 1)],
                                in0=pr[:, half + 128 * r:half + 128 * (r + 1)], in1=mskb[:],
                                op=ALU.mult))
                            if r > 0:
                                P.do("vector", lambda E, pr=pr, r=r, half=half: E.memset(
                                    pr[:, half:half + 128 * r], 0.0))
                        P.drain("vector")
                        P.do("vector", lambda E: E.nop(), inc="V")
                        v_probs[gt] = P.cnt["V"]

                exp_of = {}
                for i in range(ntc + 2):
                    if i < ntc:
                        emit_sc(i)
                    if 0 <= i - 1 < ntc:
                        emit_exp(i - 1)
                        exp_of[i - 1] = exps_all[-1]
                    if 0 <= i - 2 < ntc:
                        tcx = i - 2
                        gt = gb + tcx
                        pr = probs[gt % 4]
                        if tcx == 0:
                            P.wait("tensor", "V", prev_proj_v[0])
                        if tcx in diag_of:
                            P.wait("tensor", "V", v_probs[gt])
                        else:
                            P.wait("tensor", "SS", exp_of[tcx])
                        st = (tcx == 0)
                        sp = (tcx == ntc - 1)
                        P.do("tensor", lambda E, tc=tcx, pr=pr, st=st, sp=sp: E.matmul(
                            out=P2[0:65, 0:512], lhsT=vbuf[:, 130 * tc:130 * tc + 65],
                            rhs=pr[:, 0:512], start=st, stop=sp))
                        t_ctx_last = P.do("tensor", lambda E, tc=tcx, pr=pr, st=st, sp=sp: E.matmul(
                            out=P2[0:65, 512:1024], lhsT=vbuf[:, 130 * tc + 65:130 * tc + 130],
                            rhs=pr[:, 512:1024], start=st, stop=sp), inc="T")
                        ctx_all.append(t_ctx_last)
                        ctx_of[gt] = t_ctx_last
                P.wait("vector", "T", t_ctx_last)
                P.do("vector", lambda E: E.tensor_copy(out=tmpb, in_=P2[0:64, 512:1024]),
                     inc="V")
                P.wait("scalar", "T", t_ctx_last)
                P.do("scalar", lambda E: E.activation(out=stABw, in_=P2[64:65, 0:1024],
                                                      func=AF.Ln))
                P.drain("scalar")
                P.do("scalar", lambda E: E.activation(out=stAB[:, 0:1024], in_=stABw,
                                                      func=AF.Exp, scale=-1.0), inc="SS")
                P.wait("tensor", "V")
                P.wait("tensor", "SS")
                P.do("tensor", lambda E: E.matmul(out=P3[:, 0:512], lhsT=onesb[0:1, 0:128],
                                                  rhs=stAB[0:1, 0:512], start=True, stop=True))
                P.do("tensor", lambda E: E.matmul(out=P3[:, 512:1024], lhsT=onesb[0:1, 0:128],
                                                  rhs=stAB[0:1, 512:1024], start=True, stop=True))
                P.do("tensor", lambda E: E.matmul(out=P2[:, 512:1024], lhsT=eshb[:],
                                                  rhs=tmpb, start=True, stop=True), inc="T")
                P.wait("vector", "T")
                P.do("vector", lambda E: E.tensor_copy(out=bcsb[:], in_=P3[:, 0:1024]))
                P.drain("vector")
                P.do("vector", lambda E: E.tensor_mul(out=ctxab[0:64, :], in0=P2[0:64, 0:512],
                                                      in1=bcsb[0:64, 0:512]))
                v_ctx_scale_prev = P.do("vector", lambda E: E.tensor_mul(
                    out=ctxab[64:128, :], in0=P2[64:128, 512:1024],
                    in1=bcsb[64:128, 512:1024]), inc="V")
                P.wait("tensor", "V", v_ctx_scale_prev)
                for msub in range(4):
                    preg = P3 if msub % 2 == 0 else P2
                    jloc = 4 * sq + msub
                    if msub >= 2:
                        P.wait("tensor", "V")
                    lastm = None
                    for n in range(2):
                        last = (n == 1 and p > 0)
                        P.do("tensor", lambda E, p=p, n=n, msub=msub, preg=preg: E.matmul(
                            out=preg[:, 512 * n:512 * (n + 1)],
                            lhsT=ctxab[:, 128 * msub:128 * (msub + 1)],
                            rhs=wsl[p // 4][:, 1024 * (p % 4) + 512 * n:1024 * (p % 4) + 512 * (n + 1)],
                            start=True, stop=(p > 0)), inc=("T" if last else None))
                        if p == 0:
                            P.do("tensor", lambda E, n=n, preg=preg: E.matmul(
                                out=preg[:, 512 * n:512 * (n + 1)], lhsT=onesb[0:1, 0:128],
                                rhs=pjbb[:, 512 * n:512 * (n + 1)],
                                start=False, stop=True), inc=("T" if n == 1 else None))
                    P.wait("vector", "T")
                    if p == 0:
                        P.do("vector", lambda E, jloc=jloc, preg=preg: E.tensor_copy(
                            out=acc[:, 1024 * jloc:1024 * (jloc + 1)], in_=preg[:]), inc="V")
                    else:
                        P.do("vector", lambda E, jloc=jloc, preg=preg: E.tensor_tensor(
                            out=acc[:, 1024 * jloc:1024 * (jloc + 1)],
                            in0=acc[:, 1024 * jloc:1024 * (jloc + 1)], in1=preg[:],
                            op=ALU.add), inc="V")
                    prev_proj_v[0] = P.cnt["V"]
                gof[0] += ntc
            prev_last_exp[0] = exps_all[-1]
        def load_expert(e, which, tfree):
            sl = wsl[(e % 2) if which == "up" else 2 + (e % 2)]
            srcw = upw_bf if which == "up" else dnw_bf
            if which == "up":
                eng, sem = "sync", ("LW" if e % 2 == 0 else "LW1")
            else:
                eng, sem = "gpsimd", ("LD" if e % 2 == 0 else "LD1")
            P.wait(eng, "T", tfree)
            need_h = 4 * e + (1 if which == "up" else 3)
            P.wait(eng, *pump_state["out"][need_h])
            cnt = P.dma(eng, sem, sl[:], srcw[e])
            return [(sem, cnt)]

        t_att_end = P.cnt["T"]
        att_end_prev[0] = t_att_end
        if g == 0 and DEV["ffn"]:
            emit_pump(NE * 2 * 8)
        P.wait("scalar", "O")
        P.wait("scalar", "O1")
        P.wait("scalar", "LB")
        P.wait("scalar", "LB1")
        P.wait("scalar", "LB2")
        P.wait("scalar", "RB0")
        P.wait("scalar", "RB1")
        if not DEV["attn"] or DEV["pairs"] == 0:
            P.do("vector", lambda E: E.memset(acc[:], 0.0), inc="V")
        NE_RUN = DEV["experts"] if DEV["ffn"] else 0
        up_v = load_expert(0, "up", t_att_end) if NE_RUN else None
        dn_v = load_expert(0, "dn", t_att_end) if NE_RUN else None
        dn_waits = {0: dn_v}
        up_waits = {0: up_v}
        if NE_RUN:
            P.wait("scalar", "T")
            P.wait("scalar", "V")
            P.dma("scalar", "LC", crowb[:], combT_d[0:1, GS * g:GS * (g + 1)])

        t_up_end = [0] * NE
        t_dn_end = [0] * NE
        down_t_done = {}
        v_hb = {}
        v_dn = {}
        ss_sq = {}
        dslots = [P3[:, 0:512], P3[:, 512:1024], P2[:, 512:1024]]
        dslot_last = {}
        epr = None
        pend_down = []
        pend_scale = []
        v_bcc = {}
        for e in range(NE_RUN):
            def emit_down(e, tc):
                for sm, vv in (dn_waits.get(e) or []):
                    P.wait("tensor", sm, vv)
                P.wait("tensor", "V", v_hb[(e, tc, 3)])
                for tt in range(4):
                    jloc = 4 * tc + tt
                    for n in range(2):
                        slot = dslots[(2 * (4 * tc + tt) + n) % 3]
                        key_prev = dslot_last.get((2 * (4 * tc + tt) + n) % 3)
                        if key_prev is not None:
                            P.wait("tensor", "V", key_prev)
                        ho = 2048 * (tc % 2)
                        for i in range(4):
                            P.do("tensor", lambda E, i=i, n=n, tt=tt, e=e, slot=slot, ho=ho: E.matmul(
                                out=slot,
                                lhsT=hb[:, ho + 512 * i + 128 * tt:ho + 512 * i + 128 * (tt + 1)],
                                rhs=wsl[2 + e % 2][:, 1024 * i + 512 * n:1024 * i + 512 * (n + 1)],
                                start=(i == 0), stop=(i == 3)),
                                inc=("T" if i == 3 else None))
                        P.wait("vector", "T")
                        vv = P.do("vector", lambda E, jloc=jloc, n=n, slot=slot: E.tensor_tensor(
                            out=acc[:, 1024 * jloc + 512 * n:1024 * jloc + 512 * (n + 1)],
                            in0=acc[:, 1024 * jloc + 512 * n:1024 * jloc + 512 * (n + 1)],
                            in1=slot, op=ALU.add), inc="V")
                        dslot_last[(2 * (4 * tc + tt) + n) % 3] = vv
                        v_dn[(e, tc, tt, n)] = vv
                v_dn[(e, tc)] = v_dn[(e, tc, 3, 1)]
                down_t_done[e] = P.cnt["T"]

            for tc in range(GC):
                TC = GS * g + 512 * tc
                pend_down.append((e, tc))
                t_dn_guard = 0
                if len(pend_down) >= 3:
                    de, dtc = pend_down.pop(0)
                    emit_down(de, dtc)
                    t_dn_guard = P.cnt["T"]
                if pend_scale:
                    se, stc = pend_scale.pop(0)
                    sho = 2048 * (stc % 2)
                    P.wait("vector", "SS", ss_sq[(se, stc, 1)])
                    for i in range(4):
                        v_hb[(se, stc, i)] = P.do(
                            "vector", lambda E, i=i, sho=sho, stc=stc: E.tensor_tensor(
                                out=hb[:, sho + 512 * i:sho + 512 * (i + 1)],
                                in0=hb[:, sho + 512 * i:sho + 512 * (i + 1)],
                                in1=bcsb[:, 512 * (stc % 2):512 * (stc % 2) + 512],
                                op=ALU.mult), inc="V")
                t_up_i = {}
                for i in (2, 3, 0, 1):
                    reg = (P1 if i >= 2 else P0)[:, 512 * (i % 2):512 * (i % 2) + 512]
                    if (e, tc - 1, i) in ss_sq:
                        P.wait("tensor", "SS", ss_sq[(e, tc - 1, i)])
                    elif tc == 0 and (epr, GC - 1, i) in ss_sq:
                        P.wait("tensor", "SS", ss_sq[(epr, GC - 1, i)])
                    if tc == 0:
                        for sm, vv in (up_waits.get(e) or []):
                            P.wait("tensor", sm, vv)
                    for dc in range(8):
                        P.do("tensor", lambda E, dc=dc, i=i, TC=TC, reg=reg, e=e: E.matmul(
                            out=reg, lhsT=wsl[e % 2][:, 512 * dc + 128 * i:512 * dc + 128 * (i + 1)],
                            rhs=xnT[dc][:, TC:TC + 512],
                            start=(dc == 0), stop=(dc == 7)),
                            inc=("T" if dc == 7 else None))
                    t_up_i[i] = P.cnt["T"]
                t_up_this = P.cnt["T"]
                if tc == GC - 1:
                    t_up_end[e] = t_up_this
                if (e, tc - 1) in v_bcc:
                    P.wait("tensor", "V", v_bcc[(e, tc - 1)])
                elif tc == 0 and (epr, GC - 1) in v_bcc:
                    P.wait("tensor", "V", v_bcc[(epr, GC - 1)])
                P.wait("tensor", "LC")
                P.do("tensor", lambda E, tc=tc: E.matmul(
                    out=P2[:, 0:512], lhsT=onesb[0:1, 0:128],
                    rhs=crowb[0:1, 512 * tc:512 * (tc + 1)],
                    start=True, stop=True), inc="T")
                t_bc = P.cnt["T"]
                if tc == 0 and e + 1 < NE_RUN:
                    up_waits[e + 1] = load_expert(e + 1, "up", P.cnt["T"])
                if tc == 2 and e + 1 < NE_RUN:
                    dn_waits[e + 1] = load_expert(
                        e + 1, "dn", down_t_done.get(e - 1, 0) if e >= 1 else P.cnt["T"])
                P.wait("scalar", "T", t_dn_guard)
                if (e, tc - 2, 3) in v_hb:
                    P.wait("scalar", "V", v_hb[(e, tc - 2, 3)])
                elif tc < 2 and epr is not None and (epr, GC - 2 + tc, 3) in v_hb:
                    P.wait("scalar", "V", v_hb[(epr, GC - 2 + tc, 3)])
                ho = 2048 * (tc % 2)
                for ia, ib in ((2, 3), (0, 1)):
                    P.wait("scalar", "T", t_up_i[ib])
                    for i in (ia, ib):
                        reg = (P1 if i >= 2 else P0)[:, 512 * (i % 2):512 * (i % 2) + 512]
                        P.do("scalar", lambda E, i=i, reg=reg: E.activation(
                            out=rb[:, 512 * i:512 * (i + 1)], in_=reg, func=AF.Relu))
                    P.drain("scalar")
                    for i in (ia, ib):
                        ss_sq[(e, tc, i)] = P.do("scalar", lambda E, i=i, ho=ho: E.activation(
                            out=hb[:, ho + 512 * i:ho + 512 * (i + 1)], in_=rb[:, 512 * i:512 * (i + 1)],
                            func=AF.Square), inc="SS")
                P.wait("vector", "T", t_bc)
                v_bcc[(e, tc)] = P.do("vector", lambda E, tc=tc: E.tensor_copy(
                    out=bcsb[:, 512 * (tc % 2):512 * (tc % 2) + 512], in_=P2[:, 0:512]),
                    inc="V")
                pend_scale.append((e, tc))
            if e == NE_RUN - 1:
                while pend_scale:
                    se, stc = pend_scale.pop(0)
                    sho = 2048 * (stc % 2)
                    P.wait("vector", "SS", ss_sq[(se, stc, 1)])
                    for i in range(4):
                        v_hb[(se, stc, i)] = P.do(
                            "vector", lambda E, i=i, sho=sho, stc=stc: E.tensor_tensor(
                                out=hb[:, sho + 512 * i:sho + 512 * (i + 1)],
                                in0=hb[:, sho + 512 * i:sho + 512 * (i + 1)],
                                in1=bcsb[:, 512 * (stc % 2):512 * (stc % 2) + 512],
                                op=ALU.mult), inc="V")
                while pend_down:
                    de, dtc = pend_down.pop(0)
                    emit_down(de, dtc)
            t_dn_end[e] = P.cnt["T"]
            epr = e
            if e + 1 < NE_RUN:
                P.wait("scalar", "T", t_dn_end[e])
                P.dma("scalar", "LC", crowb[:], combT_d[e + 1:e + 2, GS * g:GS * (g + 1)])

        prev_up_end[0] = (t_up_end[NE_RUN - 2], t_up_end[NE_RUN - 1]) if NE_RUN >= 2 else None
        o_st = [0] * GT
        for j in range(GT if DEV["out"] else 0):
            bj = j % 2
            ob2 = outs if j % 2 == 0 else hbf[:, 0:1024]
            P.wait("sync", "V")
            P.dma("sync", "LX0" if bj == 0 else "LX1", stg[bj][:], ap["x"][GS * g + 128 * j:GS * g + 128 * (j + 1), :])
            P.wait("vector", "LX0" if bj == 0 else "LX1")
            if j >= 2:
                P.wait("vector", "O" if j % 2 == 0 else "O1", o_st[j - 2])
            P.do("vector", lambda E, bj=bj, j=j, ob2=ob2: E.tensor_tensor(
                out=ob2, in0=stg[bj][:], in1=acc[:, 1024 * j:1024 * (j + 1)],
                op=ALU.add), inc="V")
            P.wait("gpsimd", "V")
            o_st[j] = P.dma("gpsimd", "O" if j % 2 == 0 else "O1",
                            out_hbm[GS * g + 128 * j:GS * g + 128 * (j + 1), :], ob2)
        v_out = (P.cnt["O"], P.cnt["O1"])
        v_stg_free[0] = P.cnt["V"]

    for tname in DEV["taps"]:
        t = tens[tname]
        shp = [t.shape[0], t.shape[1]]
        tap = nc.dram_tensor("tap_" + tname, shp, t[:].dtype, kind="ExternalOutput").ap()
        P.wait("gpsimd", "V")
        P.wait("gpsimd", "SS")
        P.wait("gpsimd", "T")
        P.dma("gpsimd", "O", tap, t[:])

    P.wait("gpsimd", "O")
    P.wait("gpsimd", "O1")

    with nc.Block() as block:
        @block.sync
        def _(E):
            P.run(E, "sync", handles)

        @block.tensor
        def _(E):
            P.run(E, "tensor", handles)

        @block.scalar
        def _(E):
            P.run(E, "scalar", handles)

        @block.vector
        def _(E):
            P.run(E, "vector", handles)

        @block.gpsimd
        def _(E):
            P.run(E, "gpsimd", handles)

    es.close()
    return nc, P


def _host_prep(inputs):
    qkv_w = np.asarray(inputs["L0_attn_qkv_w"], np.float32)
    qkv_b = np.asarray(inputs["L0_attn_qkv_b"], np.float32)
    proj_w = np.asarray(inputs["L0_attn_proj_w"], np.float32)
    proj_b = np.asarray(inputs["L0_attn_proj_b"], np.float32)
    gate_w = np.asarray(inputs["L0_ffn_gate_w"], np.float32)
    e_bias = np.asarray(inputs["L0_ffn_e_bias"], np.float32)
    up_w = np.asarray(inputs["L0_ffn_up_w"], np.float32)
    down_w = np.asarray(inputs["L0_ffn_down_w"], np.float32)
    sup_w = np.asarray(inputs["L0_ffn_shared_up_w"], np.float32)
    sdown_w = np.asarray(inputs["L0_ffn_shared_down_w"], np.float32)

    d = {}
    Wq, Wk, Wv = qkv_w[0:D], qkv_w[D:2 * D], qkv_w[2 * D:3 * D]
    bq, bk, bv = qkv_b[0:D], qkv_b[D:2 * D], qkv_b[2 * D:3 * D]
    d["wq_p"] = _pack_k(np.ascontiguousarray(Wq.T))
    d["wk_p"] = _pack_k(np.ascontiguousarray(Wk.T))
    d["wv_p"] = _pack_k(np.ascontiguousarray(Wv.T))
    d["pw_p"] = _pack_k(np.ascontiguousarray(proj_w.T))
    bqk = np.zeros((128, 16), np.float32)
    for p in range(NP):
        bqk[:, 2 * p] = bq[128 * p:128 * (p + 1)]
        bqk[:, 2 * p + 1] = bk[128 * p:128 * (p + 1)]
    d["bqk"] = bqk
    d["bvp"] = np.ascontiguousarray(bv[None, :])
    d["pjb"] = np.ascontiguousarray(proj_b[None, :])
    d["gw_p"] = _pack_k(np.ascontiguousarray(gate_w.T))
    d["ebt"] = np.ascontiguousarray(np.broadcast_to(e_bias[None, :], (128, 32)).copy())
    upl, dnl = [], []
    for e in range(32):
        upl.append(_pack_k(np.ascontiguousarray(up_w[e].T)))
        dnl.append(_pack_k(np.ascontiguousarray(down_w[e].T)))
    supT = np.ascontiguousarray(sup_w.T)
    sdnT = np.ascontiguousarray(sdown_w.T)
    for c in range(4):
        upl.append(_pack_k(np.ascontiguousarray(supT[:, 512 * c:512 * (c + 1)])))
        dnl.append(_pack_k(np.ascontiguousarray(sdnT[512 * c:512 * (c + 1), :])))
    d["upw"] = np.ascontiguousarray(np.stack(upl))
    d["dnw"] = np.ascontiguousarray(np.stack(dnl))
    inv = 1.0 / (10000.0 ** (np.arange(0, DH, 2, dtype=np.float64) / DH))
    t = np.arange(S, dtype=np.float64)
    fr = np.concatenate([np.outer(t, inv), np.outer(t, inv)], axis=1)
    cosT = np.cos(fr).T.astype(np.float32)
    sinT = np.sin(fr).T.astype(np.float32)
    d["cost"] = np.ascontiguousarray(np.vstack([cosT, cosT]).astype(ml_dtypes.bfloat16))
    d["sint"] = np.ascontiguousarray(np.vstack([sinT, sinT]).astype(ml_dtypes.bfloat16))
    M64 = np.zeros((64, 64), np.float32)
    for i in range(32):
        M64[i, i + 32] = -1.0
        M64[i + 32, i] = 1.0
    MT = np.zeros((128, 128), np.float32)
    MT[0:64, 0:64] = M64.T
    MT[64:128, 64:128] = M64.T
    d["mt"] = MT
    d["idn"] = np.eye(128, dtype=np.float32)
    esh = np.zeros((64, 128), np.float32)
    for i in range(64):
        esh[i, 64 + i] = 1.0
    d["esh"] = esh
    d["msk"] = np.ascontiguousarray(
        (np.arange(128)[:, None] <= np.arange(128)[None, :]).astype(np.float32))
    return d


def kernel(**inputs):
    x = np.asarray(inputs["L0_attn_x"], np.float32)
    shared = _host_prep(inputs)
    nc, _ = build_nc()
    in_maps = []
    for c in range(B):
        m = dict(shared)
        m["x"] = np.ascontiguousarray(x[c])
        in_maps.append(m)
    res = run_bass_kernel_spmd(nc, in_maps, core_ids=list(range(B)))
    out = np.stack([np.asarray(r["out"], np.float32) for r in res.results])
    return out


# version 32 (final)
# speedup vs baseline: 1.1587x; 1.0460x over previous
import contextlib
import numpy as np
import ml_dtypes
import concourse.bass as bass
import concourse.mybir as mybir
from concourse.bass_utils import run_bass_kernel_spmd

F32 = mybir.dt.float32
BF16 = mybir.dt.bfloat16
AF = mybir.ActivationFunctionType
ALU = mybir.AluOpType
AX = mybir.AxisListType

B, S, D = 8, 4096, 1024
H, DH = 16, 64
NP = 8
NE = 36
NT = S // 128
NG = 2
GS = S // NG
GT = GS // 128
GC = GS // 512
NEG = -1e30

ENGS = ("sync", "tensor", "scalar", "vector", "gpsimd")

DEV = {"pairs": NP, "experts": NE, "groups": NG, "attn": True, "ffn": True,
       "out": True, "taps": [], "astage": 4}


class Plan:
    def __init__(self):
        self.ops = {e: [] for e in ENGS}
        self.cnt = {s: 0 for s in ("LI", "LX0", "LX1", "CS0", "CS1", "LC", "LD", "LB", "LW", "LA", "PU0", "PU1", "T", "SS", "V", "O", "PS0", "PS1", "PS2", "PS3", "PS4", "PS5", "LB1", "LB2", "RB0", "RB1", "LW1", "LD1", "PJ0", "PJ1", "O1")}
        self.lastwait = {e: {} for e in ENGS}

    def do(self, eng, fn, inc=None, amt=1):
        if inc is not None:
            self.cnt[inc] += amt
            self.ops[eng].append((fn, inc, amt))
            return self.cnt[inc]
        self.ops[eng].append((fn, None, 0))
        return None

    def dma(self, eng, sem, dst, src):
        return self.do(eng, (lambda E, d=dst, s=src: E.dma_start(out=d, in_=s)),
                       inc=sem, amt=16)

    def wait(self, eng, sem, val=None):
        v = self.cnt[sem] if val is None else val
        if v <= 0:
            return
        if self.lastwait[eng].get(sem, -1) >= v:
            return
        self.lastwait[eng][sem] = v
        self.ops[eng].append((("wait", sem, v), None, 0))

    def drain(self, eng):
        self.ops[eng].append((("drain",), None, 0))

    def run(self, E, eng, handles):
        for fn, inc, amt in self.ops[eng]:
            if isinstance(fn, tuple):
                if fn[0] == "wait":
                    E.wait_ge(handles[fn[1]], fn[2])
                else:
                    E.drain()
                continue
            r = fn(E)
            if inc is not None:
                r.then_inc(handles[inc], amt)

    def n_ops(self):
        return {e: len(v) for e, v in self.ops.items()}


def _pack_k(w):
    K, N = w.shape
    assert K % 128 == 0
    return np.ascontiguousarray(
        w.reshape(K // 128, 128, N).transpose(1, 0, 2).reshape(128, -1))


def build_nc():
    nc = bass.Bass("TRN2", target_bir_lowering=False, debug=False)
    ap = {}

    def din(name, shape, dt=F32):
        ap[name] = nc.dram_tensor(name, list(shape), dt, kind="ExternalInput").ap()

    din("x", (S, D))
    din("wq_p", (128, 8192)); din("wk_p", (128, 8192)); din("wv_p", (128, 8192))
    din("pw_p", (128, 8192))
    din("bqk", (128, 16)); din("bvp", (1, 1024)); din("pjb", (1, 1024))
    din("gw_p", (128, 256)); din("ebt", (128, 32))
    din("upw", (NE, 128, 4096)); din("dnw", (NE, 128, 4096))
    din("cost", (128, S), BF16); din("sint", (128, S), BF16)
    din("mt", (128, 128)); din("idn", (128, 128)); din("msk", (128, 128))
    din("esh", (64, 128))
    out_hbm = nc.dram_tensor("out", [S, D], F32, kind="ExternalOutput").ap()
    upw_bf = nc.dram_tensor("upw_bf", [NE, 128, 4096], BF16, kind="Internal").ap()
    dnw_bf = nc.dram_tensor("dnw_bf", [NE, 128, 4096], BF16, kind="Internal").ap()
    combT_d = nc.dram_tensor("combT_d", [36, S], BF16, kind="Internal").ap()

    es = contextlib.ExitStack()
    tens = {}

    def sb(name, shape, dt):
        t = es.enter_context(nc.sbuf_tensor(name, list(shape), dt))
        tens[name] = t
        return t

    def psm(name, shape, dt):
        return es.enter_context(nc.psum_tensor(name, list(shape), dt))

    xnT = [sb(f"xnT{c}", (128, S), BF16) for c in range(8)]
    acc = sb("acc", (128, GT * 1024), BF16)
    wsl = [sb(f"wsl{i}", (128, 4096), BF16) for i in range(4)]
    cstg = [sb(f"cstg{i}", (36, 128), BF16) for i in range(2)]
    stg = [sb(f"stg{i}", (128, 1024), F32) for i in range(2)]
    attw = sb("attw", (128, 2048), BF16)
    vbuf = sb("vbuf", (128, 4160), BF16)
    qrot = sb("qrot", (128, GS), BF16)
    krot = sb("krot", (128, S), BF16)
    qpre = [sb(f"qpre{i}", (128, 512), BF16) for i in range(2)]
    probs = [sb(f"probs{i}", (128, 1024), BF16) for i in range(4)]
    vt1 = sb("vt1", (128, 512), F32)
    vt2 = sb("vt2", (128, 512), F32)
    sqn = qpre[0][:]
    cbuf = sb("cbuf", (1, 64), F32)
    ncb = sb("ncb", (1, 4), BF16)
    negC = sb("negC", (128, 8), F32)
    stAB = sb("stAB", (1, 1024), BF16)
    stABw = stg[1][0:1, 0:1024]
    bcsb = sb("bcsb", (128, 1024), BF16)
    ctxab = sb("ctxab", (128, 512), BF16)
    tmpb = qpre[1][0:64, :]
    rb = sb("rb", (128, 2048), BF16)
    outs = rb[:].bitcast(F32)
    hb = sb("hb", (128, 4096), BF16)
    crowb = sb("crowb", (1, 2048), BF16)
    bvpb = crowb[0:1, 0:1024]
    pjbb = crowb[0:1, 1024:2048]
    comb4 = [sb(f"comb{i}", (128, 36), F32) for i in range(4)]
    rtr4 = sb("rtr4", (128, 1024), F32)
    accf = acc[:].bitcast(F32)
    xnf = [accf[:, 0:1024], accf[:, 1024:2048]]
    csc = [sb(f"csc{i}", (128, 512), BF16) for i in range(2)]
    css = [sb(f"css{i}", (128, 512), BF16) for i in range(2)]
    mskb = sb("mskb", (128, 128), BF16)
    mtb = sb("mtb", (128, 128), BF16)
    eshb = sb("eshb", (64, 128), BF16)
    idsb = sb("idsb", (128, 128), F32)
    onesb = sb("onesb", (128, 128), BF16)
    gwb = sb("gwb", (128, 256), BF16)
    ebt = sb("ebt_s", (128, 32), F32)
    bqk = sb("bqk_s", (128, 16), F32)
    bqks = sb("bqks", (128, 16), F32)
    bnst = sb("bnst", (128, 12), F32)
    mv = sb("mv", (128, 2), F32)
    sc1 = sb("sc1", (128, 1), F32)
    rstd = sb("rstd", (128, 1), F32)
    nmr = sb("nmr", (128, 1), F32)

    P0 = psm("P0", (128, 1024), F32)
    P1 = psm("P1", (128, 1024), F32)
    P2 = psm("P2", (128, 1024), F32)
    P3 = psm("P3", (128, 1024), F32)

    handles = {}
    for sname in ("LI", "LX0", "LX1", "CS0", "CS1", "LC", "LD", "LB", "LW", "LA", "PU0", "PU1", "T", "SS", "V", "O", "PS0", "PS1", "PS2", "PS3", "PS4", "PS5", "LB1", "LB2", "RB0", "RB1", "LW1", "LD1", "PJ0", "PJ1", "O1"):
        handles[sname] = es.enter_context(nc.semaphore(sname))

    P = Plan()

    P.dma("sync", "LI", idsb[:], ap["idn"])
    P.dma("sync", "LI", stg[0][:, 0:128], ap["msk"])
    P.dma("sync", "LI", stg[0][:, 128:256], ap["mt"])
    P.dma("sync", "LI", stg[0][:, 256:272], ap["bqk"])
    P.dma("sync", "LI", ebt[:], ap["ebt"])
    P.dma("sync", "LI", stg[0][:, 272:528], ap["gw_p"])
    P.dma("sync", "LI", stg[0][0:64, 528:656], ap["esh"])
    P.wait("vector", "LI")
    P.do("vector", lambda E: E.tensor_copy(out=mskb[:], in_=stg[0][:, 0:128]))
    P.do("vector", lambda E: E.tensor_copy(out=mtb[:], in_=stg[0][:, 128:256]))
    P.do("vector", lambda E: E.tensor_copy(out=bqk[:], in_=stg[0][:, 256:272]))
    P.do("vector", lambda E: E.tensor_copy(out=gwb[:], in_=stg[0][:, 272:528]))
    P.do("vector", lambda E: E.tensor_copy(out=eshb[:], in_=stg[0][0:64, 528:656]))
    P.do("vector", lambda E: E.memset(onesb[:], 1.0))
    P.do("vector", lambda E: E.memset(vbuf[:], 1.0))
    P.do("vector", lambda E: E.memset(rb[:], 0.0))
    P.drain("vector")
    P.do("vector", lambda E: E.tensor_copy(out=bqks[:], in_=bqk[:]))
    P.drain("vector")
    bqv = bqks[:].rearrange("p (a b) -> p a b", b=2)
    P.do("vector", lambda E: E.tensor_scalar_mul(
        out=bqv[:, :, 0:1], in0=bqv[:, :, 0:1], scalar1=0.125), inc="V")

    pump_state = {"k": 0, "cast": {}, "out": {}}
    hbf = hb[:].bitcast(F32)

    PUMP_TOT = NE * 2 * 4
    PUMP_SLOTS = 6
    obuf = [rb[:, 0:2048], acc[:, 12288:14336], acc[:, 14336:16384]]

    def _pump_slot(j):
        s = j % PUMP_SLOTS
        if s < 2:
            return hbf[:, 1024 * s:1024 * (s + 1)]
        return accf[:, 2048 + 1024 * (s - 2):2048 + 1024 * (s - 1)]

    def _pump_issue_in(j):
        if j >= PUMP_TOT:
            return
        mat = j // 4
        jc = j % 4
        e = mat // 2
        srcw = ap["upw"] if mat % 2 == 0 else ap["dnw"]
        if j >= PUMP_SLOTS:
            P.wait("scalar", "V", pump_state["cast"][j - PUMP_SLOTS])
        P.dma("scalar", "PS%d" % (j % PUMP_SLOTS), _pump_slot(j),
              srcw[e, :, 1024 * jc:1024 * (jc + 1)])

    def emit_pump(nchunks):
        for _ in range(nchunks):
            k = pump_state["k"]
            if k >= PUMP_TOT:
                return
            pump_state["k"] = k + 1
            mat = k // 4
            e = mat // 2
            dstw = upw_bf if mat % 2 == 0 else dnw_bf
            h = k // 2
            ob = obuf[h % 3]
            if k == 0:
                P.wait("scalar", "V", 1)
                for j in range(PUMP_SLOTS - 1):
                    _pump_issue_in(j)
            _pump_issue_in(k + PUMP_SLOTS - 1)
            P.wait("vector", "PS%d" % (k % PUMP_SLOTS))
            if h >= 3:
                P.wait("vector", *pump_state["out"][h - 3])
            pump_state["cast"][k] = P.do(
                "vector", lambda E, k=k, ob=ob: E.tensor_copy(
                    out=ob[:, 1024 * (k % 2):1024 * (k % 2) + 1024], in_=_pump_slot(k)),
                inc="V")
            if k % 2 == 1:
                P.wait("gpsimd", "V", pump_state["cast"][k])
                osem = ("LB", "LB1", "LB2")[h % 3]
                pump_state["out"][h] = (osem, P.dma(
                    "gpsimd", osem, dstw[e, :, 2048 * (h % 2):2048 * (h % 2) + 2048],
                    ob[:]))

    ss_xn = [0] * NT
    v_tr = [0] * NT
    for i in range(NT):
        b = i % 2
        if i < 2:
            P.wait("sync", "V")
        else:
            P.wait("sync", "SS", ss_xn[i - 2])
        P.dma("sync", "LX0" if b == 0 else "LX1", stg[b][:], ap["x"][128 * i:128 * (i + 1), :])
        P.wait("vector", "LX0" if b == 0 else "LX1")
        P.do("vector", lambda E, b=b: E.bn_stats(out=bnst[:, 0:6], in_=stg[b][:, 0:512]))
        P.do("vector", lambda E, b=b: E.bn_stats(out=bnst[:, 6:12], in_=stg[b][:, 512:1024]))
        P.drain("vector")
        P.do("vector", lambda E: E.bn_aggr(out=mv[:], in_=bnst[:]))
        P.drain("vector")
        P.do("vector", lambda E: E.tensor_scalar_add(out=sc1[:], in0=mv[:, 1:2], scalar1=1e-5))
        P.drain("vector")
        P.do("vector", lambda E: E.reciprocal(out=sc1[:], in_=sc1[:]), inc="V")
        P.wait("scalar", "V")
        P.do("scalar", lambda E: E.activation(out=rstd[:], in_=sc1[:], func=AF.Sqrt), inc="SS")
        P.wait("vector", "SS")
        P.do("vector", lambda E: E.tensor_scalar(out=nmr[:], in0=mv[:, 0:1],
                                                 scalar1=rstd[:], scalar2=-1.0,
                                                 op0=ALU.mult, op1=ALU.mult), inc="V")
        P.wait("scalar", "V")
        if i >= 2:
            P.wait("scalar", "T")
        xb = xnf[b]
        ss_xn[i] = P.do("scalar", lambda E, b=b, xb=xb: E.activation(
            out=xb[:], in_=stg[b][:], func=AF.Identity, bias=nmr[:], scale=rstd[:]),
            inc="SS")
        P.wait("tensor", "SS")
        v_tr4 = {}
        for c in range(8):
            reg = (P3[:, 0:128], P3[:, 512:640],
                   P0[:, 0:128], P0[:, 512:640])[c % 4]
            if c >= 4:
                P.wait("tensor", "V", v_tr4[c - 4])
            P.do("tensor", lambda E, xb=xb, c=c, reg=reg: E.transpose(
                out=reg, in_=xb[:, 128 * c:128 * (c + 1)], identity=idsb[:]), inc="T")
            P.wait("vector", "T")
            v_tr4[c] = P.do("vector", lambda E, c=c, i=i, reg=reg: E.tensor_copy(
                out=xnT[c][:, 128 * i:128 * (i + 1)], in_=reg), inc="V")
        v_tr[i] = P.cnt["V"]
        if DEV["ffn"]:
            emit_pump(9)

    comb_lb = {}
    _steps = [
        (lambda E, r: E.tensor_add(out=r[:, 32:64], in0=r[:, 0:32], in1=ebt[:]),),
        tuple(
            (lambda E, r, a=a, bb=bb, j=j: E.tensor_add(
                out=r[:, 64 + 8 * j:72 + 8 * j],
                in0=r[:, 32:64].rearrange("p (g e) -> p g e", e=4)[:, :, a:a + 1].squeeze(2),
                in1=r[:, 32:64].rearrange("p (g e) -> p g e", e=4)[:, :, bb:bb + 1].squeeze(2)))
            for j, (a, bb) in enumerate([(0, 1), (0, 2), (0, 3), (1, 2), (1, 3), (2, 3)])),
        (lambda E, r: E.tensor_max(out=r[:, 112:120], in0=r[:, 64:72], in1=r[:, 72:80]),
         lambda E, r: E.tensor_max(out=r[:, 120:128], in0=r[:, 80:88], in1=r[:, 88:96]),
         lambda E, r: E.tensor_max(out=r[:, 128:136], in0=r[:, 96:104], in1=r[:, 104:112])),
        (lambda E, r: E.tensor_max(out=r[:, 112:120], in0=r[:, 112:120], in1=r[:, 120:128]),),
        (lambda E, r: E.tensor_max(out=r[:, 112:120], in0=r[:, 112:120], in1=r[:, 128:136]),),
        (lambda E, r: E.max(out=r[:, 120:128], in_=r[:, 112:120]),),
        (lambda E, r: E.memset(r[:, 124:128], NEG),),
        (lambda E, r: E.match_replace(out=r[:, 128:136], in_to_replace=r[:, 120:128],
                                      in_values=r[:, 112:120], imm_value=NEG),),
        (lambda E, r: E.tensor_tensor(out=r[:, 120:128], in0=r[:, 128:136],
                                      in1=r[:, 112:120], op=ALU.is_lt),),
        (lambda E, r: E.tensor_tensor(
            out=r[:, 136:168].rearrange("p (g e) -> p g e", e=4),
            in0=r[:, 32:64].rearrange("p (g e) -> p g e", e=4),
            in1=r[:, 120:128].unsqueeze(2).to_broadcast([128, 8, 4]),
            op=ALU.mult),),
        (lambda E, r: E.max(out=r[:, 168:176], in_=r[:, 136:168]),),
        (lambda E, r: E.memset(r[:, 174:176], NEG),),
        (lambda E, r: E.match_replace(out=r[:, 176:208], in_to_replace=r[:, 168:176],
                                      in_values=r[:, 136:168], imm_value=NEG),),
        (lambda E, r: E.tensor_tensor(out=r[:, 176:208], in0=r[:, 176:208],
                                      in1=r[:, 136:168], op=ALU.is_lt),),
        (lambda E, r: E.tensor_mul(out=r[:, 208:240], in0=r[:, 176:208], in1=r[:, 0:32]),),
        (lambda E, r: E.reduce_sum(out=r[:, 240:241], in_=r[:, 208:240], axis=AX.X),),
        (lambda E, r: E.tensor_scalar_add(out=r[:, 240:241], in0=r[:, 240:241],
                                          scalar1=1e-20),),
        (lambda E, r: E.reciprocal(out=r[:, 241:242], in_=r[:, 240:241]),),
    ]
    for q in range(NT // 4):
        tiles = [4 * q + t for t in range(4)]
        P.wait("tensor", "V")
        P.wait("tensor", "SS")
        for t, i in enumerate(tiles):
            for c in range(8):
                P.do("tensor", lambda E, c=c, i=i, t=t: E.matmul(
                    out=P3[:, 512 + 32 * t:544 + 32 * t],
                    lhsT=xnT[c][:, 128 * i:128 * (i + 1)],
                    rhs=gwb[:, 32 * c:32 * (c + 1)],
                    start=(c == 0), stop=(c == 7)), inc=("T" if c == 7 else None))
        P.wait("scalar", "T")
        for t in range(4):
            P.do("scalar", lambda E, t=t: E.activation(
                out=rtr4[:, 256 * t:256 * t + 32], in_=P3[:, 512 + 32 * t:544 + 32 * t],
                func=AF.Sigmoid), inc="SS")
        P.wait("vector", "SS")
        for step in _steps:
            for t in range(4):
                for fn in step:
                    P.do("vector", lambda E, fn=fn, t=t: fn(
                        E, rtr4[:, 256 * t:256 * (t + 1)]))
            P.drain("vector")
        P.wait("vector", "T")
        for t in range(4):
            r4 = rtr4[:, 256 * t:256 * (t + 1)]
            P.do("vector", lambda E, cb=comb4[t], r4=r4: E.tensor_scalar(
                out=cb[:, 0:32], in0=r4[:, 208:240],
                scalar1=r4[:, 241:242], scalar2=2.5,
                op0=ALU.mult, op1=ALU.mult))
            P.do("vector", lambda E, cb=comb4[t]: E.memset(cb[:, 32:36], 1.0))
        P.drain("vector")
        P.do("vector", lambda E: E.nop(), inc="V")
        for t, i in enumerate(tiles):
            P.wait("tensor", "V")
            P.do("tensor", lambda E, cb=comb4[t]: E.transpose(
                out=P3[0:36, 384:512], in_=cb[:], identity=idsb[:]), inc="T")
            P.wait("vector", "T")
            if i >= 2:
                P.wait("vector", "RB%d" % (i % 2), comb_lb[i - 2])
            P.do("vector", lambda E, i=i: E.tensor_copy(
                out=cstg[i % 2][:], in_=P3[0:36, 384:512]), inc="V")
            P.wait("gpsimd", "V")
            comb_lb[i] = P.dma("gpsimd", "RB%d" % (i % 2),
                               combT_d[0:36, 128 * i:128 * (i + 1)], cstg[i % 2][:])

    v_out = (0, 0)
    prev_last_exp = [0]
    t_attw_free = [0]
    v_stg_free = [0]
    att_end_prev = [0]
    prev_up_end = [None]
    krotf = krot[:].bitcast(F32)
    for g in range(DEV["groups"]):
        KC = 4 * (g + 1)
        KT = 16 * (g + 1)

        P.wait("scalar", "T", att_end_prev[0])
        v_pw = {}
        for hchunk in range(8):
            sl = krotf[:, 1024 * (hchunk % 2):1024 * (hchunk % 2) + 1024]
            if hchunk >= 2:
                P.wait("scalar", "V", v_pw[hchunk - 2])
            P.dma("scalar", "PJ%d" % (hchunk % 2), sl,
                  ap["pw_p"][:, 1024 * hchunk:1024 * (hchunk + 1)])
            P.wait("vector", "PJ%d" % (hchunk % 2))
            if prev_up_end[0] is not None:
                P.wait("vector", "T", prev_up_end[0][hchunk // 4])
            dstsl = wsl[hchunk // 4]
            off = 1024 * (hchunk % 4)
            v_pw[hchunk] = P.do("vector", lambda E, s=sl, d=dstsl, off=off: E.tensor_copy(
                out=d[:, off:off + 1024], in_=s[:]), inc="V")
        if g == 1:
            P.wait("vector", "O", v_out[0])
            P.wait("vector", "O1", v_out[1])
        P.wait("sync", "T")
        P.wait("sync", "V")
        P.wait("sync", "SS")
        P.dma("sync", "LX1", stg[1][0:1, 0:1024], ap["bvp"])
        P.wait("vector", "LX1")
        P.wait("vector", "LC")
        P.wait("vector", "T")
        P.do("vector", lambda E: E.tensor_copy(out=bvpb, in_=stg[1][0:1, 0:1024]), inc="V")
        P.wait("sync", "V")
        P.dma("sync", "LX1", stg[1][0:1, 0:1024], ap["pjb"])
        P.wait("vector", "LX1")
        P.do("vector", lambda E: E.tensor_copy(out=pjbb, in_=stg[1][0:1, 0:1024]), inc="V")
        v_stg_free[0] = P.cnt["V"]

        for p in range(DEV["pairs"] if DEV["attn"] else 0):
            P.wait("scalar", "T", t_attw_free[0])
            P.wait("scalar", "V", v_stg_free[0])
            for dc in range(8):
                P.dma("scalar", "LA", stg[0][:, 128 * dc:128 * (dc + 1)],
                      ap["wq_p"][:, 1024 * dc + 128 * p:1024 * dc + 128 * (p + 1)])
            P.wait("vector", "LA")
            P.wait("vector", "T", t_attw_free[0])
            v_attw = P.do("vector", lambda E: E.tensor_copy(out=attw[:, 0:1024], in_=stg[0][:]),
                          inc="V")
            P.wait("scalar", "V", v_attw)
            for dc in range(8):
                P.dma("scalar", "LA", stg[0][:, 128 * dc:128 * (dc + 1)],
                      ap["wk_p"][:, 1024 * dc + 128 * p:1024 * dc + 128 * (p + 1)])
            P.wait("vector", "LA")
            v_attw_k = P.do("vector", lambda E: E.tensor_copy(out=attw[:, 1024:2048], in_=stg[0][:]),
                            inc="V")

            ss_pre = {}
            ropev = []
            reg_of = {}
            ckc = [0]
            t_q_end = 0
            for kind in (("q",) if DEV.get("skipk") else ("q", "k")):
                if kind == "k":
                    t_q_end = P.cnt["T"]
                    P.wait("scalar", "V", v_attw_k)
                    for dc in range(8):
                        P.dma("scalar", "LA", stg[0][:, 128 * dc:128 * (dc + 1)],
                              ap["wv_p"][:, 1024 * dc + 128 * p:1024 * dc + 128 * (p + 1)])
                    P.wait("vector", "LA")
                    P.wait("vector", "T", t_q_end)
                    v_attw = P.do("vector", lambda E: E.tensor_copy(out=attw[:, 0:1024],
                                                                    in_=stg[0][:]), inc="V")
                half = 0 if kind == "q" else 1024
                nch = GC if kind == "q" else KC

                def emit_qk_mms(ci, kind=kind, p=p, half=half):
                    c = (GC * g + ci) if kind == "q" else ci
                    b = ci % 2
                    rot = ckc[0] % 3
                    ckc[0] += 1
                    reg_of[(kind, ci)] = rot
                    P.wait("sync", "V")
                    P.dma("sync", "CS0" if b == 0 else "CS1", csc[b][:], ap["cost"][:, 512 * c:512 * (c + 1)])
                    P.dma("sync", "CS0" if b == 0 else "CS1", css[b][:], ap["sint"][:, 512 * c:512 * (c + 1)])
                    qkreg = (P3, P0, P1)[rot][:, 0:512]
                    wv_need = v_attw if kind == "q" else v_attw_k
                    if len(ropev) >= 3:
                        P.wait("tensor", "V", max(wv_need, ropev[-3]))
                    else:
                        P.wait("tensor", "V", wv_need)
                    if (kind, ci - 3) in ss_pre:
                        P.wait("tensor", "SS", ss_pre[(kind, ci - 3)])
                    elif kind == "k" and ("q", GC - 3 + ci) in ss_pre:
                        P.wait("tensor", "SS", ss_pre[("q", GC - 3 + ci)])
                    for dc in range(8):
                        P.do("tensor", lambda E, dc=dc, c=c, qkreg=qkreg, half=half: E.matmul(
                            out=qkreg, lhsT=attw[:, half + 128 * dc:half + 128 * (dc + 1)],
                            rhs=xnT[dc][:, 512 * c:512 * (c + 1)],
                            start=(dc == 0), stop=(dc == 7)),
                            inc=("T" if dc == 7 else None))
                    return P.cnt["T"]

                def emit_qk_tail(ci, kind=kind, p=p):
                    b = ci % 2
                    rot = reg_of[(kind, ci)]
                    qkreg = (P3, P0, P1)[rot][:, 0:512]
                    rreg = (P3, P0, P1)[rot][:, 512:1024]
                    P.wait("scalar", "T", qk_t[(kind, ci)])
                    if len(ropev) >= 2:
                        P.wait("scalar", "V", ropev[-2])
                    bias = bqks[:, 2 * p:2 * p + 1] if kind == "q" else bqk[:, 2 * p + 1:2 * p + 2]
                    scale = 0.125 if kind == "q" else 1.0
                    ss_pre[(kind, ci)] = P.do(
                        "scalar", lambda E, b=b, bias=bias, scale=scale, qkreg=qkreg:
                        E.activation(out=qpre[b][:], in_=qkreg, func=AF.Identity,
                                     bias=bias, scale=scale), inc="SS")
                    P.wait("tensor", "SS", ss_pre[(kind, ci)])
                    if len(ropev) >= 2:
                        P.wait("tensor", "V", ropev[-2])
                    P.do("tensor", lambda E, b=b, rreg=rreg: E.matmul(
                        out=rreg, lhsT=mtb[:], rhs=qpre[b][:], start=True, stop=True),
                        inc="T")
                    dst = qrot if kind == "q" else krot
                    dof = 512 * ci if kind == "q" else 512 * ci
                    P.wait("vector", "T")
                    P.wait("vector", "CS0" if b == 0 else "CS1")
                    P.drain("vector")
                    P.do("vector", lambda E, b=b: E.tensor_mul(
                        out=vt1[:], in0=qpre[b][:], in1=csc[b][:]))
                    P.do("vector", lambda E, b=b, rreg=rreg: E.tensor_tensor(
                        out=vt2[:], in0=rreg, in1=css[b][:], op=ALU.mult))
                    P.drain("vector")
                    ropev.append(P.do("vector", lambda E, dst=dst, dof=dof: E.tensor_add(
                        out=dst[:, dof:dof + 512], in0=vt1[:], in1=vt2[:]), inc="V"))

                qk_t = getattr(P, "_qk_t", None)
                if qk_t is None:
                    qk_t = {}
                    P._qk_t = qk_t
                for ci in range(nch + 1):
                    if ci < nch:
                        qk_t[(kind, ci)] = emit_qk_mms(ci)
                    if ci - 1 >= 0:
                        emit_qk_tail(ci - 1)
            if DEV["astage"] < 2:
                continue
            P.wait("tensor", "V")
            v_ev = [0] * KT
            for tt in range(KT):
                if g == 0 and DEV["ffn"]:
                    emit_pump(6)
                vreg = (P2[:, 0:128], P2[:, 640:768],
                        P0[:, 0:128], P0[:, 640:768])[tt % 4]
                if tt >= 4:
                    P.wait("tensor", "V", v_ev[tt - 4])
                for dc in range(8):
                    P.do("tensor", lambda E, dc=dc, tt=tt, vreg=vreg: E.matmul(
                        out=vreg, lhsT=xnT[dc][:, 128 * tt:128 * (tt + 1)],
                        rhs=attw[:, 128 * dc:128 * (dc + 1)],
                        start=(dc == 0), stop=False))
                P.do("tensor", lambda E, p=p, vreg=vreg: E.matmul(
                    out=vreg, lhsT=onesb[0:1, 0:128],
                    rhs=bvpb[:, 128 * p:128 * (p + 1)],
                    start=False, stop=True), inc="T")
                P.wait("vector", "T")
                P.do("vector", lambda E, tt=tt, vreg=vreg: E.tensor_copy(
                    out=vbuf[:, 130 * tt:130 * tt + 64], in_=vreg[:, 0:64]))
                v_ev[tt] = P.do("vector", lambda E, tt=tt, vreg=vreg: E.tensor_copy(
                    out=vbuf[:, 130 * tt + 65:130 * tt + 129], in_=vreg[:, 64:128]), inc="V")
            t_attw_free[0] = P.cnt["T"]
            v_stg_free[0] = v_attw

            if DEV["astage"] < 3:
                continue
            if True:
              for kind, nch, buf in (("q", GC, qrot), ("k", KC, krot)):
                nmm_t = {}
                sq_v = {}
                red_v = {}
                for ci in range(nch + 2):
                    if ci < nch:
                        sqb = qpre[ci % 2]
                        P.wait("vector", "T", nmm_t.get(ci - 2, 0))
                        sq_v[ci] = P.do("vector", lambda E, buf=buf, ci=ci, sqb=sqb: E.tensor_mul(
                            out=sqb[:], in0=buf[:, 512 * ci:512 * (ci + 1)],
                            in1=buf[:, 512 * ci:512 * (ci + 1)]), inc="V")
                    if 0 <= ci - 1 < nch:
                        c2 = ci - 1
                        sqb = qpre[c2 % 2]
                        par = 512 * (c2 % 2)
                        P.wait("tensor", "V", sq_v[c2])
                        P.wait("tensor", "V", red_v.get(c2 - 2, 0))
                        P.do("tensor", lambda E, sqb=sqb, par=par: E.matmul(
                            out=P2[0:1, par:par + 512], lhsT=onesb[0:64, 0:1], rhs=sqb[0:64, :],
                            start=True, stop=True, tile_position=(0, 0)))
                        nmm_t[c2] = P.do("tensor", lambda E, sqb=sqb, par=par: E.matmul(
                            out=P1[0:1, par:par + 512], lhsT=onesb[64:128, 0:1], rhs=sqb[64:128, :],
                            start=True, stop=True, tile_position=(64, 0)), inc="T")
                    if 0 <= ci - 2 < nch:
                        c2 = ci - 2
                        par = 512 * (c2 % 2)
                        P.wait("vector", "T", nmm_t[c2])
                        ofA = (0 if kind == "q" else 16) + c2
                        ofB = (8 if kind == "q" else 32) + c2
                        P.do("vector", lambda E, ofA=ofA, par=par: E.reduce_max(
                            out=cbuf[0:1, ofA:ofA + 1], in_=P2[0:1, par:par + 512], axis=AX.X))
                        red_v[c2] = P.do("vector", lambda E, ofB=ofB, par=par: E.reduce_max(
                            out=cbuf[0:1, ofB:ofB + 1], in_=P1[0:1, par:par + 512], axis=AX.X), inc="V")
            P.drain("vector")
            P.do("vector", lambda E: E.reduce_max(out=cbuf[0:1, 40:41], in_=cbuf[0:1, 0:GC], axis=AX.X))
            P.do("vector", lambda E: E.reduce_max(out=cbuf[0:1, 41:42], in_=cbuf[0:1, 8:8 + GC], axis=AX.X))
            P.do("vector", lambda E, KC=KC: E.reduce_max(out=cbuf[0:1, 42:43], in_=cbuf[0:1, 16:16 + KC], axis=AX.X))
            P.do("vector", lambda E, KC=KC: E.reduce_max(out=cbuf[0:1, 43:44], in_=cbuf[0:1, 32:32 + KC], axis=AX.X))
            P.drain("vector")
            P.do("vector", lambda E: E.tensor_max(out=cbuf[0:1, 40:41], in0=cbuf[0:1, 40:41], in1=cbuf[0:1, 41:42]))
            P.do("vector", lambda E: E.tensor_max(out=cbuf[0:1, 42:43], in0=cbuf[0:1, 42:43], in1=cbuf[0:1, 43:44]))
            P.drain("vector")
            P.do("vector", lambda E: E.tensor_add(out=cbuf[0:1, 44:45], in0=cbuf[0:1, 40:41], in1=cbuf[0:1, 42:43]))
            P.drain("vector")
            P.wait("vector", "SS", prev_last_exp[0])
            P.do("vector", lambda E: E.tensor_scalar_mul(out=ncb[0:1, 0:1], in0=cbuf[0:1, 44:45], scalar1=-1.0 / 16.0), inc="V")
            P.wait("tensor", "V")
            P.do("tensor", lambda E: E.matmul(out=P3[:, 0:1], lhsT=onesb[0:1, 0:128],
                                              rhs=ncb[0:1, 0:1], start=True, stop=True), inc="T")
            P.wait("vector", "T")
            P.do("vector", lambda E: E.tensor_copy(out=negC[:, 0:1], in_=P3[:, 0:1]), inc="V")
            v_negc = P.cnt["V"]
            P.wait("scalar", "V")

            if DEV["astage"] < 4:
                continue
            exps_all = [0, 0, 0]
            ctx_all = [0, 0]
            v_probs = {}
            sc_t = {}
            ctx_of = {}
            gof = [0]
            prev_proj_v = [0]
            for sq in range(4):
                sjg = 4 * g + sq
                ntc = 4 * sjg + 4
                gb = gof[0]
                t_ctx_last = 0
                diag_of = {}

                def emit_sc(tc, sq=sq, gb=gb):
                    PS = (P0, P1, P3)[tc % 3]
                    if PS is P3:
                        P.wait("tensor", "V", v_negc)
                    P.wait("tensor", "SS", exps_all[-2])
                    P.do("tensor", lambda E, tc=tc, sq=sq, PS=PS: E.matmul(
                        out=PS[:, 0:512], lhsT=krot[0:64, 128 * tc:128 * (tc + 1)],
                        rhs=qrot[0:64, 512 * sq:512 * (sq + 1)],
                        start=True, stop=True, tile_position=(0, 0)))
                    sc_t[gb + tc] = P.do("tensor", lambda E, tc=tc, sq=sq, PS=PS: E.matmul(
                        out=PS[:, 512:1024], lhsT=krot[64:128, 128 * tc:128 * (tc + 1)],
                        rhs=qrot[64:128, 512 * sq:512 * (sq + 1)],
                        start=True, stop=True, tile_position=(64, 0)), inc="T")

                def emit_exp(tc, gb=gb):
                    PS = (P0, P1, P3)[tc % 3]
                    gt = gb + tc
                    pr = probs[gt % 4]
                    P.wait("scalar", "T", sc_t[gt])
                    P.wait("scalar", "T", ctx_of.get(gt - 4, 0))
                    if (gt - 4) in v_probs:
                        P.wait("scalar", "V", v_probs[gt - 4])
                    ev = P.do("scalar", lambda E, PS=PS, pr=pr: E.activation(
                        out=pr[:], in_=PS[:], func=AF.Exp, bias=negC[:, 0:1]), inc="SS")
                    exps_all.append(ev)
                    r = tc - 4 * sjg
                    if r >= 0:
                        diag_of[tc] = r
                        P.wait("vector", "SS", ev)
                        for half in (0, 512):
                            P.do("vector", lambda E, pr=pr, r=r, half=half: E.tensor_tensor(
                                out=pr[:, half + 128 * r:half + 128 * (r + 1)],
                                in0=pr[:, half + 128 * r:half + 128 * (r + 1)], in1=mskb[:],
                                op=ALU.mult))
                            if r > 0:
                                P.do("vector", lambda E, pr=pr, r=r, half=half: E.memset(
                                    pr[:, half:half + 128 * r], 0.0))
                        P.drain("vector")
                        P.do("vector", lambda E: E.nop(), inc="V")
                        v_probs[gt] = P.cnt["V"]

                exp_of = {}
                for i in range(ntc + 2):
                    if i < ntc:
                        emit_sc(i)
                    if 0 <= i - 1 < ntc:
                        emit_exp(i - 1)
                        exp_of[i - 1] = exps_all[-1]
                    if 0 <= i - 2 < ntc:
                        tcx = i - 2
                        gt = gb + tcx
                        pr = probs[gt % 4]
                        if tcx == 0:
                            P.wait("tensor", "V", prev_proj_v[0])
                        if tcx in diag_of:
                            P.wait("tensor", "V", v_probs[gt])
                        else:
                            P.wait("tensor", "SS", exp_of[tcx])
                        st = (tcx == 0)
                        sp = (tcx == ntc - 1)
                        P.do("tensor", lambda E, tc=tcx, pr=pr, st=st, sp=sp: E.matmul(
                            out=P2[0:65, 0:512], lhsT=vbuf[:, 130 * tc:130 * tc + 65],
                            rhs=pr[:, 0:512], start=st, stop=sp))
                        t_ctx_last = P.do("tensor", lambda E, tc=tcx, pr=pr, st=st, sp=sp: E.matmul(
                            out=P2[0:65, 512:1024], lhsT=vbuf[:, 130 * tc + 65:130 * tc + 130],
                            rhs=pr[:, 512:1024], start=st, stop=sp), inc="T")
                        ctx_all.append(t_ctx_last)
                        ctx_of[gt] = t_ctx_last
                P.wait("vector", "T", t_ctx_last)
                P.do("vector", lambda E: E.tensor_copy(out=tmpb, in_=P2[0:64, 512:1024]),
                     inc="V")
                P.wait("scalar", "T", t_ctx_last)
                P.do("scalar", lambda E: E.activation(out=stABw, in_=P2[64:65, 0:1024],
                                                      func=AF.Ln))
                P.drain("scalar")
                P.do("scalar", lambda E: E.activation(out=stAB[:, 0:1024], in_=stABw,
                                                      func=AF.Exp, scale=-1.0), inc="SS")
                P.wait("tensor", "V")
                P.wait("tensor", "SS")
                P.do("tensor", lambda E: E.matmul(out=P3[:, 0:512], lhsT=onesb[0:1, 0:128],
                                                  rhs=stAB[0:1, 0:512], start=True, stop=True))
                P.do("tensor", lambda E: E.matmul(out=P3[:, 512:1024], lhsT=onesb[0:1, 0:128],
                                                  rhs=stAB[0:1, 512:1024], start=True, stop=True))
                P.do("tensor", lambda E: E.matmul(out=P2[:, 512:1024], lhsT=eshb[:],
                                                  rhs=tmpb, start=True, stop=True), inc="T")
                P.wait("vector", "T")
                P.do("vector", lambda E: E.tensor_copy(out=bcsb[:], in_=P3[:, 0:1024]))
                P.drain("vector")
                P.do("vector", lambda E: E.tensor_mul(out=ctxab[0:64, :], in0=P2[0:64, 0:512],
                                                      in1=bcsb[0:64, 0:512]))
                v_ctx_scale_prev = P.do("vector", lambda E: E.tensor_mul(
                    out=ctxab[64:128, :], in0=P2[64:128, 512:1024],
                    in1=bcsb[64:128, 512:1024]), inc="V")
                P.wait("tensor", "V", v_ctx_scale_prev)
                for msub in range(4):
                    preg = P3 if msub % 2 == 0 else P2
                    jloc = 4 * sq + msub
                    if msub >= 2:
                        P.wait("tensor", "V")
                    lastm = None
                    for n in range(2):
                        last = (n == 1 and p > 0)
                        P.do("tensor", lambda E, p=p, n=n, msub=msub, preg=preg: E.matmul(
                            out=preg[:, 512 * n:512 * (n + 1)],
                            lhsT=ctxab[:, 128 * msub:128 * (msub + 1)],
                            rhs=wsl[p // 4][:, 1024 * (p % 4) + 512 * n:1024 * (p % 4) + 512 * (n + 1)],
                            start=True, stop=(p > 0)), inc=("T" if last else None))
                        if p == 0:
                            P.do("tensor", lambda E, n=n, preg=preg: E.matmul(
                                out=preg[:, 512 * n:512 * (n + 1)], lhsT=onesb[0:1, 0:128],
                                rhs=pjbb[:, 512 * n:512 * (n + 1)],
                                start=False, stop=True), inc=("T" if n == 1 else None))
                    P.wait("vector", "T")
                    if p == 0:
                        P.do("vector", lambda E, jloc=jloc, preg=preg: E.tensor_copy(
                            out=acc[:, 1024 * jloc:1024 * (jloc + 1)], in_=preg[:]), inc="V")
                    else:
                        P.do("vector", lambda E, jloc=jloc, preg=preg: E.tensor_tensor(
                            out=acc[:, 1024 * jloc:1024 * (jloc + 1)],
                            in0=acc[:, 1024 * jloc:1024 * (jloc + 1)], in1=preg[:],
                            op=ALU.add), inc="V")
                    prev_proj_v[0] = P.cnt["V"]
                gof[0] += ntc
            prev_last_exp[0] = exps_all[-1]
        def load_expert(e, which, tfree):
            sl = wsl[(e % 2) if which == "up" else 2 + (e % 2)]
            srcw = upw_bf if which == "up" else dnw_bf
            if which == "up":
                eng, sem = "sync", ("LW" if e % 2 == 0 else "LW1")
            else:
                eng, sem = "gpsimd", ("LD" if e % 2 == 0 else "LD1")
            P.wait(eng, "T", tfree)
            need_h = 4 * e + (1 if which == "up" else 3)
            P.wait(eng, *pump_state["out"][need_h])
            cnt = P.dma(eng, sem, sl[:], srcw[e])
            return [(sem, cnt)]

        t_att_end = P.cnt["T"]
        att_end_prev[0] = t_att_end
        if g == 0 and DEV["ffn"]:
            emit_pump(NE * 2 * 8)
        P.wait("scalar", "O")
        P.wait("scalar", "O1")
        P.wait("scalar", "LB")
        P.wait("scalar", "LB1")
        P.wait("scalar", "LB2")
        P.wait("scalar", "RB0")
        P.wait("scalar", "RB1")
        if not DEV["attn"] or DEV["pairs"] == 0:
            P.do("vector", lambda E: E.memset(acc[:], 0.0), inc="V")
        NE_RUN = DEV["experts"] if DEV["ffn"] else 0
        up_v = load_expert(0, "up", t_att_end) if NE_RUN else None
        dn_v = load_expert(0, "dn", t_att_end) if NE_RUN else None
        dn_waits = {0: dn_v}
        up_waits = {0: up_v}
        if NE_RUN:
            P.wait("scalar", "T")
            P.wait("scalar", "V")
            P.dma("scalar", "LC", crowb[:], combT_d[0:1, GS * g:GS * (g + 1)])

        t_up_end = [0] * NE
        t_dn_end = [0] * NE
        down_t_done = {}
        v_hb = {}
        v_dn = {}
        ss_sq = {}
        dslots = [P3[:, 0:512], P3[:, 512:1024], P2[:, 512:1024]]
        dslot_last = {}
        epr = None
        pend_down = []
        pend_scale = []
        v_bcc = {}
        for e in range(NE_RUN):
            def emit_down(e, tc):
                for sm, vv in (dn_waits.get(e) or []):
                    P.wait("tensor", sm, vv)
                P.wait("tensor", "V", v_hb[(e, tc, 3)])
                for tt in range(4):
                    jloc = 4 * tc + tt
                    for n in range(2):
                        slot = dslots[(2 * (4 * tc + tt) + n) % 3]
                        key_prev = dslot_last.get((2 * (4 * tc + tt) + n) % 3)
                        if key_prev is not None:
                            P.wait("tensor", "V", key_prev)
                        ho = 2048 * (tc % 2)
                        for i in range(4):
                            P.do("tensor", lambda E, i=i, n=n, tt=tt, e=e, slot=slot, ho=ho: E.matmul(
                                out=slot,
                                lhsT=hb[:, ho + 512 * i + 128 * tt:ho + 512 * i + 128 * (tt + 1)],
                                rhs=wsl[2 + e % 2][:, 1024 * i + 512 * n:1024 * i + 512 * (n + 1)],
                                start=(i == 0), stop=(i == 3)),
                                inc=("T" if i == 3 else None))
                        P.wait("vector", "T")
                        vv = P.do("vector", lambda E, jloc=jloc, n=n, slot=slot: E.tensor_tensor(
                            out=acc[:, 1024 * jloc + 512 * n:1024 * jloc + 512 * (n + 1)],
                            in0=acc[:, 1024 * jloc + 512 * n:1024 * jloc + 512 * (n + 1)],
                            in1=slot, op=ALU.add), inc="V")
                        dslot_last[(2 * (4 * tc + tt) + n) % 3] = vv
                        v_dn[(e, tc, tt, n)] = vv
                v_dn[(e, tc)] = v_dn[(e, tc, 3, 1)]
                down_t_done[e] = P.cnt["T"]

            for tc in range(GC):
                TC = GS * g + 512 * tc
                pend_down.append((e, tc))
                t_dn_guard = 0
                if len(pend_down) >= 3:
                    de, dtc = pend_down.pop(0)
                    emit_down(de, dtc)
                    t_dn_guard = P.cnt["T"]
                if pend_scale:
                    se, stc = pend_scale.pop(0)
                    sho = 2048 * (stc % 2)
                    P.wait("vector", "SS", ss_sq[(se, stc, 1)])
                    for i in range(4):
                        v_hb[(se, stc, i)] = P.do(
                            "vector", lambda E, i=i, sho=sho, stc=stc: E.tensor_tensor(
                                out=hb[:, sho + 512 * i:sho + 512 * (i + 1)],
                                in0=hb[:, sho + 512 * i:sho + 512 * (i + 1)],
                                in1=bcsb[:, 512 * (stc % 2):512 * (stc % 2) + 512],
                                op=ALU.mult), inc="V")
                t_up_i = {}
                for i in (2, 3, 0, 1):
                    reg = (P1 if i >= 2 else P0)[:, 512 * (i % 2):512 * (i % 2) + 512]
                    if (e, tc - 1, i) in ss_sq:
                        P.wait("tensor", "SS", ss_sq[(e, tc - 1, i)])
                    elif tc == 0 and (epr, GC - 1, i) in ss_sq:
                        P.wait("tensor", "SS", ss_sq[(epr, GC - 1, i)])
                    if tc == 0:
                        for sm, vv in (up_waits.get(e) or []):
                            P.wait("tensor", sm, vv)
                    for dc in range(8):
                        P.do("tensor", lambda E, dc=dc, i=i, TC=TC, reg=reg, e=e: E.matmul(
                            out=reg, lhsT=wsl[e % 2][:, 512 * dc + 128 * i:512 * dc + 128 * (i + 1)],
                            rhs=xnT[dc][:, TC:TC + 512],
                            start=(dc == 0), stop=(dc == 7)),
                            inc=("T" if dc == 7 else None))
                    t_up_i[i] = P.cnt["T"]
                t_up_this = P.cnt["T"]
                if tc == GC - 1:
                    t_up_end[e] = t_up_this
                if (e, tc - 1) in v_bcc:
                    P.wait("tensor", "V", v_bcc[(e, tc - 1)])
                elif tc == 0 and (epr, GC - 1) in v_bcc:
                    P.wait("tensor", "V", v_bcc[(epr, GC - 1)])
                P.wait("tensor", "LC")
                P.do("tensor", lambda E, tc=tc: E.matmul(
                    out=P2[:, 0:512], lhsT=onesb[0:1, 0:128],
                    rhs=crowb[0:1, 512 * tc:512 * (tc + 1)],
                    start=True, stop=True), inc="T")
                t_bc = P.cnt["T"]
                if tc == 0 and e + 1 < NE_RUN:
                    up_waits[e + 1] = load_expert(e + 1, "up", P.cnt["T"])
                if tc == 2 and e + 1 < NE_RUN:
                    dn_waits[e + 1] = load_expert(
                        e + 1, "dn", down_t_done.get(e - 1, 0) if e >= 1 else P.cnt["T"])
                P.wait("scalar", "T", t_dn_guard)
                if (e, tc - 2, 3) in v_hb:
                    P.wait("scalar", "V", v_hb[(e, tc - 2, 3)])
                elif tc < 2 and epr is not None and (epr, GC - 2 + tc, 3) in v_hb:
                    P.wait("scalar", "V", v_hb[(epr, GC - 2 + tc, 3)])
                ho = 2048 * (tc % 2)
                for ia, ib in ((2, 3), (0, 1)):
                    P.wait("scalar", "T", t_up_i[ib])
                    for i in (ia, ib):
                        reg = (P1 if i >= 2 else P0)[:, 512 * (i % 2):512 * (i % 2) + 512]
                        P.do("scalar", lambda E, i=i, reg=reg: E.activation(
                            out=rb[:, 512 * i:512 * (i + 1)], in_=reg, func=AF.Relu))
                    P.drain("scalar")
                    for i in (ia, ib):
                        ss_sq[(e, tc, i)] = P.do("scalar", lambda E, i=i, ho=ho: E.activation(
                            out=hb[:, ho + 512 * i:ho + 512 * (i + 1)], in_=rb[:, 512 * i:512 * (i + 1)],
                            func=AF.Square), inc="SS")
                P.wait("vector", "T", t_bc)
                v_bcc[(e, tc)] = P.do("vector", lambda E, tc=tc: E.tensor_copy(
                    out=bcsb[:, 512 * (tc % 2):512 * (tc % 2) + 512], in_=P2[:, 0:512]),
                    inc="V")
                pend_scale.append((e, tc))
            if e == NE_RUN - 1:
                while pend_scale:
                    se, stc = pend_scale.pop(0)
                    sho = 2048 * (stc % 2)
                    P.wait("vector", "SS", ss_sq[(se, stc, 1)])
                    for i in range(4):
                        v_hb[(se, stc, i)] = P.do(
                            "vector", lambda E, i=i, sho=sho, stc=stc: E.tensor_tensor(
                                out=hb[:, sho + 512 * i:sho + 512 * (i + 1)],
                                in0=hb[:, sho + 512 * i:sho + 512 * (i + 1)],
                                in1=bcsb[:, 512 * (stc % 2):512 * (stc % 2) + 512],
                                op=ALU.mult), inc="V")
                while pend_down:
                    de, dtc = pend_down.pop(0)
                    emit_down(de, dtc)
            t_dn_end[e] = P.cnt["T"]
            epr = e
            if e + 1 < NE_RUN:
                P.wait("scalar", "T", t_dn_end[e])
                P.dma("scalar", "LC", crowb[:], combT_d[e + 1:e + 2, GS * g:GS * (g + 1)])

        prev_up_end[0] = (t_up_end[NE_RUN - 2], t_up_end[NE_RUN - 1]) if NE_RUN >= 2 else None
        o_st = [0] * GT
        for j in range(GT if DEV["out"] else 0):
            bj = j % 2
            ob2 = outs if j % 2 == 0 else hbf[:, 0:1024]
            P.wait("sync", "V")
            P.dma("sync", "LX0" if bj == 0 else "LX1", stg[bj][:], ap["x"][GS * g + 128 * j:GS * g + 128 * (j + 1), :])
            P.wait("vector", "LX0" if bj == 0 else "LX1")
            if j >= 2:
                P.wait("vector", "O" if j % 2 == 0 else "O1", o_st[j - 2])
            P.do("vector", lambda E, bj=bj, j=j, ob2=ob2: E.tensor_tensor(
                out=ob2, in0=stg[bj][:], in1=acc[:, 1024 * j:1024 * (j + 1)],
                op=ALU.add), inc="V")
            P.wait("gpsimd", "V")
            o_st[j] = P.dma("gpsimd", "O" if j % 2 == 0 else "O1",
                            out_hbm[GS * g + 128 * j:GS * g + 128 * (j + 1), :], ob2)
        v_out = (P.cnt["O"], P.cnt["O1"])
        v_stg_free[0] = P.cnt["V"]

    for tname in DEV["taps"]:
        t = tens[tname]
        shp = [t.shape[0], t.shape[1]]
        tap = nc.dram_tensor("tap_" + tname, shp, t[:].dtype, kind="ExternalOutput").ap()
        P.wait("gpsimd", "V")
        P.wait("gpsimd", "SS")
        P.wait("gpsimd", "T")
        P.dma("gpsimd", "O", tap, t[:])

    P.wait("gpsimd", "O")
    P.wait("gpsimd", "O1")

    with nc.Block() as block:
        @block.sync
        def _(E):
            P.run(E, "sync", handles)

        @block.tensor
        def _(E):
            P.run(E, "tensor", handles)

        @block.scalar
        def _(E):
            P.run(E, "scalar", handles)

        @block.vector
        def _(E):
            P.run(E, "vector", handles)

        @block.gpsimd
        def _(E):
            P.run(E, "gpsimd", handles)

    es.close()
    return nc, P


def _host_prep(inputs):
    qkv_w = np.asarray(inputs["L0_attn_qkv_w"], np.float32)
    qkv_b = np.asarray(inputs["L0_attn_qkv_b"], np.float32)
    proj_w = np.asarray(inputs["L0_attn_proj_w"], np.float32)
    proj_b = np.asarray(inputs["L0_attn_proj_b"], np.float32)
    gate_w = np.asarray(inputs["L0_ffn_gate_w"], np.float32)
    e_bias = np.asarray(inputs["L0_ffn_e_bias"], np.float32)
    up_w = np.asarray(inputs["L0_ffn_up_w"], np.float32)
    down_w = np.asarray(inputs["L0_ffn_down_w"], np.float32)
    sup_w = np.asarray(inputs["L0_ffn_shared_up_w"], np.float32)
    sdown_w = np.asarray(inputs["L0_ffn_shared_down_w"], np.float32)

    d = {}
    Wq, Wk, Wv = qkv_w[0:D], qkv_w[D:2 * D], qkv_w[2 * D:3 * D]
    bq, bk, bv = qkv_b[0:D], qkv_b[D:2 * D], qkv_b[2 * D:3 * D]
    d["wq_p"] = _pack_k(np.ascontiguousarray(Wq.T))
    d["wk_p"] = _pack_k(np.ascontiguousarray(Wk.T))
    d["wv_p"] = _pack_k(np.ascontiguousarray(Wv.T))
    d["pw_p"] = _pack_k(np.ascontiguousarray(proj_w.T))
    bqk = np.zeros((128, 16), np.float32)
    for p in range(NP):
        bqk[:, 2 * p] = bq[128 * p:128 * (p + 1)]
        bqk[:, 2 * p + 1] = bk[128 * p:128 * (p + 1)]
    d["bqk"] = bqk
    d["bvp"] = np.ascontiguousarray(bv[None, :])
    d["pjb"] = np.ascontiguousarray(proj_b[None, :])
    d["gw_p"] = _pack_k(np.ascontiguousarray(gate_w.T))
    d["ebt"] = np.ascontiguousarray(np.broadcast_to(e_bias[None, :], (128, 32)).copy())
    upl, dnl = [], []
    for e in range(32):
        upl.append(_pack_k(np.ascontiguousarray(up_w[e].T)))
        dnl.append(_pack_k(np.ascontiguousarray(down_w[e].T)))
    supT = np.ascontiguousarray(sup_w.T)
    sdnT = np.ascontiguousarray(sdown_w.T)
    for c in range(4):
        upl.append(_pack_k(np.ascontiguousarray(supT[:, 512 * c:512 * (c + 1)])))
        dnl.append(_pack_k(np.ascontiguousarray(sdnT[512 * c:512 * (c + 1), :])))
    d["upw"] = np.ascontiguousarray(np.stack(upl))
    d["dnw"] = np.ascontiguousarray(np.stack(dnl))
    inv = 1.0 / (10000.0 ** (np.arange(0, DH, 2, dtype=np.float64) / DH))
    t = np.arange(S, dtype=np.float64)
    fr = np.concatenate([np.outer(t, inv), np.outer(t, inv)], axis=1)
    cosT = np.cos(fr).T.astype(np.float32)
    sinT = np.sin(fr).T.astype(np.float32)
    d["cost"] = np.ascontiguousarray(np.vstack([cosT, cosT]).astype(ml_dtypes.bfloat16))
    d["sint"] = np.ascontiguousarray(np.vstack([sinT, sinT]).astype(ml_dtypes.bfloat16))
    M64 = np.zeros((64, 64), np.float32)
    for i in range(32):
        M64[i, i + 32] = -1.0
        M64[i + 32, i] = 1.0
    MT = np.zeros((128, 128), np.float32)
    MT[0:64, 0:64] = M64.T
    MT[64:128, 64:128] = M64.T
    d["mt"] = MT
    d["idn"] = np.eye(128, dtype=np.float32)
    esh = np.zeros((64, 128), np.float32)
    for i in range(64):
        esh[i, 64 + i] = 1.0
    d["esh"] = esh
    d["msk"] = np.ascontiguousarray(
        (np.arange(128)[:, None] <= np.arange(128)[None, :]).astype(np.float32))
    return d


def kernel(**inputs):
    x = np.asarray(inputs["L0_attn_x"], np.float32)
    shared = _host_prep(inputs)
    nc, _ = build_nc()
    in_maps = []
    for c in range(B):
        m = dict(shared)
        m["x"] = np.ascontiguousarray(x[c])
        in_maps.append(m)
    res = run_bass_kernel_spmd(nc, in_maps, core_ids=list(range(B)))
    out = np.stack([np.asarray(r["out"], np.float32) for r in res.results])
    return out


# version 33
# speedup vs baseline: 1.1624x; 1.0032x over previous
import contextlib
import numpy as np
import ml_dtypes
import concourse.bass as bass
import concourse.mybir as mybir
from concourse.bass_utils import run_bass_kernel_spmd

F32 = mybir.dt.float32
BF16 = mybir.dt.bfloat16
AF = mybir.ActivationFunctionType
ALU = mybir.AluOpType
AX = mybir.AxisListType

B, S, D = 8, 4096, 1024
H, DH = 16, 64
NP = 8
NE = 36
NT = S // 128
NG = 2
GS = S // NG
GT = GS // 128
GC = GS // 512
NEG = -1e30

ENGS = ("sync", "tensor", "scalar", "vector", "gpsimd")

DEV = {"pairs": NP, "experts": NE, "groups": NG, "attn": True, "ffn": True,
       "out": True, "taps": [], "astage": 4}


class Plan:
    def __init__(self):
        self.ops = {e: [] for e in ENGS}
        self.cnt = {s: 0 for s in ("LI", "LX0", "LX1", "CS0", "CS1", "LC", "LD", "LB", "LW", "LA", "PU0", "PU1", "T", "SS", "V", "O", "PS0", "PS1", "PS2", "PS3", "PS4", "PS5", "LB1", "LB2", "RB0", "RB1", "LW1", "LD1", "PJ0", "PJ1", "O1")}
        self.lastwait = {e: {} for e in ENGS}

    def do(self, eng, fn, inc=None, amt=1):
        if inc is not None:
            self.cnt[inc] += amt
            self.ops[eng].append((fn, inc, amt))
            return self.cnt[inc]
        self.ops[eng].append((fn, None, 0))
        return None

    def dma(self, eng, sem, dst, src):
        return self.do(eng, (lambda E, d=dst, s=src: E.dma_start(out=d, in_=s)),
                       inc=sem, amt=16)

    def wait(self, eng, sem, val=None):
        v = self.cnt[sem] if val is None else val
        if v <= 0:
            return
        if self.lastwait[eng].get(sem, -1) >= v:
            return
        self.lastwait[eng][sem] = v
        self.ops[eng].append((("wait", sem, v), None, 0))

    def drain(self, eng):
        self.ops[eng].append((("drain",), None, 0))

    def run(self, E, eng, handles):
        for fn, inc, amt in self.ops[eng]:
            if isinstance(fn, tuple):
                if fn[0] == "wait":
                    E.wait_ge(handles[fn[1]], fn[2])
                else:
                    E.drain()
                continue
            r = fn(E)
            if inc is not None:
                r.then_inc(handles[inc], amt)

    def n_ops(self):
        return {e: len(v) for e, v in self.ops.items()}


def _pack_k(w):
    K, N = w.shape
    assert K % 128 == 0
    return np.ascontiguousarray(
        w.reshape(K // 128, 128, N).transpose(1, 0, 2).reshape(128, -1))


def build_nc():
    nc = bass.Bass("TRN2", target_bir_lowering=False, debug=False)
    ap = {}

    def din(name, shape, dt=F32):
        ap[name] = nc.dram_tensor(name, list(shape), dt, kind="ExternalInput").ap()

    din("x", (S, D))
    din("wq_p", (128, 8192)); din("wk_p", (128, 8192)); din("wv_p", (128, 8192))
    din("pw_p", (128, 8192))
    din("bqk", (128, 16)); din("bvp", (1, 1024)); din("pjb", (1, 1024))
    din("gw_p", (128, 256)); din("ebt", (128, 32))
    din("upw", (NE, 128, 4096)); din("dnw", (NE, 128, 4096))
    din("cost", (128, S), BF16); din("sint", (128, S), BF16)
    din("mt", (128, 128)); din("idn", (128, 128)); din("msk", (128, 128))
    din("esh", (64, 128))
    out_hbm = nc.dram_tensor("out", [S, D], F32, kind="ExternalOutput").ap()
    upw_bf = nc.dram_tensor("upw_bf", [NE, 128, 4096], BF16, kind="Internal").ap()
    dnw_bf = nc.dram_tensor("dnw_bf", [NE, 128, 4096], BF16, kind="Internal").ap()
    combT_d = nc.dram_tensor("combT_d", [36, S], BF16, kind="Internal").ap()

    es = contextlib.ExitStack()
    tens = {}

    def sb(name, shape, dt):
        t = es.enter_context(nc.sbuf_tensor(name, list(shape), dt))
        tens[name] = t
        return t

    def psm(name, shape, dt):
        return es.enter_context(nc.psum_tensor(name, list(shape), dt))

    xnT = [sb(f"xnT{c}", (128, S), BF16) for c in range(8)]
    acc = sb("acc", (128, GT * 1024), BF16)
    wsl = [sb(f"wsl{i}", (128, 4096), BF16) for i in range(4)]
    cstg = [sb(f"cstg{i}", (36, 128), BF16) for i in range(2)]
    stg = [sb(f"stg{i}", (128, 1024), F32) for i in range(2)]
    attw = sb("attw", (128, 2048), BF16)
    vbuf = sb("vbuf", (128, 4160), BF16)
    qrot = sb("qrot", (128, GS), BF16)
    krot = sb("krot", (128, S), BF16)
    qpre = [sb(f"qpre{i}", (128, 512), BF16) for i in range(2)]
    probs = [sb(f"probs{i}", (128, 1024), BF16) for i in range(4)]
    vt1 = sb("vt1", (128, 512), F32)
    vt2 = sb("vt2", (128, 512), F32)
    sqn = qpre[0][:]
    cbuf = sb("cbuf", (1, 64), F32)
    ncb = sb("ncb", (1, 4), BF16)
    negC = sb("negC", (128, 8), F32)
    stAB = sb("stAB", (1, 1024), BF16)
    stABw = stg[1][0:1, 0:1024]
    bcsb = sb("bcsb", (128, 1024), BF16)
    ctxab = sb("ctxab", (128, 512), BF16)
    tmpb = qpre[1][0:64, :]
    rb = sb("rb", (128, 2048), BF16)
    outs = rb[:].bitcast(F32)
    hb = sb("hb", (128, 4096), BF16)
    crowb = sb("crowb", (1, 2048), BF16)
    bvpb = crowb[0:1, 0:1024]
    pjbb = crowb[0:1, 1024:2048]
    comb4 = [sb(f"comb{i}", (128, 36), F32) for i in range(4)]
    rtr4 = sb("rtr4", (128, 1024), F32)
    accf = acc[:].bitcast(F32)
    xnf = [accf[:, 0:1024], accf[:, 1024:2048]]
    csc = [sb(f"csc{i}", (128, 512), BF16) for i in range(2)]
    css = [sb(f"css{i}", (128, 512), BF16) for i in range(2)]
    mskb = sb("mskb", (128, 128), BF16)
    mtb = sb("mtb", (128, 128), BF16)
    eshb = sb("eshb", (64, 128), BF16)
    idsb = sb("idsb", (128, 128), F32)
    onesb = sb("onesb", (128, 128), BF16)
    gwb = sb("gwb", (128, 256), BF16)
    ebt = sb("ebt_s", (128, 32), F32)
    bqk = sb("bqk_s", (128, 16), F32)
    bqks = sb("bqks", (128, 16), F32)
    bnst = sb("bnst", (128, 12), F32)
    mv = sb("mv", (128, 2), F32)
    sc1 = sb("sc1", (128, 1), F32)
    rstd = sb("rstd", (128, 1), F32)
    nmr = sb("nmr", (128, 1), F32)

    P0 = psm("P0", (128, 1024), F32)
    P1 = psm("P1", (128, 1024), F32)
    P2 = psm("P2", (128, 1024), F32)
    P3 = psm("P3", (128, 1024), F32)

    handles = {}
    for sname in ("LI", "LX0", "LX1", "CS0", "CS1", "LC", "LD", "LB", "LW", "LA", "PU0", "PU1", "T", "SS", "V", "O", "PS0", "PS1", "PS2", "PS3", "PS4", "PS5", "LB1", "LB2", "RB0", "RB1", "LW1", "LD1", "PJ0", "PJ1", "O1"):
        handles[sname] = es.enter_context(nc.semaphore(sname))

    P = Plan()

    P.dma("sync", "LI", idsb[:], ap["idn"])
    P.dma("sync", "LI", stg[0][:, 0:128], ap["msk"])
    P.dma("sync", "LI", stg[0][:, 128:256], ap["mt"])
    P.dma("sync", "LI", stg[0][:, 256:272], ap["bqk"])
    P.dma("sync", "LI", ebt[:], ap["ebt"])
    P.dma("sync", "LI", stg[0][:, 272:528], ap["gw_p"])
    P.dma("sync", "LI", stg[0][0:64, 528:656], ap["esh"])
    P.wait("vector", "LI")
    P.do("vector", lambda E: E.tensor_copy(out=mskb[:], in_=stg[0][:, 0:128]))
    P.do("vector", lambda E: E.tensor_copy(out=mtb[:], in_=stg[0][:, 128:256]))
    P.do("vector", lambda E: E.tensor_copy(out=bqk[:], in_=stg[0][:, 256:272]))
    P.do("vector", lambda E: E.tensor_copy(out=gwb[:], in_=stg[0][:, 272:528]))
    P.do("vector", lambda E: E.tensor_copy(out=eshb[:], in_=stg[0][0:64, 528:656]))
    P.do("vector", lambda E: E.memset(onesb[:], 1.0))
    P.do("vector", lambda E: E.memset(vbuf[:], 1.0))
    P.do("vector", lambda E: E.memset(rb[:], 0.0))
    P.drain("vector")
    P.do("vector", lambda E: E.tensor_copy(out=bqks[:], in_=bqk[:]))
    P.drain("vector")
    bqv = bqks[:].rearrange("p (a b) -> p a b", b=2)
    P.do("vector", lambda E: E.tensor_scalar_mul(
        out=bqv[:, :, 0:1], in0=bqv[:, :, 0:1], scalar1=0.125), inc="V")

    pump_state = {"k": 0, "cast": {}, "out": {}}
    hbf = hb[:].bitcast(F32)

    PUMP_TOT = NE * 2 * 4
    PUMP_SLOTS = 6
    obuf = [rb[:, 0:2048], acc[:, 12288:14336], acc[:, 14336:16384]]

    def _pump_slot(j):
        s = j % PUMP_SLOTS
        if s < 2:
            return hbf[:, 1024 * s:1024 * (s + 1)]
        return accf[:, 2048 + 1024 * (s - 2):2048 + 1024 * (s - 1)]

    def _pump_issue_in(j):
        if j >= PUMP_TOT:
            return
        mat = j // 4
        jc = j % 4
        e = mat // 2
        srcw = ap["upw"] if mat % 2 == 0 else ap["dnw"]
        if j >= PUMP_SLOTS:
            P.wait("scalar", "V", pump_state["cast"][j - PUMP_SLOTS])
        P.dma("scalar", "PS%d" % (j % PUMP_SLOTS), _pump_slot(j),
              srcw[e, :, 1024 * jc:1024 * (jc + 1)])

    def emit_pump(nchunks):
        for _ in range(nchunks):
            k = pump_state["k"]
            if k >= PUMP_TOT:
                return
            pump_state["k"] = k + 1
            mat = k // 4
            e = mat // 2
            dstw = upw_bf if mat % 2 == 0 else dnw_bf
            h = k // 2
            ob = obuf[h % 3]
            if k == 0:
                P.wait("scalar", "V", 1)
                for j in range(PUMP_SLOTS - 1):
                    _pump_issue_in(j)
            _pump_issue_in(k + PUMP_SLOTS - 1)
            P.wait("vector", "PS%d" % (k % PUMP_SLOTS))
            if h >= 3:
                P.wait("vector", *pump_state["out"][h - 3])
            pump_state["cast"][k] = P.do(
                "vector", lambda E, k=k, ob=ob: E.tensor_copy(
                    out=ob[:, 1024 * (k % 2):1024 * (k % 2) + 1024], in_=_pump_slot(k)),
                inc="V")
            if k % 2 == 1:
                P.wait("gpsimd", "V", pump_state["cast"][k])
                osem = ("LB", "LB1", "LB2")[h % 3]
                pump_state["out"][h] = (osem, P.dma(
                    "gpsimd", osem, dstw[e, :, 2048 * (h % 2):2048 * (h % 2) + 2048],
                    ob[:]))

    ss_xn = [0] * NT
    v_tr = [0] * NT
    for i in range(NT):
        b = i % 2
        if i < 2:
            P.wait("sync", "V")
        else:
            P.wait("sync", "SS", ss_xn[i - 2])
        P.dma("sync", "LX0" if b == 0 else "LX1", stg[b][:], ap["x"][128 * i:128 * (i + 1), :])
        P.wait("vector", "LX0" if b == 0 else "LX1")
        P.do("vector", lambda E, b=b: E.bn_stats(out=bnst[:, 0:6], in_=stg[b][:, 0:512]))
        P.do("vector", lambda E, b=b: E.bn_stats(out=bnst[:, 6:12], in_=stg[b][:, 512:1024]))
        P.drain("vector")
        P.do("vector", lambda E: E.bn_aggr(out=mv[:], in_=bnst[:]))
        P.drain("vector")
        P.do("vector", lambda E: E.tensor_scalar_add(out=sc1[:], in0=mv[:, 1:2], scalar1=1e-5))
        P.drain("vector")
        P.do("vector", lambda E: E.reciprocal(out=sc1[:], in_=sc1[:]), inc="V")
        P.wait("scalar", "V")
        P.do("scalar", lambda E: E.activation(out=rstd[:], in_=sc1[:], func=AF.Sqrt), inc="SS")
        P.wait("vector", "SS")
        P.do("vector", lambda E: E.tensor_scalar(out=nmr[:], in0=mv[:, 0:1],
                                                 scalar1=rstd[:], scalar2=-1.0,
                                                 op0=ALU.mult, op1=ALU.mult), inc="V")
        P.wait("scalar", "V")
        if i >= 2:
            P.wait("scalar", "T")
        xb = xnf[b]
        ss_xn[i] = P.do("scalar", lambda E, b=b, xb=xb: E.activation(
            out=xb[:], in_=stg[b][:], func=AF.Identity, bias=nmr[:], scale=rstd[:]),
            inc="SS")
        P.wait("tensor", "SS")
        v_tr4 = {}
        for c in range(8):
            reg = (P3[:, 0:128], P3[:, 512:640],
                   P0[:, 0:128], P0[:, 512:640])[c % 4]
            if c >= 4:
                P.wait("tensor", "V", v_tr4[c - 4])
            P.do("tensor", lambda E, xb=xb, c=c, reg=reg: E.transpose(
                out=reg, in_=xb[:, 128 * c:128 * (c + 1)], identity=idsb[:]), inc="T")
            P.wait("vector", "T")
            v_tr4[c] = P.do("vector", lambda E, c=c, i=i, reg=reg: E.tensor_copy(
                out=xnT[c][:, 128 * i:128 * (i + 1)], in_=reg), inc="V")
        v_tr[i] = P.cnt["V"]
        if DEV["ffn"]:
            emit_pump(9)

    comb_lb = {}
    _steps = [
        (lambda E, r: E.tensor_add(out=r[:, 32:64], in0=r[:, 0:32], in1=ebt[:]),),
        tuple(
            (lambda E, r, a=a, bb=bb, j=j: E.tensor_add(
                out=r[:, 64 + 8 * j:72 + 8 * j],
                in0=r[:, 32:64].rearrange("p (g e) -> p g e", e=4)[:, :, a:a + 1].squeeze(2),
                in1=r[:, 32:64].rearrange("p (g e) -> p g e", e=4)[:, :, bb:bb + 1].squeeze(2)))
            for j, (a, bb) in enumerate([(0, 1), (0, 2), (0, 3), (1, 2), (1, 3), (2, 3)])),
        (lambda E, r: E.tensor_max(out=r[:, 112:120], in0=r[:, 64:72], in1=r[:, 72:80]),
         lambda E, r: E.tensor_max(out=r[:, 120:128], in0=r[:, 80:88], in1=r[:, 88:96]),
         lambda E, r: E.tensor_max(out=r[:, 128:136], in0=r[:, 96:104], in1=r[:, 104:112])),
        (lambda E, r: E.tensor_max(out=r[:, 112:120], in0=r[:, 112:120], in1=r[:, 120:128]),),
        (lambda E, r: E.tensor_max(out=r[:, 112:120], in0=r[:, 112:120], in1=r[:, 128:136]),),
        (lambda E, r: E.max(out=r[:, 120:128], in_=r[:, 112:120]),),
        (lambda E, r: E.memset(r[:, 124:128], NEG),),
        (lambda E, r: E.match_replace(out=r[:, 128:136], in_to_replace=r[:, 120:128],
                                      in_values=r[:, 112:120], imm_value=NEG),),
        (lambda E, r: E.tensor_tensor(out=r[:, 120:128], in0=r[:, 128:136],
                                      in1=r[:, 112:120], op=ALU.is_lt),),
        (lambda E, r: E.tensor_tensor(
            out=r[:, 136:168].rearrange("p (g e) -> p g e", e=4),
            in0=r[:, 32:64].rearrange("p (g e) -> p g e", e=4),
            in1=r[:, 120:128].unsqueeze(2).to_broadcast([128, 8, 4]),
            op=ALU.mult),),
        (lambda E, r: E.max(out=r[:, 168:176], in_=r[:, 136:168]),),
        (lambda E, r: E.memset(r[:, 174:176], NEG),),
        (lambda E, r: E.match_replace(out=r[:, 176:208], in_to_replace=r[:, 168:176],
                                      in_values=r[:, 136:168], imm_value=NEG),),
        (lambda E, r: E.tensor_tensor(out=r[:, 176:208], in0=r[:, 176:208],
                                      in1=r[:, 136:168], op=ALU.is_lt),),
        (lambda E, r: E.tensor_mul(out=r[:, 208:240], in0=r[:, 176:208], in1=r[:, 0:32]),),
        (lambda E, r: E.reduce_sum(out=r[:, 240:241], in_=r[:, 208:240], axis=AX.X),),
        (lambda E, r: E.tensor_scalar_add(out=r[:, 240:241], in0=r[:, 240:241],
                                          scalar1=1e-20),),
        (lambda E, r: E.reciprocal(out=r[:, 241:242], in_=r[:, 240:241]),),
    ]
    for q in range(NT // 4):
        tiles = [4 * q + t for t in range(4)]
        P.wait("tensor", "V")
        P.wait("tensor", "SS")
        for t, i in enumerate(tiles):
            for c in range(8):
                P.do("tensor", lambda E, c=c, i=i, t=t: E.matmul(
                    out=P3[:, 512 + 32 * t:544 + 32 * t],
                    lhsT=xnT[c][:, 128 * i:128 * (i + 1)],
                    rhs=gwb[:, 32 * c:32 * (c + 1)],
                    start=(c == 0), stop=(c == 7)), inc=("T" if c == 7 else None))
        P.wait("scalar", "T")
        for t in range(4):
            P.do("scalar", lambda E, t=t: E.activation(
                out=rtr4[:, 256 * t:256 * t + 32], in_=P3[:, 512 + 32 * t:544 + 32 * t],
                func=AF.Sigmoid), inc="SS")
        P.wait("vector", "SS")
        for step in _steps:
            for t in range(4):
                for fn in step:
                    P.do("vector", lambda E, fn=fn, t=t: fn(
                        E, rtr4[:, 256 * t:256 * (t + 1)]))
            P.drain("vector")
        P.wait("vector", "T")
        for t in range(4):
            r4 = rtr4[:, 256 * t:256 * (t + 1)]
            P.do("vector", lambda E, cb=comb4[t], r4=r4: E.tensor_scalar(
                out=cb[:, 0:32], in0=r4[:, 208:240],
                scalar1=r4[:, 241:242], scalar2=2.5,
                op0=ALU.mult, op1=ALU.mult))
            P.do("vector", lambda E, cb=comb4[t]: E.memset(cb[:, 32:36], 1.0))
        P.drain("vector")
        P.do("vector", lambda E: E.nop(), inc="V")
        for t, i in enumerate(tiles):
            P.wait("tensor", "V")
            P.do("tensor", lambda E, cb=comb4[t]: E.transpose(
                out=P3[0:36, 384:512], in_=cb[:], identity=idsb[:]), inc="T")
            P.wait("vector", "T")
            if i >= 2:
                P.wait("vector", "RB%d" % (i % 2), comb_lb[i - 2])
            P.do("vector", lambda E, i=i: E.tensor_copy(
                out=cstg[i % 2][:], in_=P3[0:36, 384:512]), inc="V")
            P.wait("gpsimd", "V")
            comb_lb[i] = P.dma("gpsimd", "RB%d" % (i % 2),
                               combT_d[0:36, 128 * i:128 * (i + 1)], cstg[i % 2][:])

    v_out = (0, 0)
    prev_last_exp = [0]
    t_attw_free = [0]
    v_stg_free = [0]
    att_end_prev = [0]
    prev_up_end = [None]
    krotf = krot[:].bitcast(F32)
    for g in range(DEV["groups"]):
        KC = 4 * (g + 1)
        KT = 16 * (g + 1)

        P.wait("scalar", "T", att_end_prev[0])
        v_pw = {}
        for hchunk in range(8):
            sl = krotf[:, 1024 * (hchunk % 2):1024 * (hchunk % 2) + 1024]
            if hchunk >= 2:
                P.wait("scalar", "V", v_pw[hchunk - 2])
            P.dma("scalar", "PJ%d" % (hchunk % 2), sl,
                  ap["pw_p"][:, 1024 * hchunk:1024 * (hchunk + 1)])
            P.wait("vector", "PJ%d" % (hchunk % 2))
            if prev_up_end[0] is not None:
                P.wait("vector", "T", prev_up_end[0][hchunk // 4])
            dstsl = wsl[hchunk // 4]
            off = 1024 * (hchunk % 4)
            v_pw[hchunk] = P.do("vector", lambda E, s=sl, d=dstsl, off=off: E.tensor_copy(
                out=d[:, off:off + 1024], in_=s[:]), inc="V")
        if g == 1:
            P.wait("vector", "O", v_out[0])
            P.wait("vector", "O1", v_out[1])
        P.wait("sync", "T")
        P.wait("sync", "V")
        P.wait("sync", "SS")
        P.dma("sync", "LX1", stg[1][0:1, 0:1024], ap["bvp"])
        P.wait("vector", "LX1")
        P.wait("vector", "LC")
        P.wait("vector", "T")
        P.do("vector", lambda E: E.tensor_copy(out=bvpb, in_=stg[1][0:1, 0:1024]), inc="V")
        P.wait("sync", "V")
        P.dma("sync", "LX1", stg[1][0:1, 0:1024], ap["pjb"])
        P.wait("vector", "LX1")
        P.do("vector", lambda E: E.tensor_copy(out=pjbb, in_=stg[1][0:1, 0:1024]), inc="V")
        v_stg_free[0] = P.cnt["V"]

        for p in range(DEV["pairs"] if DEV["attn"] else 0):
            P.wait("scalar", "T", t_attw_free[0])
            P.wait("scalar", "V", v_stg_free[0])
            for dc in range(8):
                P.dma("scalar", "LA", stg[0][:, 128 * dc:128 * (dc + 1)],
                      ap["wq_p"][:, 1024 * dc + 128 * p:1024 * dc + 128 * (p + 1)])
            P.wait("vector", "LA")
            P.wait("vector", "T", t_attw_free[0])
            v_attw = P.do("vector", lambda E: E.tensor_copy(out=attw[:, 0:1024], in_=stg[0][:]),
                          inc="V")
            P.wait("scalar", "V", v_attw)
            for dc in range(8):
                P.dma("scalar", "LA", stg[0][:, 128 * dc:128 * (dc + 1)],
                      ap["wk_p"][:, 1024 * dc + 128 * p:1024 * dc + 128 * (p + 1)])
            P.wait("vector", "LA")
            v_attw_k = P.do("vector", lambda E: E.tensor_copy(out=attw[:, 1024:2048], in_=stg[0][:]),
                            inc="V")

            ss_pre = {}
            ropev = []
            reg_of = {}
            ckc = [0]
            t_q_end = 0
            for kind in (("q",) if DEV.get("skipk") else ("q", "k")):
                if kind == "k":
                    t_q_end = P.cnt["T"]
                    P.wait("scalar", "V", v_attw_k)
                    for dc in range(8):
                        P.dma("scalar", "LA", stg[0][:, 128 * dc:128 * (dc + 1)],
                              ap["wv_p"][:, 1024 * dc + 128 * p:1024 * dc + 128 * (p + 1)])
                    P.wait("vector", "LA")
                    P.wait("vector", "T", t_q_end)
                    v_attw = P.do("vector", lambda E: E.tensor_copy(out=attw[:, 0:1024],
                                                                    in_=stg[0][:]), inc="V")
                half = 0 if kind == "q" else 1024
                nch = GC if kind == "q" else KC

                def emit_qk_mms(ci, kind=kind, p=p, half=half):
                    c = (GC * g + ci) if kind == "q" else ci
                    b = ci % 2
                    rot = ckc[0] % 3
                    ckc[0] += 1
                    reg_of[(kind, ci)] = rot
                    P.wait("sync", "V")
                    P.dma("sync", "CS0" if b == 0 else "CS1", csc[b][:], ap["cost"][:, 512 * c:512 * (c + 1)])
                    P.dma("sync", "CS0" if b == 0 else "CS1", css[b][:], ap["sint"][:, 512 * c:512 * (c + 1)])
                    qkreg = (P3, P0, P1)[rot][:, 0:512]
                    wv_need = v_attw if kind == "q" else v_attw_k
                    if len(ropev) >= 3:
                        P.wait("tensor", "V", max(wv_need, ropev[-3]))
                    else:
                        P.wait("tensor", "V", wv_need)
                    if (kind, ci - 3) in ss_pre:
                        P.wait("tensor", "SS", ss_pre[(kind, ci - 3)])
                    elif kind == "k" and ("q", GC - 3 + ci) in ss_pre:
                        P.wait("tensor", "SS", ss_pre[("q", GC - 3 + ci)])
                    for dc in range(8):
                        P.do("tensor", lambda E, dc=dc, c=c, qkreg=qkreg, half=half: E.matmul(
                            out=qkreg, lhsT=attw[:, half + 128 * dc:half + 128 * (dc + 1)],
                            rhs=xnT[dc][:, 512 * c:512 * (c + 1)],
                            start=(dc == 0), stop=(dc == 7)),
                            inc=("T" if dc == 7 else None))
                    return P.cnt["T"]

                def emit_qk_tail(ci, kind=kind, p=p):
                    b = ci % 2
                    rot = reg_of[(kind, ci)]
                    qkreg = (P3, P0, P1)[rot][:, 0:512]
                    rreg = (P3, P0, P1)[rot][:, 512:1024]
                    P.wait("scalar", "T", qk_t[(kind, ci)])
                    if len(ropev) >= 2:
                        P.wait("scalar", "V", ropev[-2])
                    bias = bqks[:, 2 * p:2 * p + 1] if kind == "q" else bqk[:, 2 * p + 1:2 * p + 2]
                    scale = 0.125 if kind == "q" else 1.0
                    ss_pre[(kind, ci)] = P.do(
                        "scalar", lambda E, b=b, bias=bias, scale=scale, qkreg=qkreg:
                        E.activation(out=qpre[b][:], in_=qkreg, func=AF.Identity,
                                     bias=bias, scale=scale), inc="SS")
                    P.wait("tensor", "SS", ss_pre[(kind, ci)])
                    if len(ropev) >= 2:
                        P.wait("tensor", "V", ropev[-2])
                    P.do("tensor", lambda E, b=b, rreg=rreg: E.matmul(
                        out=rreg, lhsT=mtb[:], rhs=qpre[b][:], start=True, stop=True),
                        inc="T")
                    dst = qrot if kind == "q" else krot
                    dof = 512 * ci if kind == "q" else 512 * ci
                    P.wait("vector", "T")
                    P.wait("vector", "CS0" if b == 0 else "CS1")
                    P.drain("vector")
                    P.do("vector", lambda E, b=b: E.tensor_mul(
                        out=vt1[:], in0=qpre[b][:], in1=csc[b][:]))
                    P.do("vector", lambda E, b=b, rreg=rreg: E.tensor_tensor(
                        out=vt2[:], in0=rreg, in1=css[b][:], op=ALU.mult))
                    P.drain("vector")
                    ropev.append(P.do("vector", lambda E, dst=dst, dof=dof: E.tensor_add(
                        out=dst[:, dof:dof + 512], in0=vt1[:], in1=vt2[:]), inc="V"))

                qk_t = getattr(P, "_qk_t", None)
                if qk_t is None:
                    qk_t = {}
                    P._qk_t = qk_t
                for ci in range(nch + 1):
                    if ci < nch:
                        qk_t[(kind, ci)] = emit_qk_mms(ci)
                    if ci - 1 >= 0:
                        emit_qk_tail(ci - 1)
            if DEV["astage"] < 2:
                continue
            P.wait("tensor", "V")
            v_ev = [0] * KT
            for tt in range(KT):
                if g == 0 and DEV["ffn"]:
                    emit_pump(6)
                vreg = (P2[:, 0:128], P2[:, 640:768],
                        P0[:, 0:128], P0[:, 640:768])[tt % 4]
                if tt >= 4:
                    P.wait("tensor", "V", v_ev[tt - 4])
                for dc in range(8):
                    P.do("tensor", lambda E, dc=dc, tt=tt, vreg=vreg: E.matmul(
                        out=vreg, lhsT=xnT[dc][:, 128 * tt:128 * (tt + 1)],
                        rhs=attw[:, 128 * dc:128 * (dc + 1)],
                        start=(dc == 0), stop=False))
                P.do("tensor", lambda E, p=p, vreg=vreg: E.matmul(
                    out=vreg, lhsT=onesb[0:1, 0:128],
                    rhs=bvpb[:, 128 * p:128 * (p + 1)],
                    start=False, stop=True), inc="T")
                P.wait("vector", "T")
                P.do("vector", lambda E, tt=tt, vreg=vreg: E.tensor_copy(
                    out=vbuf[:, 130 * tt:130 * tt + 64], in_=vreg[:, 0:64]))
                v_ev[tt] = P.do("vector", lambda E, tt=tt, vreg=vreg: E.tensor_copy(
                    out=vbuf[:, 130 * tt + 65:130 * tt + 129], in_=vreg[:, 64:128]), inc="V")
            t_attw_free[0] = P.cnt["T"]
            v_stg_free[0] = v_attw

            if DEV["astage"] < 3:
                continue
            if True:
              for kind, nch, buf in (("q", GC, qrot), ("k", KC, krot)):
                nmm_t = {}
                sq_v = {}
                red_v = {}
                for ci in range(nch + 2):
                    if ci < nch:
                        sqb = qpre[ci % 2]
                        P.wait("vector", "T", nmm_t.get(ci - 2, 0))
                        sq_v[ci] = P.do("vector", lambda E, buf=buf, ci=ci, sqb=sqb: E.tensor_mul(
                            out=sqb[:], in0=buf[:, 512 * ci:512 * (ci + 1)],
                            in1=buf[:, 512 * ci:512 * (ci + 1)]), inc="V")
                    if 0 <= ci - 1 < nch:
                        c2 = ci - 1
                        sqb = qpre[c2 % 2]
                        par = 512 * (c2 % 2)
                        P.wait("tensor", "V", sq_v[c2])
                        P.wait("tensor", "V", red_v.get(c2 - 2, 0))
                        P.do("tensor", lambda E, sqb=sqb, par=par: E.matmul(
                            out=P2[0:1, par:par + 512], lhsT=onesb[0:64, 0:1], rhs=sqb[0:64, :],
                            start=True, stop=True, tile_position=(0, 0)))
                        nmm_t[c2] = P.do("tensor", lambda E, sqb=sqb, par=par: E.matmul(
                            out=P1[0:1, par:par + 512], lhsT=onesb[64:128, 0:1], rhs=sqb[64:128, :],
                            start=True, stop=True, tile_position=(64, 0)), inc="T")
                    if 0 <= ci - 2 < nch:
                        c2 = ci - 2
                        par = 512 * (c2 % 2)
                        P.wait("vector", "T", nmm_t[c2])
                        ofA = (0 if kind == "q" else 16) + c2
                        ofB = (8 if kind == "q" else 32) + c2
                        P.do("vector", lambda E, ofA=ofA, par=par: E.reduce_max(
                            out=cbuf[0:1, ofA:ofA + 1], in_=P2[0:1, par:par + 512], axis=AX.X))
                        red_v[c2] = P.do("vector", lambda E, ofB=ofB, par=par: E.reduce_max(
                            out=cbuf[0:1, ofB:ofB + 1], in_=P1[0:1, par:par + 512], axis=AX.X), inc="V")
            P.drain("vector")
            P.do("vector", lambda E: E.reduce_max(out=cbuf[0:1, 40:41], in_=cbuf[0:1, 0:GC], axis=AX.X))
            P.do("vector", lambda E: E.reduce_max(out=cbuf[0:1, 41:42], in_=cbuf[0:1, 8:8 + GC], axis=AX.X))
            P.do("vector", lambda E, KC=KC: E.reduce_max(out=cbuf[0:1, 42:43], in_=cbuf[0:1, 16:16 + KC], axis=AX.X))
            P.do("vector", lambda E, KC=KC: E.reduce_max(out=cbuf[0:1, 43:44], in_=cbuf[0:1, 32:32 + KC], axis=AX.X))
            P.drain("vector")
            P.do("vector", lambda E: E.tensor_max(out=cbuf[0:1, 40:41], in0=cbuf[0:1, 40:41], in1=cbuf[0:1, 41:42]))
            P.do("vector", lambda E: E.tensor_max(out=cbuf[0:1, 42:43], in0=cbuf[0:1, 42:43], in1=cbuf[0:1, 43:44]))
            P.drain("vector")
            P.do("vector", lambda E: E.tensor_add(out=cbuf[0:1, 44:45], in0=cbuf[0:1, 40:41], in1=cbuf[0:1, 42:43]))
            P.drain("vector")
            P.wait("vector", "SS", prev_last_exp[0])
            P.do("vector", lambda E: E.tensor_scalar_mul(out=ncb[0:1, 0:1], in0=cbuf[0:1, 44:45], scalar1=-1.0 / 16.0), inc="V")
            P.wait("tensor", "V")
            P.do("tensor", lambda E: E.matmul(out=P3[:, 0:1], lhsT=onesb[0:1, 0:128],
                                              rhs=ncb[0:1, 0:1], start=True, stop=True), inc="T")
            P.wait("vector", "T")
            P.do("vector", lambda E: E.tensor_copy(out=negC[:, 0:1], in_=P3[:, 0:1]), inc="V")
            v_negc = P.cnt["V"]
            P.wait("scalar", "V")

            if DEV["astage"] < 4:
                continue
            exps_all = [0, 0, 0]
            ctx_all = [0, 0]
            v_probs = {}
            sc_t = {}
            ctx_of = {}
            gof = [0]
            prev_proj_v = [0]
            for sq in range(4):
                sjg = 4 * g + sq
                ntc = 4 * sjg + 4
                gb = gof[0]
                t_ctx_last = 0
                diag_of = {}

                def emit_sc(tc, sq=sq, gb=gb):
                    PS = (P0, P1, P3)[tc % 3]
                    if PS is P3:
                        P.wait("tensor", "V", v_negc)
                    P.wait("tensor", "SS", exps_all[-2])
                    P.do("tensor", lambda E, tc=tc, sq=sq, PS=PS: E.matmul(
                        out=PS[:, 0:512], lhsT=krot[0:64, 128 * tc:128 * (tc + 1)],
                        rhs=qrot[0:64, 512 * sq:512 * (sq + 1)],
                        start=True, stop=True, tile_position=(0, 0)))
                    sc_t[gb + tc] = P.do("tensor", lambda E, tc=tc, sq=sq, PS=PS: E.matmul(
                        out=PS[:, 512:1024], lhsT=krot[64:128, 128 * tc:128 * (tc + 1)],
                        rhs=qrot[64:128, 512 * sq:512 * (sq + 1)],
                        start=True, stop=True, tile_position=(64, 0)), inc="T")

                def emit_exp(tc, gb=gb):
                    PS = (P0, P1, P3)[tc % 3]
                    gt = gb + tc
                    pr = probs[gt % 4]
                    P.wait("scalar", "T", sc_t[gt])
                    P.wait("scalar", "T", ctx_of.get(gt - 4, 0))
                    if (gt - 4) in v_probs:
                        P.wait("scalar", "V", v_probs[gt - 4])
                    ev = P.do("scalar", lambda E, PS=PS, pr=pr: E.activation(
                        out=pr[:], in_=PS[:], func=AF.Exp, bias=negC[:, 0:1]), inc="SS")
                    exps_all.append(ev)
                    r = tc - 4 * sjg
                    if r >= 0:
                        diag_of[tc] = r
                        P.wait("vector", "SS", ev)
                        for half in (0, 512):
                            P.do("vector", lambda E, pr=pr, r=r, half=half: E.tensor_tensor(
                                out=pr[:, half + 128 * r:half + 128 * (r + 1)],
                                in0=pr[:, half + 128 * r:half + 128 * (r + 1)], in1=mskb[:],
                                op=ALU.mult))
                            if r > 0:
                                P.do("vector", lambda E, pr=pr, r=r, half=half: E.memset(
                                    pr[:, half:half + 128 * r], 0.0))
                        P.drain("vector")
                        P.do("vector", lambda E: E.nop(), inc="V")
                        v_probs[gt] = P.cnt["V"]

                exp_of = {}
                for i in range(ntc + 2):
                    if i < ntc:
                        emit_sc(i)
                    if 0 <= i - 1 < ntc:
                        emit_exp(i - 1)
                        exp_of[i - 1] = exps_all[-1]
                    if 0 <= i - 2 < ntc:
                        tcx = i - 2
                        gt = gb + tcx
                        pr = probs[gt % 4]
                        if tcx == 0:
                            P.wait("tensor", "V", prev_proj_v[0])
                        if tcx in diag_of:
                            P.wait("tensor", "V", v_probs[gt])
                        else:
                            P.wait("tensor", "SS", exp_of[tcx])
                        st = (tcx == 0)
                        sp = (tcx == ntc - 1)
                        P.do("tensor", lambda E, tc=tcx, pr=pr, st=st, sp=sp: E.matmul(
                            out=P2[0:65, 0:512], lhsT=vbuf[:, 130 * tc:130 * tc + 65],
                            rhs=pr[:, 0:512], start=st, stop=sp))
                        t_ctx_last = P.do("tensor", lambda E, tc=tcx, pr=pr, st=st, sp=sp: E.matmul(
                            out=P2[0:65, 512:1024], lhsT=vbuf[:, 130 * tc + 65:130 * tc + 130],
                            rhs=pr[:, 512:1024], start=st, stop=sp), inc="T")
                        ctx_all.append(t_ctx_last)
                        ctx_of[gt] = t_ctx_last
                P.wait("vector", "T", t_ctx_last)
                P.do("vector", lambda E: E.tensor_copy(out=tmpb, in_=P2[0:64, 512:1024]),
                     inc="V")
                P.wait("scalar", "T", t_ctx_last)
                P.do("scalar", lambda E: E.activation(out=stABw, in_=P2[64:65, 0:1024],
                                                      func=AF.Ln))
                P.drain("scalar")
                P.do("scalar", lambda E: E.activation(out=stAB[:, 0:1024], in_=stABw,
                                                      func=AF.Exp, scale=-1.0), inc="SS")
                P.wait("tensor", "SS")
                P.do("tensor", lambda E: E.matmul(out=P3[:, 0:512], lhsT=onesb[0:1, 0:128],
                                                  rhs=stAB[0:1, 0:512], start=True, stop=True))
                t_bc2 = P.do("tensor", lambda E: E.matmul(
                    out=P3[:, 512:1024], lhsT=onesb[0:1, 0:128],
                    rhs=stAB[0:1, 512:1024], start=True, stop=True), inc="T")
                P.wait("tensor", "V")
                P.do("tensor", lambda E: E.matmul(out=P2[:, 512:1024], lhsT=eshb[:],
                                                  rhs=tmpb, start=True, stop=True), inc="T")
                P.wait("scalar", "T", t_bc2)
                P.do("scalar", lambda E: E.activation(out=bcsb[:], in_=P3[:, 0:1024],
                                                      func=AF.Identity), inc="SS")
                P.wait("vector", "T")
                P.wait("vector", "SS")
                P.do("vector", lambda E: E.tensor_mul(out=ctxab[0:64, :], in0=P2[0:64, 0:512],
                                                      in1=bcsb[0:64, 0:512]))
                v_ctx_scale_prev = P.do("vector", lambda E: E.tensor_mul(
                    out=ctxab[64:128, :], in0=P2[64:128, 512:1024],
                    in1=bcsb[64:128, 512:1024]), inc="V")
                P.wait("tensor", "V", v_ctx_scale_prev)
                for msub in range(4):
                    preg = P3 if msub % 2 == 0 else P2
                    jloc = 4 * sq + msub
                    if msub >= 2:
                        P.wait("tensor", "V")
                    lastm = None
                    for n in range(2):
                        last = (n == 1 and p > 0)
                        P.do("tensor", lambda E, p=p, n=n, msub=msub, preg=preg: E.matmul(
                            out=preg[:, 512 * n:512 * (n + 1)],
                            lhsT=ctxab[:, 128 * msub:128 * (msub + 1)],
                            rhs=wsl[p // 4][:, 1024 * (p % 4) + 512 * n:1024 * (p % 4) + 512 * (n + 1)],
                            start=True, stop=(p > 0)), inc=("T" if last else None))
                        if p == 0:
                            P.do("tensor", lambda E, n=n, preg=preg: E.matmul(
                                out=preg[:, 512 * n:512 * (n + 1)], lhsT=onesb[0:1, 0:128],
                                rhs=pjbb[:, 512 * n:512 * (n + 1)],
                                start=False, stop=True), inc=("T" if n == 1 else None))
                    P.wait("vector", "T")
                    if p == 0:
                        P.do("vector", lambda E, jloc=jloc, preg=preg: E.tensor_copy(
                            out=acc[:, 1024 * jloc:1024 * (jloc + 1)], in_=preg[:]), inc="V")
                    else:
                        P.do("vector", lambda E, jloc=jloc, preg=preg: E.tensor_tensor(
                            out=acc[:, 1024 * jloc:1024 * (jloc + 1)],
                            in0=acc[:, 1024 * jloc:1024 * (jloc + 1)], in1=preg[:],
                            op=ALU.add), inc="V")
                    prev_proj_v[0] = P.cnt["V"]
                gof[0] += ntc
            prev_last_exp[0] = exps_all[-1]
        def load_expert(e, which, tfree):
            sl = wsl[(e % 2) if which == "up" else 2 + (e % 2)]
            srcw = upw_bf if which == "up" else dnw_bf
            if which == "up":
                eng, sem = "sync", ("LW" if e % 2 == 0 else "LW1")
            else:
                eng, sem = "gpsimd", ("LD" if e % 2 == 0 else "LD1")
            P.wait(eng, "T", tfree)
            need_h = 4 * e + (1 if which == "up" else 3)
            P.wait(eng, *pump_state["out"][need_h])
            cnt = P.dma(eng, sem, sl[:], srcw[e])
            return [(sem, cnt)]

        t_att_end = P.cnt["T"]
        att_end_prev[0] = t_att_end
        if g == 0 and DEV["ffn"]:
            emit_pump(NE * 2 * 8)
        P.wait("scalar", "O")
        P.wait("scalar", "O1")
        P.wait("scalar", "LB")
        P.wait("scalar", "LB1")
        P.wait("scalar", "LB2")
        P.wait("scalar", "RB0")
        P.wait("scalar", "RB1")
        if not DEV["attn"] or DEV["pairs"] == 0:
            P.do("vector", lambda E: E.memset(acc[:], 0.0), inc="V")
        NE_RUN = DEV["experts"] if DEV["ffn"] else 0
        up_v = load_expert(0, "up", t_att_end) if NE_RUN else None
        dn_v = load_expert(0, "dn", t_att_end) if NE_RUN else None
        dn_waits = {0: dn_v}
        up_waits = {0: up_v}
        if NE_RUN:
            P.wait("scalar", "T")
            P.wait("scalar", "V")
            P.dma("scalar", "LC", crowb[:], combT_d[0:1, GS * g:GS * (g + 1)])

        t_up_end = [0] * NE
        t_dn_end = [0] * NE
        down_t_done = {}
        v_hb = {}
        v_dn = {}
        ss_sq = {}
        dslots = [P3[:, 0:512], P3[:, 512:1024], P2[:, 512:1024]]
        dslot_last = {}
        epr = None
        pend_down = []
        pend_scale = []
        v_bcc = {}
        for e in range(NE_RUN):
            def emit_down(e, tc):
                for sm, vv in (dn_waits.get(e) or []):
                    P.wait("tensor", sm, vv)
                P.wait("tensor", "V", v_hb[(e, tc, 3)])
                for tt in range(4):
                    jloc = 4 * tc + tt
                    for n in range(2):
                        slot = dslots[(2 * (4 * tc + tt) + n) % 3]
                        key_prev = dslot_last.get((2 * (4 * tc + tt) + n) % 3)
                        if key_prev is not None:
                            P.wait("tensor", "V", key_prev)
                        ho = 2048 * (tc % 2)
                        for i in range(4):
                            P.do("tensor", lambda E, i=i, n=n, tt=tt, e=e, slot=slot, ho=ho: E.matmul(
                                out=slot,
                                lhsT=hb[:, ho + 512 * i + 128 * tt:ho + 512 * i + 128 * (tt + 1)],
                                rhs=wsl[2 + e % 2][:, 1024 * i + 512 * n:1024 * i + 512 * (n + 1)],
                                start=(i == 0), stop=(i == 3)),
                                inc=("T" if i == 3 else None))
                        P.wait("vector", "T")
                        vv = P.do("vector", lambda E, jloc=jloc, n=n, slot=slot: E.tensor_tensor(
                            out=acc[:, 1024 * jloc + 512 * n:1024 * jloc + 512 * (n + 1)],
                            in0=acc[:, 1024 * jloc + 512 * n:1024 * jloc + 512 * (n + 1)],
                            in1=slot, op=ALU.add), inc="V")
                        dslot_last[(2 * (4 * tc + tt) + n) % 3] = vv
                        v_dn[(e, tc, tt, n)] = vv
                v_dn[(e, tc)] = v_dn[(e, tc, 3, 1)]
                down_t_done[e] = P.cnt["T"]

            for tc in range(GC):
                TC = GS * g + 512 * tc
                pend_down.append((e, tc))
                t_dn_guard = 0
                if len(pend_down) >= 3:
                    de, dtc = pend_down.pop(0)
                    emit_down(de, dtc)
                    t_dn_guard = P.cnt["T"]
                if pend_scale:
                    se, stc = pend_scale.pop(0)
                    sho = 2048 * (stc % 2)
                    P.wait("vector", "SS", ss_sq[(se, stc, 1)])
                    for i in range(4):
                        v_hb[(se, stc, i)] = P.do(
                            "vector", lambda E, i=i, sho=sho, stc=stc: E.tensor_tensor(
                                out=hb[:, sho + 512 * i:sho + 512 * (i + 1)],
                                in0=hb[:, sho + 512 * i:sho + 512 * (i + 1)],
                                in1=bcsb[:, 512 * (stc % 2):512 * (stc % 2) + 512],
                                op=ALU.mult), inc="V")
                t_up_i = {}
                for i in (2, 3, 0, 1):
                    reg = (P1 if i >= 2 else P0)[:, 512 * (i % 2):512 * (i % 2) + 512]
                    if (e, tc - 1, i) in ss_sq:
                        P.wait("tensor", "SS", ss_sq[(e, tc - 1, i)])
                    elif tc == 0 and (epr, GC - 1, i) in ss_sq:
                        P.wait("tensor", "SS", ss_sq[(epr, GC - 1, i)])
                    if tc == 0:
                        for sm, vv in (up_waits.get(e) or []):
                            P.wait("tensor", sm, vv)
                    for dc in range(8):
                        P.do("tensor", lambda E, dc=dc, i=i, TC=TC, reg=reg, e=e: E.matmul(
                            out=reg, lhsT=wsl[e % 2][:, 512 * dc + 128 * i:512 * dc + 128 * (i + 1)],
                            rhs=xnT[dc][:, TC:TC + 512],
                            start=(dc == 0), stop=(dc == 7)),
                            inc=("T" if dc == 7 else None))
                    t_up_i[i] = P.cnt["T"]
                t_up_this = P.cnt["T"]
                if tc == GC - 1:
                    t_up_end[e] = t_up_this
                if (e, tc - 1) in v_bcc:
                    P.wait("tensor", "V", v_bcc[(e, tc - 1)])
                elif tc == 0 and (epr, GC - 1) in v_bcc:
                    P.wait("tensor", "V", v_bcc[(epr, GC - 1)])
                P.wait("tensor", "LC")
                P.do("tensor", lambda E, tc=tc: E.matmul(
                    out=P2[:, 0:512], lhsT=onesb[0:1, 0:128],
                    rhs=crowb[0:1, 512 * tc:512 * (tc + 1)],
                    start=True, stop=True), inc="T")
                t_bc = P.cnt["T"]
                if tc == 0 and e + 1 < NE_RUN:
                    up_waits[e + 1] = load_expert(e + 1, "up", P.cnt["T"])
                if tc == 2 and e + 1 < NE_RUN:
                    dn_waits[e + 1] = load_expert(
                        e + 1, "dn", down_t_done.get(e - 1, 0) if e >= 1 else P.cnt["T"])
                P.wait("scalar", "T", t_dn_guard)
                if (e, tc - 2, 3) in v_hb:
                    P.wait("scalar", "V", v_hb[(e, tc - 2, 3)])
                elif tc < 2 and epr is not None and (epr, GC - 2 + tc, 3) in v_hb:
                    P.wait("scalar", "V", v_hb[(epr, GC - 2 + tc, 3)])
                ho = 2048 * (tc % 2)
                for ia, ib in ((2, 3), (0, 1)):
                    P.wait("scalar", "T", t_up_i[ib])
                    for i in (ia, ib):
                        reg = (P1 if i >= 2 else P0)[:, 512 * (i % 2):512 * (i % 2) + 512]
                        P.do("scalar", lambda E, i=i, reg=reg: E.activation(
                            out=rb[:, 512 * i:512 * (i + 1)], in_=reg, func=AF.Relu))
                    P.drain("scalar")
                    for i in (ia, ib):
                        ss_sq[(e, tc, i)] = P.do("scalar", lambda E, i=i, ho=ho: E.activation(
                            out=hb[:, ho + 512 * i:ho + 512 * (i + 1)], in_=rb[:, 512 * i:512 * (i + 1)],
                            func=AF.Square), inc="SS")
                P.wait("vector", "T", t_bc)
                v_bcc[(e, tc)] = P.do("vector", lambda E, tc=tc: E.tensor_copy(
                    out=bcsb[:, 512 * (tc % 2):512 * (tc % 2) + 512], in_=P2[:, 0:512]),
                    inc="V")
                pend_scale.append((e, tc))
            if e == NE_RUN - 1:
                while pend_scale:
                    se, stc = pend_scale.pop(0)
                    sho = 2048 * (stc % 2)
                    P.wait("vector", "SS", ss_sq[(se, stc, 1)])
                    for i in range(4):
                        v_hb[(se, stc, i)] = P.do(
                            "vector", lambda E, i=i, sho=sho, stc=stc: E.tensor_tensor(
                                out=hb[:, sho + 512 * i:sho + 512 * (i + 1)],
                                in0=hb[:, sho + 512 * i:sho + 512 * (i + 1)],
                                in1=bcsb[:, 512 * (stc % 2):512 * (stc % 2) + 512],
                                op=ALU.mult), inc="V")
                while pend_down:
                    de, dtc = pend_down.pop(0)
                    emit_down(de, dtc)
            t_dn_end[e] = P.cnt["T"]
            epr = e
            if e + 1 < NE_RUN:
                P.wait("scalar", "T", t_dn_end[e])
                P.dma("scalar", "LC", crowb[:], combT_d[e + 1:e + 2, GS * g:GS * (g + 1)])

        prev_up_end[0] = (t_up_end[NE_RUN - 2], t_up_end[NE_RUN - 1]) if NE_RUN >= 2 else None
        o_st = [0] * GT
        for j in range(GT if DEV["out"] else 0):
            bj = j % 2
            ob2 = outs if j % 2 == 0 else hbf[:, 0:1024]
            P.wait("sync", "V")
            P.dma("sync", "LX0" if bj == 0 else "LX1", stg[bj][:], ap["x"][GS * g + 128 * j:GS * g + 128 * (j + 1), :])
            P.wait("vector", "LX0" if bj == 0 else "LX1")
            if j >= 2:
                P.wait("vector", "O" if j % 2 == 0 else "O1", o_st[j - 2])
            P.do("vector", lambda E, bj=bj, j=j, ob2=ob2: E.tensor_tensor(
                out=ob2, in0=stg[bj][:], in1=acc[:, 1024 * j:1024 * (j + 1)],
                op=ALU.add), inc="V")
            P.wait("gpsimd", "V")
            o_st[j] = P.dma("gpsimd", "O" if j % 2 == 0 else "O1",
                            out_hbm[GS * g + 128 * j:GS * g + 128 * (j + 1), :], ob2)
        v_out = (P.cnt["O"], P.cnt["O1"])
        v_stg_free[0] = P.cnt["V"]

    for tname in DEV["taps"]:
        t = tens[tname]
        shp = [t.shape[0], t.shape[1]]
        tap = nc.dram_tensor("tap_" + tname, shp, t[:].dtype, kind="ExternalOutput").ap()
        P.wait("gpsimd", "V")
        P.wait("gpsimd", "SS")
        P.wait("gpsimd", "T")
        P.dma("gpsimd", "O", tap, t[:])

    P.wait("gpsimd", "O")
    P.wait("gpsimd", "O1")

    with nc.Block() as block:
        @block.sync
        def _(E):
            P.run(E, "sync", handles)

        @block.tensor
        def _(E):
            P.run(E, "tensor", handles)

        @block.scalar
        def _(E):
            P.run(E, "scalar", handles)

        @block.vector
        def _(E):
            P.run(E, "vector", handles)

        @block.gpsimd
        def _(E):
            P.run(E, "gpsimd", handles)

    es.close()
    return nc, P


def _host_prep(inputs):
    qkv_w = np.asarray(inputs["L0_attn_qkv_w"], np.float32)
    qkv_b = np.asarray(inputs["L0_attn_qkv_b"], np.float32)
    proj_w = np.asarray(inputs["L0_attn_proj_w"], np.float32)
    proj_b = np.asarray(inputs["L0_attn_proj_b"], np.float32)
    gate_w = np.asarray(inputs["L0_ffn_gate_w"], np.float32)
    e_bias = np.asarray(inputs["L0_ffn_e_bias"], np.float32)
    up_w = np.asarray(inputs["L0_ffn_up_w"], np.float32)
    down_w = np.asarray(inputs["L0_ffn_down_w"], np.float32)
    sup_w = np.asarray(inputs["L0_ffn_shared_up_w"], np.float32)
    sdown_w = np.asarray(inputs["L0_ffn_shared_down_w"], np.float32)

    d = {}
    Wq, Wk, Wv = qkv_w[0:D], qkv_w[D:2 * D], qkv_w[2 * D:3 * D]
    bq, bk, bv = qkv_b[0:D], qkv_b[D:2 * D], qkv_b[2 * D:3 * D]
    d["wq_p"] = _pack_k(np.ascontiguousarray(Wq.T))
    d["wk_p"] = _pack_k(np.ascontiguousarray(Wk.T))
    d["wv_p"] = _pack_k(np.ascontiguousarray(Wv.T))
    d["pw_p"] = _pack_k(np.ascontiguousarray(proj_w.T))
    bqk = np.zeros((128, 16), np.float32)
    for p in range(NP):
        bqk[:, 2 * p] = bq[128 * p:128 * (p + 1)]
        bqk[:, 2 * p + 1] = bk[128 * p:128 * (p + 1)]
    d["bqk"] = bqk
    d["bvp"] = np.ascontiguousarray(bv[None, :])
    d["pjb"] = np.ascontiguousarray(proj_b[None, :])
    d["gw_p"] = _pack_k(np.ascontiguousarray(gate_w.T))
    d["ebt"] = np.ascontiguousarray(np.broadcast_to(e_bias[None, :], (128, 32)).copy())
    upl, dnl = [], []
    for e in range(32):
        upl.append(_pack_k(np.ascontiguousarray(up_w[e].T)))
        dnl.append(_pack_k(np.ascontiguousarray(down_w[e].T)))
    supT = np.ascontiguousarray(sup_w.T)
    sdnT = np.ascontiguousarray(sdown_w.T)
    for c in range(4):
        upl.append(_pack_k(np.ascontiguousarray(supT[:, 512 * c:512 * (c + 1)])))
        dnl.append(_pack_k(np.ascontiguousarray(sdnT[512 * c:512 * (c + 1), :])))
    d["upw"] = np.ascontiguousarray(np.stack(upl))
    d["dnw"] = np.ascontiguousarray(np.stack(dnl))
    inv = 1.0 / (10000.0 ** (np.arange(0, DH, 2, dtype=np.float64) / DH))
    t = np.arange(S, dtype=np.float64)
    fr = np.concatenate([np.outer(t, inv), np.outer(t, inv)], axis=1)
    cosT = np.cos(fr).T.astype(np.float32)
    sinT = np.sin(fr).T.astype(np.float32)
    d["cost"] = np.ascontiguousarray(np.vstack([cosT, cosT]).astype(ml_dtypes.bfloat16))
    d["sint"] = np.ascontiguousarray(np.vstack([sinT, sinT]).astype(ml_dtypes.bfloat16))
    M64 = np.zeros((64, 64), np.float32)
    for i in range(32):
        M64[i, i + 32] = -1.0
        M64[i + 32, i] = 1.0
    MT = np.zeros((128, 128), np.float32)
    MT[0:64, 0:64] = M64.T
    MT[64:128, 64:128] = M64.T
    d["mt"] = MT
    d["idn"] = np.eye(128, dtype=np.float32)
    esh = np.zeros((64, 128), np.float32)
    for i in range(64):
        esh[i, 64 + i] = 1.0
    d["esh"] = esh
    d["msk"] = np.ascontiguousarray(
        (np.arange(128)[:, None] <= np.arange(128)[None, :]).astype(np.float32))
    return d


def kernel(**inputs):
    x = np.asarray(inputs["L0_attn_x"], np.float32)
    shared = _host_prep(inputs)
    nc, _ = build_nc()
    in_maps = []
    for c in range(B):
        m = dict(shared)
        m["x"] = np.ascontiguousarray(x[c])
        in_maps.append(m)
    res = run_bass_kernel_spmd(nc, in_maps, core_ids=list(range(B)))
    out = np.stack([np.asarray(r["out"], np.float32) for r in res.results])
    return out
